# version 08
# baseline (speedup 1.0000x reference)
import sys
if "/opt/trn_rl_repo" not in sys.path:
    sys.path.insert(0, "/opt/trn_rl_repo")
import numpy as np
import ml_dtypes
import concourse.bacc as bacc_mod
import concourse.bass as bass
import concourse.mybir as mybir
from concourse.bass_utils import run_bass_kernel_spmd
from contextlib import ExitStack

F32 = mybir.dt.float32
BF16 = mybir.dt.bfloat16
I16 = mybir.dt.int16
I32 = mybir.dt.int32
AF = mybir.ActivationFunctionType
ALU = mybir.AluOpType
AX = mybir.AxisListType

B, T, D = 8, 2048, 1024
H, DK, LOWR, LEN = 16, 64, 20, 200
E, FF, CAP = 16, 1408, 2560
CB = 384
NU = 10
US = 384
YROWS = E * CB + NU * US
NT = T // 128
KC = D // 128
FC = FF // 128
NSAMP = 256
BIG = 1024.0
COLS = CB // 16


class K:
    NSLOT = 8

    def __init__(self, nc, stack):
        self.nc = nc
        self.sem = {}
        self.cnt = {}
        for e in ("v", "s", "t"):
            self.sem[e] = stack.enter_context(nc.semaphore("ch_" + e))
            self.cnt[e] = 0
        self.engs = {"v": nc.vector, "s": nc.scalar, "t": nc.tensor,
                     "g": nc.gpsimd, "y": nc.sync}
        self.qsem, self.qcnt, self.qidx = {}, {}, {}
        for q in ("y", "g"):
            self.qsem[q] = [stack.enter_context(nc.semaphore(f"q_{q}{i}")) for i in range(self.NSLOT)]
            self.qcnt[q] = [0] * self.NSLOT
            self.qidx[q] = 0
        self._lw = {}
        self.wait_log = {}

    def _resolve(self, tok):
        se, vv = tok
        return (self.sem[se], vv) if isinstance(se, str) else (se, vv)

    def _wait(self, ekey, eng, tok):
        if tok[1] <= 0:
            return
        s, vv = self._resolve(tok)
        key = (ekey, id(s))
        if self._lw.get(key, -1) >= vv:
            return
        eng.wait_ge(s, vv)
        self._lw[key] = vv
        self.wait_log.setdefault(id(s), set()).add(vv)

    def op(self, e, method, *args, wait=(), inc=True, **kw):
        eng = self.engs[e]
        w = list(wait)
        if e in ("v", "s"):
            w.append((e, self.cnt[e]))
        for tok in w:
            self._wait(e, eng, tok)
        inst = getattr(eng, method)(*args, **kw)
        if inc:
            self.cnt[e] += 1
            inst.then_inc(self.sem[e], 1)
        return (e, self.cnt[e])

    def _q(self, q, method, args, kw, wait):
        eng = self.engs[q]
        i = self.qidx[q] % self.NSLOT
        self.qidx[q] += 1
        s = self.qsem[q][i]
        if self.qcnt[q][i] > 0:
            self._wait(q, eng, (s, self.qcnt[q][i]))
        for tok in wait:
            self._wait(q, eng, tok)
        inst = getattr(eng, method)(*args, **kw)
        self.qcnt[q][i] += 16
        inst.then_inc(s, 16)
        return (s, self.qcnt[q][i])

    def dma(self, q, *args, wait=(), **kw):
        return self._q(q, "dma_start", args, kw, wait)

    def dma_sem(self, q, sem, cnt_ref, method, *args, wait=(), **kw):
        eng = self.engs[q]
        for tok in wait:
            self._wait(q, eng, tok)
        inst = getattr(eng, method)(*args, **kw)
        cnt_ref[0] += 16
        inst.then_inc(sem, 16)
        return (sem, cnt_ref[0])

    def gop(self, method, *args, wait=(), **kw):
        return self._q("g", method, args, kw, wait)

    def v(self, m, *a, **kw): return self.op("v", m, *a, **kw)
    def s(self, m, *a, **kw): return self.op("s", m, *a, **kw)
    def t(self, m, *a, **kw): return self.op("t", m, *a, **kw)

    def marks(self):
        toks = [("v", self.cnt["v"]), ("s", self.cnt["s"]), ("t", self.cnt["t"])]
        for q in ("y", "g"):
            for i in range(self.NSLOT):
                if self.qcnt[q][i] > 0:
                    toks.append((self.qsem[q][i], self.qcnt[q][i]))
        return toks


def build(debug=False):
    import os
    PH = int(os.environ.get("MK_PHASES", "9"))
    CS = int(os.environ.get("MK_CSUB", "9"))
    nc = bacc_mod.Bacc("TRN2")
    dram = lambda nm, shp, dt=F32, kind="ExternalInput": nc.dram_tensor(nm, shp, dt, kind=kind).ap()

    xo = dram("xown", [T, D])
    xf = dram("xfull", [B * T, D])
    wq_h = dram("Wq", [D, D]); bq_h = dram("bq", [D])
    wk_h = dram("Wk", [D, D]); bk_h = dram("bk", [D])
    we_h = dram("We", [H, LEN, LOWR]); wr_h = dram("Wr", [H, LEN, LOWR])
    wc_h = dram("Wc", [2 * LOWR, DK]); bc_h = dram("bc", [DK])
    wo_h = dram("Wo", [D, D]); bo_h = dram("bo", [D])
    rw_h = dram("router_w", [D, E]); rb_h = dram("router_b", [E])
    wg_h = dram("w_gate", [E, D, FF]); wu_h = dram("w_up", [E, D, FF])
    wd_h = dram("w_down", [E, FF, D])
    swg_h = dram("sw_gate", [D, FF]); swu_h = dram("sw_up", [D, FF])
    swd_h = dram("sw_down", [FF, D])
    ident_h = dram("c_ident", [128, 128])
    identb_h = dram("c_identb", [128, 128], BF16)
    triu_h = dram("c_triu", [128, 128])
    iota16_h = dram("c_iota16", [128, E])
    t1_h = dram("c_t1", [128, E])
    iotatok_h = dram("c_iotatok", [128, 1])
    tokv_h = dram("c_tokv", [128, NT])
    mask8_h = dram("c_mask8", [8, 1])
    headsel_h = dram("c_headsel", [128, KC, E], BF16)
    headbc_h = dram("c_headbc", [16, KC, 128], BF16)
    sampidx_h = dram("c_sampidx", [128, NSAMP // 16], I16)
    out_h = dram("out", [T, D], kind="ExternalOutput")
    xnrows = dram("d_xnrows", [T, D], BF16, kind="Internal")
    wgc_d = dram("d_wgc", [E, D, FF], BF16, kind="Internal")
    wuc_d = dram("d_wuc", [E, D, FF], BF16, kind="Internal")
    wdc_d = dram("d_wdc", [E, FF, D], BF16, kind="Internal")
    tabf_d = dram("d_tab", [YROWS], kind="Internal")
    bgidx_d = dram("d_bgidx", [2, T], kind="Internal")
    yexp = dram("d_yexp", [YROWS, D], BF16, kind="Internal")
    ysh_d = dram("d_ysh", [T, D], BF16, kind="Internal")
    stm_d = dram("d_stm", [T, D], BF16, kind="Internal")
    cnt_d = dram("d_cnt", [B * E], kind="Internal")
    uex_d = dram("d_uex", [16], kind="Internal")
    uex2_d = dram("d_uex2", [16], kind="Internal")
    uinfo_d = dram("d_uinfo", [2 * NU], kind="Internal")
    dbg = {}
    if debug:
        dbg["xn"] = dram("dbg_xn", [T, D], BF16, kind="ExternalOutput")
        dbg["logits"] = dram("dbg_logits", [128, NT, E], kind="ExternalOutput")
        dbg["route"] = dram("dbg_route", [128, NT, 8], kind="ExternalOutput")
        dbg["cnt"] = dram("dbg_cnt", [8, E], kind="ExternalOutput")
        dbg["wts"] = dram("dbg_wts", [128, NT, 2], kind="ExternalOutput")
        dbg["flog"] = dram("dbg_flog", [128, B * NT, E], kind="ExternalOutput")
        dbg["s"] = dram("dbg_s", [T, D], BF16, kind="ExternalOutput")
        dbg["ysh"] = dram("dbg_ysh", [T, D], BF16, kind="ExternalOutput")
        dbg["moe"] = dram("dbg_moe", [T, D], kind="ExternalOutput")

    big = ExitStack()
    sbP = lambda nm, shp, dt=F32: big.enter_context(nc.sbuf_tensor(nm, shp, dt))
    ps_ = [big.enter_context(nc.psum_tensor(f"ps{i}", [128, 512], F32)) for i in range(8)]
    k = K(nc, big)

    xnfm = sbP("xnfm", [128, KC, T], BF16)
    possb = sbP("possb", [128, NT, E])
    idxk = sbP("idxk", [128, NT, 2])
    valk = sbP("valk", [128, NT, 2])
    rnkk = sbP("rnkk", [128, NT, 2])
    dstk = sbP("dstk", [128, NT, 2])
    wts = sbP("wts", [128, NT, 2])
    cntT = sbP("cntT", [8, E])
    cnt_row = sbP("cnt_row", [1, 8 * E])
    prior_b = sbP("prior_b", [128, E])
    ident = sbP("ident_s", [128, 128]); identb = sbP("identb_s", [128, 128], BF16)
    triu = sbP("triu_s", [128, 128])
    iota16 = sbP("iota16_s", [128, E]); t1c = sbP("t1_s", [128, E])
    iotatok = sbP("iotatok_s", [128, 1]); mask8 = sbP("mask8_s", [8, 1])
    headsel = sbP("headsel_s", [128, KC, E], BF16)
    headbc = sbP("headbc_s", [16, KC, 128], BF16)
    sampidx = sbP("sampidx_s", [128, NSAMP // 16], I16)
    ones1 = sbP("ones1", [1, 128]); onescol = sbP("onescol", [128, 1])
    epsln = sbP("epsln", [128, 1])
    rw_sb = sbP("rw_sb", [128, KC, E]); rw2 = sbP("rw2", [2, E])
    brow = sbP("brow", [1, E]); cwrow = sbP("cwrow", [1, E])
    wcpad32 = sbP("wcpad32", [128, 128]); wcpad = sbP("wcpad", [128, 128], BF16)
    bqs = sbP("bqs", [128, KC]); bks = sbP("bks", [128, KC]); bos_row = sbP("bos_row", [1, D])
    bcrep = sbP("bcrep", [128, 1]); bbo = sbP("bbo", [128, D], BF16)
    scWe = sbP("scWe", [128, KC, 128], BF16); scWr = sbP("scWr", [128, KC, 128], BF16)
    invq = sbP("invq", [16, 512]); invqb = sbP("invqb", [16, 512], BF16)
    invk_ = sbP("invk", [16, 512]); invkb = sbP("invkb", [16, 512], BF16)
    phAB = ExitStack()
    sbA = lambda nm, shp, dt=F32: phAB.enter_context(nc.sbuf_tensor(nm, shp, dt, side="right"))
    xfb = [sbA(f"xfb{j}", [128, 2, D]) for j in range(2)]
    xs32 = [xfb[0][:, 0, :], xfb[1][:, 0, :]]
    xn32 = sbA("xn32", [128, D]); fmscr = sbA("fmscr", [128, D])
    sqscr = sbA("sqscr", [128, D])
    xnb = [sbP(f"xnb{j}", [128, D], BF16) for j in range(2)]
    musum = sbP("musum", [128, 1]); ssq = sbP("ssq", [128, 1])
    mu = sbP("mu", [128, 1]); var = sbP("var", [128, 1]); var2 = sbP("var2", [128, 1])
    sig = sbP("sig", [128, 1]); inv = sbP("inv", [128, 1]); negmuinv = sbP("negmuinv", [128, 1])
    ltile = sbP("ltile", [128, E])
    m1 = sbP("m1", [128, 1]); m2 = sbP("m2", [128, 1]); negm1 = sbP("negm1", [128, 1])
    etile = sbP("etile", [128, E]); probs = sbP("probs", [128, E])
    oh1 = sbP("oh1", [128, E]); oh2 = sbP("oh2", [128, E]); ohs = sbP("ohs", [128, E])
    selt = sbP("selt", [128, E])
    i1 = sbP("i1", [128, 1]); i2 = sbP("i2", [128, 1])
    p1 = sbP("p1", [128, 1]); p2 = sbP("p2", [128, 1])
    ssum = sbP("ssum", [128, 1]); rcp = sbP("rcp", [128, 1])
    rsel = sbP("rsel", [128, E])
    r1 = sbP("r1", [128, 1]); r2 = sbP("r2", [128, 1])
    d1 = sbP("d1", [128, 1]); d2 = sbP("d2", [128, 1])
    tokval = sbP("tokval", [128, 1])
    a32 = sbP("a32", [128, NT, 2], I32)
    m1a = sbP("m1a", [128, NT, 1]); sa1 = sbP("sa1", [128, NT, 1]); ra1 = sbP("ra1", [128, NT, 1])
    p1a = sbP("p1a", [128, NT, 1]); p2a = sbP("p2a", [128, NT, 1])
    i1a = sbP("i1a", [128, NT, 1]); i2a = sbP("i2a", [128, NT, 1])
    pk16 = sbP("pk16", [128, NT, 1])
    ubs2 = sbP("ubs2", [128, NT, 2]); mlt2 = sbP("mlt2", [128, NT, 2]); rp2 = sbP("rp2", [128, NT, 2])
    j62 = sbP("j62", [128, NT, 2]); tA2 = sbP("tA2", [128, NT, 2]); tB2 = sbP("tB2", [128, NT, 2])
    rem2 = sbP("rem2", [128, NT, 2])
    tokv = sbP("tokv_s", [128, NT])
    offrow = sbP("offrow", [1, E]); tmprow = sbP("tmprow", [1, E])
    musig = sbP("musig", [128, 2]); musigT = sbP("musigT", [2, 128])
    orow = sbP("orow", [1, E]); urow = sbP("urow", [1, E]); trow = sbP("trow", [1, E])
    tot1 = sbP("tot1", [1, 1]); ubrow = sbP("ubrow", [1, E])
    ubcol = sbP("ubcol", [16, 1]); ucol = sbP("ucol", [16, 1]); totcol = sbP("totcol", [16, 1])
    eucol = sbP("eucol", [16, 1]); nacol = sbP("nacol", [16, 1]); m16 = sbP("m16", [16, E])
    uinf2 = sbP("uinf2", [16, 2]); uirow = sbP("uirow", [1, 2 * NU]); ui32 = sbP("ui32", [1, 2 * NU], I32)
    ubb = sbP("ubb", [128, E])
    rtile = sbP("rtile", [128, 1]); mlt = sbP("mlt", [128, 1]); rp_ = sbP("rp_", [128, 1])
    j6 = sbP("j6", [128, 1]); rem_ = sbP("rem_", [128, 1]); ubsel = sbP("ubsel", [128, 1])
    tA = sbP("tA", [128, 1]); tB = sbP("tB", [128, 1])
    dbgst = sbP("dbgst", [128, 8]); dbg_free = [("v", 0)]; ldbg_free = [("v", 0)]
    prow = sbP("prow", [1, E])
    idx_all = sbP("idx_all", [128, (E + NU) * COLS], I16)
    tabf16 = sbP("tabf16", [16, (E + NU) * COLS])
    zz = sbP("zz", [16, YROWS // 16])

    cl = {}
    for nm, dst, src in [("id", ident, ident_h), ("idb", identb, identb_h), ("tr", triu, triu_h),
                         ("io", iota16, iota16_h), ("t1", t1c, t1_h), ("it", iotatok, iotatok_h),
                         ("m8", mask8, mask8_h), ("hs", headsel, headsel_h), ("hb", headbc, headbc_h),
                         ("si", sampidx, sampidx_h), ("tv", tokv, tokv_h)]:
        cl[nm] = k.dma("y", out=dst[:], in_=src[:])
    t_ones = k.v("memset", ones1[:], 1.0)
    t_eps = k.v("memset", epsln[:], 1e-5)
    t_onc = k.v("memset", onescol[:], 1.0)
    t_zz = k.v("memset", zz[:], 0.0)
    rw_ld = [k.dma("y", out=rw_sb[:, kc, :], in_=rw_h[kc * 128:(kc + 1) * 128, :]) for kc in range(KC)]
    t_br = k.dma("y", out=brow[:], in_=rb_h.rearrange("(a b) -> a b", a=1))
    for kc in range(KC):
        k.t("matmul", ps_[7][0:1, 0:E], onescol[:], rw_sb[:, kc, :],
            start=(kc == 0), stop=(kc == KC - 1), wait=[rw_ld[kc], t_onc], inc=(kc == KC - 1))
    t_cw = ("t", k.cnt["t"])
    t_cwev = k.s("copy", cwrow[:], ps_[7][0:1, 0:E], wait=[t_cw])
    t_rw2a = k.v("tensor_scalar", rw2[0:1, :], cwrow[:], -1.0, None, op0=ALU.mult, wait=[t_cwev])
    t_rw2b = k.dma("y", out=rw2[1:2, :], in_=rb_h.rearrange("(a b) -> a b", a=1))
    t_tz = k.dma("y", out=tabf_d.rearrange("(p c) -> p c", p=16), in_=zz[:], wait=[t_zz])
    zbf = sbP("zbf", [128, D], BF16)
    t_zb = k.v("memset", zbf[:], 0.0)
    yexp_zero_toks = []
    for zi in range(YROWS // 128):
        yexp_zero_toks.append(k.dma("g", out=yexp[zi * 128:(zi + 1) * 128, :], in_=zbf[:], wait=[t_zb]))

    t_bq = [k.dma("y", out=bqs[:, kc:kc + 1], in_=bq_h[kc * 128:(kc + 1) * 128].rearrange("(p a) -> p a", a=1)) for kc in range(KC)]
    t_bk = [k.dma("y", out=bks[:, kc:kc + 1], in_=bk_h[kc * 128:(kc + 1) * 128].rearrange("(p a) -> p a", a=1)) for kc in range(KC)]
    t_bc1 = k.dma("y", out=bcrep[0:64, :], in_=bc_h.rearrange("(p a) -> p a", a=1))
    t_bc2 = k.dma("y", out=bcrep[64:128, :], in_=bc_h.rearrange("(p a) -> p a", a=1))
    t_bo = k.dma("y", out=bos_row[:], in_=bo_h.rearrange("(a b) -> a b", a=1))
    t_wz = k.v("memset", wcpad32[:], 0.0)
    t_wc = []
    t_wc.append(k.dma("y", out=wcpad32[0:20, 0:64], in_=wc_h[0:20, :], wait=[t_wz]))
    t_wc.append(k.dma("y", out=wcpad32[32:52, 0:64], in_=wc_h[20:40, :], wait=[t_wz]))
    t_wc.append(k.dma("y", out=wcpad32[64:84, 64:128], in_=wc_h[0:20, :], wait=[t_wz]))
    t_wc.append(k.dma("y", out=wcpad32[96:116, 64:128], in_=wc_h[20:40, :], wait=[t_wz]))
    t_wcc = k.v("tensor_copy", wcpad[:], wcpad32[:], wait=t_wc)
    bbo_toks = []
    for nt in range(2):
        tt = k.t("matmul", ps_[7][:, 0:512], ones1[0:1, 0:128], bos_row[0:1, nt * 512:(nt + 1) * 512],
                 start=True, stop=True, wait=[t_bo, t_ones, t_cwev] + bbo_toks)
        bbo_toks.append(k.s("copy", bbo[:, nt * 512:(nt + 1) * 512], ps_[7][:, 0:512], wait=[tt]))

    lall = sbA("lall", [128, NT, E])
    da = sbA("da", [128, NT, E])
    ea = sbA("ea", [128, NT, E])
    oh1a = sbA("oh1a", [128, NT, E])
    oh2a = sbA("oh2a", [128, NT, E])
    ohsa = sbA("ohsa", [128, NT, E])
    log_bank = ps_[2]
    off_bank = ps_[6]
    pos_bank = ps_[3]
    tp = [ps_[0], ps_[1]]

    a_done = []
    scat_toks = []
    own_ld = [k.dma("y", out=xfb[0][:, 0, :], in_=xo[0:128, :]), None]
    xnb_free = [[("v", 0)], [("v", 0)]]
    prev_log_evac = ("s", 0)
    prev_pos_evac = ("s", 0)
    prev_off_read = ("s", 0)
    prev_tmp_ev = ("s", 0)
    prev_fm_users = [("t", 0), ("v", 0)]
    prev_bg = [("v", 0), ("v", 0)]
    prev_g2 = ("v", 0)
    prev_tp_evac = [("s", 0), ("s", 0)]

    for i in range(NT):
        buf = xfb[i % 2][:, 0, :]
        ld_tok = own_ld[i % 2]
        if i + 1 < NT:
            w = [a_done[i - 1]] if i >= 1 else []
            own_ld[(i + 1) % 2] = k.dma("y", out=xfb[(i + 1) % 2][:, 0, :], in_=xo[128 * (i + 1):128 * (i + 2), :], wait=w)
        k.s("activation", sqscr[:], buf[:], AF.Copy, accum_out=musum[:], wait=[ld_tok])
        k.v("tensor_scalar", mu[:], musum[:], 1.0 / D, None, op0=ALU.mult, wait=[("s", k.cnt["s"])])
        t_nmu = k.v("tensor_scalar", negmuinv[:], mu[:], -1.0, None, op0=ALU.mult)
        k.s("activation", sqscr[:], buf[:], AF.Square, bias=negmuinv[:], scale=1.0,
            accum_out=ssq[:], wait=[t_nmu])
        k.v("tensor_scalar", var[:], ssq[:], 1.0 / D, None, op0=ALU.mult, wait=[("s", k.cnt["s"])])
        t_sig = k.s("activation", sig[:], var[:], AF.Sqrt, bias=epsln[:], scale=1.0, wait=[("v", k.cnt["v"]), t_eps])
        t_inv = k.v("reciprocal", inv[:], sig[:], wait=[t_sig])
        t_nmi = k.v("tensor_scalar", negmuinv[:], mu[:], inv[:], -1.0, op0=ALU.mult, op1=ALU.mult)
        t_xn = k.s("activation", xn32[:], buf[:], AF.Identity, bias=negmuinv[:], scale=inv[:],
                   wait=[t_nmi, prev_fm_users[0], prev_fm_users[1]])
        t_xnb = k.v("tensor_copy", xnb[i % 2][:], xn32[:], wait=[t_xn] + xnb_free[i % 2])
        xnb_free[i % 2] = [k.dma("y", out=xnrows[128 * i:128 * (i + 1), :], in_=xnb[i % 2][:], wait=[t_xnb])]
        if debug:
            xnb_free[i % 2].append(k.dma("y", out=dbg["xn"][128 * i:128 * (i + 1), :], in_=xnb[i % 2][:], wait=[t_xnb]))
        tt_toks = []
        for h2 in range(2):
            for j in range(4):
                kc = h2 * 4 + j
                tt_toks.append(k.t("transpose", tp[h2][:, j * 128:(j + 1) * 128],
                                   xn32[:, kc * 128:(kc + 1) * 128], ident[:],
                                   wait=[t_xn, prev_tp_evac[h2], cl["id"]]))
        ev0 = k.s("copy", fmscr[:, 0:512], tp[0][:, 0:512], wait=[tt_toks[3], prev_fm_users[0], prev_fm_users[1]])
        ev1 = k.s("copy", fmscr[:, 512:1024], tp[1][:, 0:512], wait=[tt_toks[7]])
        prev_tp_evac = [ev0, ev1]
        t_fmv = k.v("tensor_copy",
                    xnfm[:, :, 128 * i:128 * (i + 1)],
                    fmscr.rearrange("p (c t) -> p c t", c=KC), wait=[ev1])
        for kc in range(KC):
            k.t("matmul", log_bank[:, 16 * i:16 * i + 16], fmscr[:, kc * 128:(kc + 1) * 128],
                rw_sb[:, kc, :], start=(kc == 0), stop=False,
                wait=([ev0] if kc < 4 else [ev1]) + ([rw_ld[kc]] if kc == 0 else []), inc=False)
        t_rb = k.t("matmul", log_bank[:, 16 * i:16 * i + 16], ones1[0:1, 0:128], brow[:],
                   start=False, stop=True, wait=[t_br, t_ones])
        prev_fm_users = [t_rb, t_fmv]
        a_done.append(t_xn)

    t_lev = k.s("copy", lall[:], log_bank[:, 0:NT * E].rearrange("p (t e) -> p t e", e=E),
                wait=[prev_fm_users[0]])
    if debug:
        ldbg_free[0] = k.dma("y", out=dbg["logits"][:], in_=lall[:], wait=[t_lev])
    io3 = iota16[:].rearrange("p (a e) -> p a e", a=1).to_broadcast([128, NT, E])
    t13 = t1c[:].rearrange("p (a e) -> p a e", a=1).to_broadcast([128, NT, E])
    pa = da
    sel = lall
    pa2 = ea
    k.v("tensor_reduce", m1a[:], lall[:], axis=AX.X, op=ALU.max, wait=[t_lev])
    t_d = k.v("tensor_tensor", da[:], lall[:], m1a[:].to_broadcast([128, NT, E]), op=ALU.subtract)
    t_exp = k.s("activation", ea[:], da[:], AF.Exp, wait=[t_d])
    k.v("tensor_reduce", sa1[:], ea[:], axis=AX.X, op=ALU.add, wait=[t_exp])
    k.v("reciprocal", ra1[:], sa1[:])
    k.v("tensor_tensor", pa[:], ea[:], ra1[:].to_broadcast([128, NT, E]), op=ALU.mult)
    k.v("tensor_reduce", p1a[:], pa[:], axis=AX.X, op=ALU.max)
    k.v("tensor_tensor", sel[:], pa[:], p1a[:].to_broadcast([128, NT, E]), op=ALU.is_equal,
        wait=[ldbg_free[0]])
    k.v("tensor_tensor", sel[:], sel[:], t13, op=ALU.mult, wait=[cl["t1"]])
    k.v("tensor_scalar", sel[:], sel[:], BIG, None, op0=ALU.add)
    k.v("tensor_reduce", i1a[:], sel[:], axis=AX.X, op=ALU.min)
    k.v("tensor_tensor", oh1a[:], io3, i1a[:].to_broadcast([128, NT, E]), op=ALU.is_equal,
        wait=[cl["io"]])
    k.v("tensor_scalar", sel[:], oh1a[:], 2.0, None, op0=ALU.mult)
    k.v("tensor_tensor", pa2[:], pa[:], sel[:], op=ALU.subtract)
    k.v("tensor_reduce", p2a[:], pa2[:], axis=AX.X, op=ALU.max)
    k.v("tensor_tensor", sel[:], pa2[:], p2a[:].to_broadcast([128, NT, E]), op=ALU.is_equal)
    k.v("tensor_tensor", sel[:], sel[:], t13, op=ALU.mult)
    k.v("tensor_scalar", sel[:], sel[:], BIG, None, op0=ALU.add)
    k.v("tensor_reduce", i2a[:], sel[:], axis=AX.X, op=ALU.min)
    k.v("tensor_tensor", oh2a[:], io3, i2a[:].to_broadcast([128, NT, E]), op=ALU.is_equal)
    t_ohs = k.v("tensor_tensor", ohsa[:], oh1a[:], oh2a[:], op=ALU.add)
    k.v("tensor_tensor", sa1[:], p1a[:], p2a[:], op=ALU.add)
    k.v("tensor_scalar", sa1[:], sa1[:], 1e-9, None, op0=ALU.add)
    k.v("reciprocal", ra1[:], sa1[:])
    k.v("tensor_tensor", valk[:, :, 0:1], p1a[:], ra1[:], op=ALU.mult)
    k.v("tensor_tensor", valk[:, :, 1:2], p2a[:], ra1[:], op=ALU.mult)
    k.v("tensor_copy", idxk[:, :, 0:1], i1a[:])
    t_idxa = k.v("tensor_copy", idxk[:, :, 1:2], i2a[:])

    for i in range(NT):
        if i == 0:
            t_or = k.v("memset", offrow[:], 0.0)
        else:
            t_or = prev_off_read
        k.t("matmul", pos_bank[:, 0:16], triu[:], ohsa[:, i, :], start=True, stop=False,
            wait=[t_ohs, t_or, prev_pos_evac, cl["tr"]], inc=False)
        t_pm = k.t("matmul", pos_bank[:, 0:16], ones1[0:1, 0:128], offrow[:], start=False, stop=True)
        t_ou = k.t("matmul", off_bank[0:1, 0:16], onescol[:], ohsa[:, i, :],
                   start=True, stop=True, wait=[t_or, prev_tmp_ev])
        prev_tmp_ev = k.s("copy", tmprow[:], off_bank[0:1, 0:16], wait=[t_ou])
        prev_off_read = k.v("tensor_tensor", offrow[:], offrow[:], tmprow[:], op=ALU.add,
                            wait=[prev_tmp_ev, t_pm])
        t_pev = k.s("copy", possb[:, i, :], pos_bank[:, 0:16], wait=[t_pm])
        prev_pos_evac = t_pev

    k.v("tensor_tensor", da[:], possb[:], oh1a[:], op=ALU.mult, wait=[prev_pos_evac])
    k.v("tensor_reduce", rnkk[:, :, 0:1], da[:], axis=AX.X, op=ALU.add)
    k.v("tensor_tensor", da[:], possb[:], oh2a[:], op=ALU.mult)
    t_r2 = k.v("tensor_reduce", rnkk[:, :, 1:2], da[:], axis=AX.X, op=ALU.add)
    if debug:
        for i in range(NT):
            for col, srcb in [(0, i1a), (1, i2a), (2, p1a), (3, p2a)]:
                k.v("tensor_copy", dbgst[:, col:col + 1], srcb[:, i, :],
                    wait=[dbg_free[0]] if col == 0 else [])
            for col, kkk in [(4, 0), (5, 1), (6, 0), (7, 1)]:
                k.v("tensor_copy", dbgst[:, col:col + 1], rnkk[:, i, kkk:kkk + 1], wait=[t_r2])
            dbg_free[0] = k.dma("y", out=dbg["route"][:, i, :], in_=dbgst[:], wait=[("v", k.cnt["v"])])

    a_marks = k.marks()

    t_o = k.v("tensor_scalar", orow[:], offrow[:], -float(CB), 0.0, op0=ALU.add, op1=ALU.max,
              wait=[prev_off_read])
    t_u = k.v("memset", urow[:], 0.0)
    for j in range(5):
        k.v("tensor_scalar", trow[:], orow[:], float(US * j), None, op0=ALU.is_gt)
        t_u = k.v("tensor_tensor", urow[:], urow[:], trow[:], op=ALU.add)
    t_tot = k.v("reduce_sum", tot1[:], urow[:], axis=AX.X)
    y_u1 = k.dma("y", out=uex_d.rearrange("(a b) -> a b", a=1), in_=urow[:], wait=[t_u])
    y_u2 = k.dma("y", out=ucol[:], in_=uex_d.rearrange("(p a) -> p a", a=1), wait=[y_u1])
    t_ubm = k.t("matmul", pos_bank[0:16, 0:1], triu[0:16, 0:16], ucol[:], start=True, stop=True,
                wait=[y_u2, prev_pos_evac])
    t_ube = k.s("copy", ubcol[:], pos_bank[0:16, 0:1], wait=[t_ubm])
    y_u3 = k.dma("y", out=uex2_d.rearrange("(a b) -> a b", a=1),
                 in_=ubcol.rearrange("p a -> a (p a)"), wait=[t_ube]) if False else None
    y_u3 = k.dma("y", out=uex2_d.rearrange("(p a) -> p a", a=1), in_=ubcol[:], wait=[t_ube])
    y_u4 = k.dma("y", out=ubrow[:], in_=uex2_d.rearrange("(a b) -> a b", a=1), wait=[y_u3])
    t_ubbm = k.t("matmul", pos_bank[:, 1:17], ones1[0:1, 0:128], ubrow[:], start=True, stop=True,
                 wait=[y_u4, t_ube])
    t_ubbe = k.s("copy", ubb[:], pos_bank[:, 1:17], wait=[t_ubbm])
    k.v("tensor_scalar", m16[:], ubb[0:16, :] if False else ubb[0:16, :], None, None, op0=ALU.bypass) if False else None
    k.v("tensor_scalar", m16[:], ubb[0:16, :], iotatok[0:16, 0:1], None, op0=ALU.is_le, wait=[t_ubbe])
    k.v("reduce_sum", eucol[:], m16[:], axis=AX.X)
    t_eu = k.v("tensor_scalar", eucol[:], eucol[:], -1.0, 0.0, op0=ALU.add, op1=ALU.max)
    t_tc2 = k.t("matmul", pos_bank[0:16, 17:18], ones1[0:1, 0:16], tot1[:], start=True, stop=True,
                wait=[t_tot, t_ubbe])
    t_tce = k.s("copy", totcol[:], pos_bank[0:16, 17:18], wait=[t_tc2])
    k.v("tensor_scalar", nacol[:], iotatok[0:16, 0:1], totcol[:], None, op0=ALU.is_ge, wait=[t_tce])
    k.v("tensor_copy", uinf2[:, 0:1], eucol[:])
    t_ui = k.v("tensor_copy", uinf2[:, 1:2], nacol[:])
    y_u5 = k.dma("y", out=uinfo_d.rearrange("(p a) -> p a", a=2), in_=uinf2[0:NU, :], wait=[t_ui])
    y_u6 = k.dma("y", out=uirow[:], in_=uinfo_d.rearrange("(a b) -> a b", a=1), wait=[y_u5])
    t_ui32 = k.v("tensor_copy", ui32[:], uirow[:], wait=[y_u6])
    ubb3 = ubb[:].rearrange("p (a e) -> p a e", a=1).to_broadcast([128, NT, E])
    k.v("tensor_tensor", da[:], oh1a[:], ubb3, op=ALU.mult, wait=[t_ubbe])
    k.v("tensor_reduce", ubs2[:, :, 0:1], da[:], axis=AX.X, op=ALU.add)
    k.v("tensor_tensor", da[:], oh2a[:], ubb3, op=ALU.mult)
    k.v("tensor_reduce", ubs2[:, :, 1:2], da[:], axis=AX.X, op=ALU.add)
    k.v("tensor_scalar", mlt2[:], rnkk[:], float(CB), None, op0=ALU.is_lt, wait=[t_r2])
    k.v("tensor_scalar", rp2[:], rnkk[:], -float(CB), 0.0, op0=ALU.add, op1=ALU.max)
    k.v("memset", j62[:], 0.0)
    for mm in range(1, 5):
        k.v("tensor_scalar", tA2[:], rp2[:], float(US * mm), None, op0=ALU.is_ge)
        k.v("tensor_tensor", j62[:], j62[:], tA2[:], op=ALU.add)
    k.v("tensor_scalar", tA2[:], j62[:], float(US), None, op0=ALU.mult)
    k.v("tensor_tensor", rem2[:], rp2[:], tA2[:], op=ALU.subtract)
    k.v("tensor_tensor", tB2[:], ubs2[:], j62[:], op=ALU.add)
    k.v("tensor_scalar", tB2[:], tB2[:], float(US), float(E * CB), op0=ALU.mult, op1=ALU.add)
    k.v("tensor_tensor", tB2[:], tB2[:], rem2[:], op=ALU.add)
    k.v("tensor_scalar", tA2[:], idxk[:], float(CB), None, op0=ALU.mult, wait=[t_idxa])
    k.v("tensor_tensor", tA2[:], tA2[:], rnkk[:], op=ALU.add)
    k.v("tensor_tensor", tA2[:], tA2[:], tB2[:], op=ALU.subtract)
    k.v("tensor_tensor", tA2[:], tA2[:], mlt2[:], op=ALU.mult)
    k.v("tensor_tensor", dstk[:], tB2[:], tA2[:], op=ALU.add)
    t_dst = k.v("tensor_scalar", dstk[:], dstk[:], float(YROWS - 1), None, op0=ALU.min)
    t_a32 = k.v("tensor_copy", a32[:], dstk[:])
    lt2 = sbP("ltile2", [128, E])
    mu2 = sbP("mu2", [128, 1]); sg2 = sbP("sig2", [128, 1])
    iv2b = [sbP("inv2a", [128, 1]), sbP("inv2b", [128, 1])]
    ms2 = sbP("musum2", [128, 1]); sq2 = sbP("ssq2", [128, 1]); vv2 = sbP("var2b", [128, 1])
    mm2t = sbP("m2t", [128, 1]); o1t = sbP("o1t", [128, E]); ost = sbP("ost", [128, E])
    musig2 = [sbP("musig2a", [128, 2]), sbP("musig2b", [128, 2])]
    musigT2 = [sbP("musigT2a", [2, 128]), sbP("musigT2b", [2, 128])]
    phDw = ExitStack()
    swg = phDw.enter_context(nc.sbuf_tensor("swg", [128, KC, FF], BF16))
    swu = phDw.enter_context(nc.sbuf_tensor("swu", [128, KC, FF], BF16))
    swd = phDw.enter_context(nc.sbuf_tensor("swd", [128, FC, D], BF16))
    t_swg = k.dma("g", out=swg[:], in_=swg_h.rearrange("(c p) f -> p c f", p=128))
    t_swu = k.dma("g", out=swu[:], in_=swu_h.rearrange("(c p) f -> p c f", p=128))
    t_swd = k.dma("g", out=swd[:], in_=swd_h.rearrange("(c p) f -> p c f", p=128))
    prev_g2 = ("v", 0)
    a2_ps3 = t_ubbe
    for i in range(NT):
        for kk in range(2):
            prev_g2 = k.gop("indirect_dma_start", tabf_d.rearrange("(a b) -> a b", b=1),
                            bass.IndirectOffsetOnAxis(a32[:, i, kk:kk + 1], 0), tokv[:, i:i + 1], None,
                            wait=([t_a32, cl["tv"], t_tz] if (i == 0 and kk == 0) else [prev_g2]))
            scat_toks.append(prev_g2)
            k.dma("y", out=bgidx_d[kk, 128 * i:128 * (i + 1)].rearrange("(p a) -> p a", a=1),
                  in_=dstk[:, i, kk:kk + 1], wait=[t_dst] if (i == 0 and kk == 0) else [])
    a_marks = k.marks()

    def _finish():
        for i in range(K.NSLOT):
            if k.qcnt["y"][i] > 0:
                nc.sync.wait_ge(k.qsem["y"][i], k.qcnt["y"][i])
            if k.qcnt["g"][i] > 0:
                nc.sync.wait_ge(k.qsem["g"][i], k.qcnt["g"][i])
        nc.compile()
        return nc
    if PH <= 1:
        return _finish()

    bf_state = {"step": 0, "ld": [None, None],
                "prev_ev": ("s", 0), "prev_tp": ("s", 0), "prev_users": [("t", 0), ("t", 0)],
                "prev_lev": ("s", 0), "prev_cnt": ("t", 0), "guard": list(a_marks[:3]),
                "pending": None, "started": False,
                "msev": [("s", 0), ("s", 0)], "iv": [("v", 0), ("v", 0)]}
    fm2 = fmscr
    sq2scr = sqscr
    bf_tp = ps_[4]
    bf_log = ps_[5]
    NPAIR = NT // 2

    def bf_load(g):
        st = bf_state
        b2, p2 = divmod(g, NPAIR)
        st["ld"][g % 2] = k.dma("y", out=xfb[g % 2][:],
                                in_=xf[b2 * T + 256 * p2: b2 * T + 256 * (p2 + 1), :],
                                wait=[st["prev_users"][g % 2]] + st["guard"])
        st["guard"] = []

    def bf_stats(s2):
        st = bf_state
        par = s2 % 2
        j2 = s2 % NT
        buf = xfb[(j2 // 2) % 2][:, j2 % 2, :]
        ld_tok = st["ld"][(j2 // 2) % 2]
        k.s("activation", sq2scr[:], buf, AF.Copy, accum_out=ms2[:], wait=[ld_tok])
        t_s2 = k.s("activation", sq2scr[:], buf, AF.Square, accum_out=sq2[:])
        k.v("tensor_scalar", mu2[:], ms2[:], 1.0 / D, None, op0=ALU.mult, wait=[t_s2])
        k.v("tensor_tensor", mm2t[:], mu2[:], mu2[:], op=ALU.mult)
        k.v("tensor_scalar", vv2[:], sq2[:], 1.0 / D, None, op0=ALU.mult)
        t_vv = k.v("tensor_tensor", vv2[:], vv2[:], mm2t[:], op=ALU.subtract)
        t_sg = k.s("activation", sg2[:], vv2[:], AF.Sqrt, bias=epsln[:], scale=1.0, wait=[t_vv, t_eps])
        st["iv"][par] = k.v("reciprocal", iv2b[par][:], sg2[:], wait=[t_sg, st["prev_lev"]])
        k.v("tensor_copy", musig2[par][:, 0:1], mu2[:])
        t_msg = k.v("tensor_copy", musig2[par][:, 1:2], sg2[:])
        t_mst = k.t("transpose", bf_log[0:2, 128 + 128 * par:256 + 128 * par], musig2[par][:],
                    ident[:], wait=[t_msg, st["prev_lev"]])
        st["msev"][par] = k.s("copy", musigT2[par][:], bf_log[0:2, 128 + 128 * par:256 + 128 * par],
                              wait=[t_mst])

    def bf_pump(n):
        st = bf_state
        for _ in range(n):
            step = st["step"]
            if step >= B * NT:
                return
            b, j = divmod(step, NT)
            pair, jj = divmod(j, 2)
            bufi = pair % 2
            par = step % 2
            g = step // 2
            if not st["started"]:
                bf_load(g)
                bf_stats(0)
                st["started"] = True
            if st["pending"] is not None:
                pb, pj, p_os = st["pending"]
                st["prev_cnt"] = k.t("matmul", off_bank[0:1, 16 + 16 * pb:32 + 16 * pb], onescol[:],
                                     ost[:], start=(pj == 0), stop=(pj == NT - 1),
                                     wait=[p_os] + (a_marks[:3] if (pb == 0 and pj == 0) else []))
                st["pending"] = None
            buf = xfb[bufi][:, jj, :]
            ld_tok = st["ld"][bufi]
            if jj == 0 and g + 1 < B * NPAIR:
                bf_load(g + 1)
            tts = []
            for h2 in range(2):
                for jb in range(4):
                    kc = h2 * 4 + jb
                    tts.append(k.t("transpose", bf_tp[:, jb * 128:(jb + 1) * 128],
                                   buf[:, kc * 128:(kc + 1) * 128],
                                   ident[:], wait=[ld_tok, st["prev_tp"], st["prev_ev"]]))
                ev = k.s("copy", fm2[:, h2 * 512:(h2 + 1) * 512], bf_tp[:, 0:512], wait=[tts[-1]])
                st["prev_tp"] = ev
            st["prev_ev"] = st["prev_tp"]
            for kc in range(KC):
                k.t("matmul", bf_log[:, 0:16], fm2[:, kc * 128:(kc + 1) * 128], rw_sb[:, kc, :],
                    start=(kc == 0), stop=False,
                    wait=[st["prev_ev"], st["prev_lev"]] if kc == 0 else [], inc=False)
            t_rmm = k.t("matmul", bf_log[:, 0:16], musigT2[par][:], rw2[:], start=False, stop=True,
                        wait=[st["msev"][par], t_rw2a, t_rw2b])
            st["prev_users"][bufi] = t_rmm
            t_lev = k.s("activation", lt2[:], bf_log[:, 0:16], AF.Identity, scale=iv2b[par][:],
                        wait=[t_rmm, st["iv"][par], st.get("flog_free", ("v", 0))])
            st["prev_lev"] = t_lev
            if debug:
                st["flog_free"] = k.dma("y", out=dbg["flog"][:, b * NT + j, :], in_=lt2[:], wait=[t_lev])
            k.v("reduce_max", m1[:], lt2[:], axis=AX.X, wait=[t_lev, st["prev_cnt"]])
            k.v("tensor_scalar", o1t[:], lt2[:], m1[:], None, op0=ALU.is_equal)
            k.v("tensor_scalar", o1t[:], o1t[:], BIG, None, op0=ALU.mult)
            k.v("tensor_tensor", o1t[:], lt2[:], o1t[:], op=ALU.subtract)
            k.v("reduce_max", m2[:], o1t[:], axis=AX.X)
            t_os = k.v("tensor_scalar", ost[:], lt2[:], m2[:], None, op0=ALU.is_ge)
            st["pending"] = (b, j, t_os)
            if step + 1 < B * NT:
                bf_stats(step + 1)
            st["step"] += 1

    cast_state = {"step": 0}
    cast_toks = [[None, None, None] for _ in range(E)]

    def cast_pump(n):
        for _ in range(n):
            s3 = cast_state["step"]
            if s3 >= 3 * E:
                return
            e3, m3 = divmod(s3, 3)
            src3, dst3 = [(wg_h, wgc_d), (wu_h, wuc_d), (wd_h, wdc_d)][m3]
            cast_toks[e3][m3] = k.dma("g", out=dst3[e3], in_=src3[e3])
            cast_state["step"] += 1

    def bf_flush():
        st = bf_state
        if st["pending"] is not None:
            pb, pj, p_os = st["pending"]
            st["prev_cnt"] = k.t("matmul", off_bank[0:1, 16 + 16 * pb:32 + 16 * pb], onescol[:],
                                 ost[:], start=(pj == 0), stop=(pj == NT - 1), wait=[p_os])
            st["pending"] = None

    with ExitStack() as phD:
        gs8 = phD.enter_context(nc.sbuf_tensor("gs8", [128, FC, 512], BF16))
        hs8 = phD.enter_context(nc.sbuf_tensor("hs8", [128, FC, 512], BF16))
        yst = [phD.enter_context(nc.sbuf_tensor(f"yst{j}", [128, D], BF16)) for j in range(2)]
        pr = [("s", 0), ("s", 0)]
        prd = ("s", 0)
        ys_free = [("y", 0), ("y", 0)]
        nbank = [0, 1, 3, 7]
        pr4 = [("s", 0)] * 4
        prd2 = [("s", 0), ("s", 0)]
        for g8 in range(4):
            c0 = g8 * 512
            for mt in range(FC):
                bank = ps_[nbank[mt % 4]]
                for kc in range(KC):
                    k.t("matmul", bank[:, 0:512], swg[:, kc, mt * 128:(mt + 1) * 128],
                        xnfm[:, kc, c0:c0 + 512], start=(kc == 0), stop=(kc == KC - 1),
                        wait=[t_swg, pr4[mt % 4]] + (prd2 if mt < 4 else [])
                             + (a_marks[:3] if (g8 == 0 and mt < 4) else []),
                        inc=(kc == KC - 1))
                tmm = ("t", k.cnt["t"])
                pr4[mt % 4] = k.s("activation", gs8[:, mt, :], bank[:, 0:512], AF.Silu, wait=[tmm])
            for mt in range(FC):
                bank = ps_[nbank[mt % 4]]
                for kc in range(KC):
                    k.t("matmul", bank[:, 0:512], swu[:, kc, mt * 128:(mt + 1) * 128],
                        xnfm[:, kc, c0:c0 + 512], start=(kc == 0), stop=(kc == KC - 1),
                        wait=[t_swu, pr4[mt % 4]], inc=(kc == KC - 1))
                tmm = ("t", k.cnt["t"])
                pr4[mt % 4] = k.v("tensor_tensor", hs8[:, mt, :], bank[:, 0:512], gs8[:, mt, :],
                                 op=ALU.mult, wait=[tmm, pr4[mt % 4]])
            for it in range(4):
                ybuf = yst[it % 2]
                bank = ps_[2 if it % 2 == 0 else 0]
                for nt2 in range(2):
                    for kc in range(FC):
                        k.t("matmul", bank[:, 0:512], hs8[:, kc, it * 128:(it + 1) * 128],
                            swd[:, kc, nt2 * 512:(nt2 + 1) * 512], start=(kc == 0), stop=(kc == FC - 1),
                            wait=([t_swd, prd2[it % 2], pr4[0], pr4[1], pr4[2], pr4[3]]
                                  + (a_marks if g8 == 0 else [])) if kc == 0 else [],
                            inc=(kc == FC - 1))
                    tmm = ("t", k.cnt["t"])
                    prd2[it % 2] = k.s("copy", ybuf[:, nt2 * 512:(nt2 + 1) * 512], bank[:, 0:512],
                              wait=[tmm, ys_free[it % 2]])
                ys_free[it % 2] = k.dma("y", out=ysh_d[c0 + it * 128:c0 + (it + 1) * 128, :],
                                        in_=ybuf[:], wait=[prd2[it % 2]])
            bf_pump(16)
            cast_pump(6)
        d_marks = k.marks()

    if PH <= 2:
        return _finish()
    phDw.close()
    t_tls = []
    for e in range(E + NU):
        t_tls.append(k.dma("y", out=tabf16[:, e * COLS:(e + 1) * COLS],
                           in_=tabf_d[e * CB:(e + 1) * CB].rearrange("(s p) -> p s", p=16),
                           wait=scat_toks if e == 0 else [], allow_slow_non_contiguous=True))
    t_tc = k.v("tensor_copy", idx_all[0:16, :], tabf16[:], wait=t_tls)
    rep_toks = []
    for r in range(1, 8):
        rep_toks.append(k.dma("y", out=idx_all[16 * r:16 * (r + 1), :], in_=idx_all[0:16, :], wait=[t_tc]))

    with ExitStack() as phC:
        wqsb = phC.enter_context(nc.sbuf_tensor("wqsb", [128, KC, D], BF16))
        wksb = phC.enter_context(nc.sbuf_tensor("wksb", [128, KC, D], BF16))
        wosb = phC.enter_context(nc.sbuf_tensor("wosb", [128, KC, D], BF16))
        qsb = phC.enter_context(nc.sbuf_tensor("qsb", [128, KC, 512], BF16))
        ksb = phC.enter_context(nc.sbuf_tensor("ksb", [128, KC, 512], BF16))
        scsb = phC.enter_context(nc.sbuf_tensor("scsb", [128, KC, 512], BF16))
        qq = phC.enter_context(nc.sbuf_tensor("qq", [128, 512], BF16))
        xsg = phC.enter_context(nc.sbuf_tensor("xsg", [128, KC, NSAMP], BF16))
        xst = phC.enter_context(nc.sbuf_tensor("xst", [128, 2, D], BF16))
        sst = [phC.enter_context(nc.sbuf_tensor(f"sst{j}", [128, D], BF16)) for j in range(2)]
        t_wq = k.dma("g", out=wqsb[:], in_=wq_h.rearrange("(c p) f -> p c f", p=128), wait=d_marks[:1])
        t_wk = k.dma("g", out=wksb[:], in_=wk_h.rearrange("(c p) f -> p c f", p=128))
        t_wo = k.dma("g", out=wosb[:], in_=wo_h.rearrange("(c p) f -> p c f", p=128))
        t_wes, t_wrs = [], []
        wesb0 = phC.enter_context(nc.sbuf_tensor("wesb0", [128, H, LOWR], BF16))
        wesb1 = phC.enter_context(nc.sbuf_tensor("wesb1", [72, H, LOWR], BF16))
        wrsb0 = phC.enter_context(nc.sbuf_tensor("wrsb0", [128, H, LOWR], BF16))
        wrsb1 = phC.enter_context(nc.sbuf_tensor("wrsb1", [72, H, LOWR], BF16))
        for h in range(H if CS >= 3 else 0):
            t_wes.append(k.dma("g", out=wesb0[:, h, :], in_=we_h[h, 0:128, :]))
            t_wes.append(k.dma("g", out=wesb1[:, h, :], in_=we_h[h, 128:200, :]))
            t_wrs.append(k.dma("g", out=wrsb0[:, h, :], in_=wr_h[h, 0:128, :]))
            t_wrs.append(k.dma("g", out=wrsb1[:, h, :], in_=wr_h[h, 128:200, :]))
        if CS < 2:
            t_z1 = k.v("memset", scWe[:], 0.0)
            t_z2 = k.v("memset", scWr[:], 0.0)
            c_prep = k.marks()
        t_xsg = None if CS < 2 else k.gop("dma_gather", xsg[:], xnrows[:], sampidx[:], NSAMP, NSAMP, D,
                      transpose=True, wait=xnb_free[0] + xnb_free[1] + [cl["si"]])
        t_xstv = []
        if CS < 2:
            pass
        else:
            ps3b = ps_[3][:].bitcast(BF16)
            prev_cp = ("v", 0)
            for h2 in range(2):
                for g4 in range(2):
                    tt = None
                    for j4 in range(4):
                        kc = g4 * 4 + j4
                        tt = k.t("transpose", ps3b[:, j4 * 128:(j4 + 1) * 128],
                                 xsg[:, kc, h2 * 128:(h2 + 1) * 128], identb[:],
                                 wait=[t_xsg, cl["idb"], prev_cp])
                    prev_cp = k.v("tensor_copy", xst[:, h2, g4 * 512:(g4 + 1) * 512],
                                  ps3b[:, 0:512], wait=[tt])
                    t_xstv.append(prev_cp)
            t_z1 = k.v("memset", scWe[:], 0.0)
            t_z2 = k.v("memset", scWr[:], 0.0)
            prev_we_ev = ("s", 0)
            for h in range(H if CS >= 3 else 0):
                kc, par = h // 2, h % 2
                for (w0, w1, dstt, colbase, wtoks) in [(wesb0, wesb1, scWe, 0, t_wes), (wrsb0, wrsb1, scWr, 32, t_wrs)]:
                    k.t("matmul", ps_[7][0:64, 0:LOWR], xst[:, 0, h * 64:(h + 1) * 64], w0[:, h, :],
                        start=True, stop=False,
                        wait=t_xstv[-1:] + [prev_we_ev, wtoks[2 * h]], inc=False)
                    tmm = k.t("matmul", ps_[7][0:64, 0:LOWR], xst[0:72, 1, h * 64:(h + 1) * 64], w1[:, h, :],
                              start=False, stop=True, wait=[wtoks[2 * h + 1]])
                    prev_we_ev = k.s("copy",
                                     dstt[par * 64:(par + 1) * 64, kc, par * 64 + colbase:par * 64 + colbase + LOWR],
                                     ps_[7][0:64, 0:LOWR], wait=[tmm, t_z1, t_z2])
        c_prep = k.marks()
        pr = [("s", 0), ("s", 0)]
        prs = ("v", 0)
        prq = ("v", 0)
        prqL = [("v", 0), ("v", 0)]
        prm = ("s", 0)
        ss_free = [("y", 0), ("y", 0)]
        sq_ev = ("s", 0)
        binv_ev = ("v", 0)
        nbank = [0, 1]
        for g8 in range(4):
            c0 = g8 * 512
            for (wsb, t_w, bias_t, qdst, invdst, invbdst) in [
                    (wqsb, t_wq, bqs, qsb, invq, invqb), (wksb, t_wk, bks, ksb, invk_, invkb)]:
                for mt in range(KC):
                    bank = ps_[nbank[mt % 2]]
                    for kc in range(KC):
                        k.t("matmul", bank[:, 0:512], wsb[:, kc, mt * 128:(mt + 1) * 128],
                            xnfm[:, kc, c0:c0 + 512], start=(kc == 0), stop=(kc == KC - 1),
                            wait=[t_w, pr[mt % 2]] + (prqL if mt < 2 else [])
                                 + (d_marks if (g8 == 0 and mt < 2) else []),
                            inc=(kc == KC - 1))
                    tmm = ("t", k.cnt["t"])
                    t_ev = k.s("activation", qdst[:, mt, :], bank[:, 0:512], AF.Identity,
                               bias=bias_t[:, mt:mt + 1], scale=1.0, wait=[tmm])
                    t_sq = k.s("activation", qq[:], qdst[:, mt, :], AF.Square, wait=[t_ev])
                    pr[mt % 2] = t_sq
                    k.t("matmul", ps_[3][0:16, 0:512], headsel[:, mt, :], qq[:],
                        start=(mt == 0), stop=(mt == KC - 1),
                        wait=[t_sq, cl["hs"]] + (([sq_ev, binv_ev, prm] + (c_prep if g8 == 0 else [])) if mt == 0 else []),
                        inc=(mt == KC - 1))
                t_ssq = ("t", k.cnt["t"])
                t_sqr = k.s("activation", invdst[0:16, :],
                            ps_[3][0:16, 0:512], AF.Sqrt, wait=[t_ssq])
                sq_ev = t_sqr
                k.v("tensor_scalar", invdst[0:16, :], invdst[0:16, :],
                    1e-12, None, op0=ALU.max, wait=[t_sqr])
                k.v("reciprocal", invdst[0:16, :], invdst[0:16, :])
                t_ivb = k.v("tensor_copy", invbdst[0:16, :], invdst[0:16, :])
                for kc in range(KC):
                    tbm = k.t("matmul", ps_[7][:, 0:512], headbc[:, kc, :], invbdst[0:16, :],
                              start=True, stop=True, wait=[t_ivb, cl["hb"], binv_ev])
                    binv_ev = k.v("tensor_tensor", qdst[:, kc, :], qdst[:, kc, :], ps_[7][:, 0:512],
                                  op=ALU.mult, wait=[tbm, pr[0], pr[1]])
            for kc in range(KC if CS >= 4 else 0):
                k.t("matmul", ps_[2][:, 0:512], scWe[:, kc, :], qsb[:, kc, :], start=True, stop=False,
                    wait=[binv_ev, prs, prm] + (c_prep if (g8 == 0 and kc == 0) else []), inc=False)
                tmm = k.t("matmul", ps_[2][:, 0:512], scWr[:, kc, :], ksb[:, kc, :], start=False, stop=True)
                prs = k.v("tensor_copy", scsb[:, kc, :], ps_[2][:, 0:512], wait=[tmm])
                tmm2 = k.t("matmul", ps_[3][:, 0:512], wcpad[:], scsb[:, kc, :], start=True, stop=True,
                           wait=[prs, t_wcc, prm] + ([sq_ev] if kc == 0 else []))
                prm = k.s("activation", scsb[:, kc, :], ps_[3][:, 0:512], AF.Identity,
                          bias=bcrep[:], scale=1.0, wait=[tmm2, t_bc1, t_bc2])
            for it in range(4 if CS >= 5 else 0):
                sbuf_ = sst[it % 2]
                bank = ps_[0 if it % 2 == 0 else 1]
                for nt2 in range(2):
                    for kc in range(KC):
                        k.t("matmul", bank[:, 0:512], scsb[:, kc, it * 128:(it + 1) * 128],
                            wosb[:, kc, nt2 * 512:(nt2 + 1) * 512], start=(kc == 0), stop=(kc == KC - 1),
                            wait=[t_wo, prm, prq, binv_ev, pr[0], pr[1]] if kc == 0 else [], inc=(kc == KC - 1))
                    tmm = ("t", k.cnt["t"])
                    prq = k.v("tensor_tensor", sbuf_[:, nt2 * 512:(nt2 + 1) * 512], bank[:, 0:512],
                              bbo[:, nt2 * 512:(nt2 + 1) * 512], op=ALU.add,
                              wait=[tmm, bbo_toks[nt2], ss_free[it % 2]])
                    prqL[it % 2] = prq
                ss_free[it % 2] = k.dma("y", out=stm_d[c0 + it * 128:c0 + (it + 1) * 128, :],
                                        in_=sbuf_[:], wait=[prq])
            bf_pump(16)
            cast_pump(6)
        c_marks = k.marks()

    if PH <= 3:
        return _finish()
    bf_pump(B * NT)
    cast_pump(3 * E)
    bf_flush()
    phAB.close()

    cnt_evs = []
    for b in range(B):
        cnt_evs.append(k.s("copy", cnt_row[0:1, E * b:E * (b + 1)], off_bank[0:1, 16 + 16 * b:32 + 16 * b],
                           wait=[bf_state["prev_cnt"]]))
    cnt_evs = [k.dma("y", out=cnt_d.rearrange("(a b) -> a b", a=1), in_=cnt_row[:], wait=cnt_evs)]
    cnt_evs = [k.dma("y", out=cntT[:], in_=cnt_d.rearrange("(p e) -> p e", e=E), wait=cnt_evs)]
    if debug:
        k.dma("y", out=dbg["cnt"][:], in_=cntT[:], wait=cnt_evs)
    t_pm = k.t("matmul", ps_[5][0:1, 0:16], mask8[:], cntT[:], start=True, stop=True,
               wait=cnt_evs + [cl["m8"]])
    t_pev = k.s("copy", prow[:], ps_[5][0:1, 0:16], wait=[t_pm])
    t_pb = k.t("matmul", ps_[5][:, 16:32], ones1[0:1, 0:128], prow[:], start=True, stop=True, wait=[t_pev])
    t_pbe = k.s("copy", prior_b[:], ps_[5][:, 16:32], wait=[t_pb])

    io3c = iota16[:].rearrange("p (a e) -> p a e", a=1).to_broadcast([128, NT, E])
    pb3 = prior_b[:].rearrange("p (a e) -> p a e", a=1).to_broadcast([128, NT, E])
    for kkk in range(2):
        k.v("tensor_tensor", possb[:], io3c, idxk[:, :, kkk:kkk + 1].to_broadcast([128, NT, E]),
            op=ALU.is_equal, wait=[t_pbe, cl["io"]] if kkk == 0 else [])
        k.v("tensor_tensor", possb[:], possb[:], pb3, op=ALU.mult)
        k.v("tensor_reduce", pk16[:], possb[:], axis=AX.X, op=ALU.add)
        k.v("tensor_tensor", pk16[:], pk16[:], rnkk[:, :, kkk:kkk + 1], op=ALU.add)
        k.v("tensor_scalar", pk16[:], pk16[:], float(CAP), None, op0=ALU.is_lt)
        t_w = k.v("tensor_tensor", wts[:, :, kkk:kkk + 1], valk[:, :, kkk:kkk + 1], pk16[:], op=ALU.mult)
    wts_toks = [t_w] * NT
    if debug:
        k.dma("y", out=dbg["wts"][:], in_=wts[:], wait=[t_w])

    if PH <= 4:
        return _finish()
    with ExitStack() as phE:
        wgb = [phE.enter_context(nc.sbuf_tensor(f"wgb{j}", [128, KC, FF], BF16)) for j in range(2)]
        wub = phE.enter_context(nc.sbuf_tensor("wub", [128, KC, FF], BF16))
        wdb = phE.enter_context(nc.sbuf_tensor("wdb", [128, FC, D], BF16))
        hgb = [phE.enter_context(nc.sbuf_tensor(f"hgb{j}", [128, KC, CB], BF16)) for j in range(2)]
        gatb = phE.enter_context(nc.sbuf_tensor("gatb", [128, FC, CB], BF16))
        hbb = gatb
        yeb = xnb
        guard = c_marks
        st_ = {"gate_done": ("t", 0), "up_done": ("t", 0), "dn_done": ("t", 0),
               "pr": [("s", 0), ("s", 0)],
               "ye_free": [yexp_zero_toks[-2], yexp_zero_toks[-1]],
               "dma_y": lambda **kw: k.dma("y", **kw)}

        def moe_worker(wgbuf, wg_toks, hgbuf, t_hg_tok, wu_toks, wd_toks, yrow0, extra0):
            pr = st_["pr"]
            for mt in range(FC):
                bank = ps_[mt % 2]
                for kc in range(KC):
                    k.t("matmul", bank[:, 0:CB], wgbuf[:, kc, mt * 128:(mt + 1) * 128],
                        hgbuf[:, kc, :], start=(kc == 0), stop=(kc == KC - 1),
                        wait=[wg_toks[kc], t_hg_tok, pr[mt % 2]] + (extra0 if mt < 2 else []),
                        inc=(kc == KC - 1))
                tmm = ("t", k.cnt["t"])
                pr[mt % 2] = k.s("activation", gatb[:, mt, :], bank[:, 0:CB], AF.Silu, wait=[tmm])
            st_["gate_done"] = ("t", k.cnt["t"])
            for mt in range(FC):
                bank = ps_[mt % 2]
                for kc in range(KC):
                    k.t("matmul", bank[:, 0:CB], wub[:, kc, mt * 128:(mt + 1) * 128],
                        hgbuf[:, kc, :], start=(kc == 0), stop=(kc == KC - 1),
                        wait=[wu_toks[kc], pr[mt % 2]], inc=(kc == KC - 1))
                tmm = ("t", k.cnt["t"])
                pr[mt % 2] = k.v("tensor_tensor", hbb[:, mt, :], bank[:, 0:CB], gatb[:, mt, :],
                                 op=ALU.mult, wait=[tmm, pr[mt % 2]])
            st_["up_done"] = ("t", k.cnt["t"])
            prd = ("s", 0)
            for i3 in range(CB // 128):
                ybuf = yeb[i3 % 2]
                for nt2 in range(2):
                    for kc in range(FC):
                        k.t("matmul", ps_[2][:, 0:512], hbb[:, kc, i3 * 128:(i3 + 1) * 128],
                            wdb[:, kc, nt2 * 512:(nt2 + 1) * 512], start=(kc == 0), stop=(kc == FC - 1),
                            wait=([wd_toks[kc], prd, pr[0], pr[1]] + extra0) if kc == 0 else [wd_toks[kc]],
                            inc=(kc == FC - 1))
                    tmm = ("t", k.cnt["t"])
                    prd = k.s("copy", ybuf[:, nt2 * 512:(nt2 + 1) * 512], ps_[2][:, 0:512],
                              wait=[tmm, st_["ye_free"][i3 % 2]])
                st_["ye_free"][i3 % 2] = st_["dma_y"](out=yexp[yrow0 + i3 * 128:yrow0 + (i3 + 1) * 128, :],
                                                      in_=ybuf[:], wait=[prd])
            st_["dn_done"] = ("t", k.cnt["t"])

        t_wg = [None, None]
        t_hg = [None, None]
        t_wg[0] = k.dma("g", out=wgb[0][:], in_=wgc_d[0].rearrange("(c p) f -> p c f", p=128), wait=guard + [cast_toks[0][0]])
        t_hg[0] = k.gop("dma_gather", hgb[0][:], xnrows[:], idx_all[:, 0:COLS], CB, CB, D,
                        transpose=True, wait=[t_tc] + rep_toks + xnb_free[0] + xnb_free[1])
        for e in range(E):
            cur = e % 2
            if e + 1 < E:
                t_wg[1 - cur] = k.dma("g", out=wgb[1 - cur][:],
                                      in_=wgc_d[e + 1].rearrange("(c p) f -> p c f", p=128),
                                      wait=[st_["gate_done"], cast_toks[e + 1][0]])
                t_hg[1 - cur] = k.gop("dma_gather", hgb[1 - cur][:], xnrows[:],
                                      idx_all[:, (e + 1) * COLS:(e + 2) * COLS], CB, CB, D,
                                      transpose=True,
                                      wait=[st_["gate_done"], st_["up_done"], st_["dn_done"]] + rep_toks)
            t_wu = k.dma("g", out=wub[:], in_=wuc_d[e].rearrange("(c p) f -> p c f", p=128),
                         wait=[st_["up_done"], cast_toks[e][1]] + (guard if e == 0 else []))
            t_wd = k.dma("g", out=wdb[:], in_=wdc_d[e].rearrange("(c p) f -> p c f", p=128),
                         wait=[st_["dn_done"], cast_toks[e][2]] + (guard if e == 0 else []))
            moe_worker(wgb[cur], [t_wg[cur]] * KC, hgb[cur], t_hg[cur], [t_wu] * KC, [t_wd] * FC,
                       e * CB, guard if e == 0 else [])
        ET = mybir.EngineType
        sw_engs = [nc.tensor, nc.vector, nc.scalar, nc.sync]
        usems_pool = [[phE.enter_context(nc.semaphore(f"u{g}_{j}")) for j in range(4)] for g in range(2)]
        ucnt_pool = [[[0] for _ in range(4)] for g in range(2)]
        prev_hgu = [t_hg[0], t_hg[1]]
        for u in range(NU):
            t_hgu = k.gop("dma_gather", hgb[u % 2][:], xnrows[:],
                          idx_all[:, (E + u) * COLS:(E + u + 1) * COLS], CB, CB, D,
                          transpose=True,
                          wait=[st_["gate_done"], st_["up_done"], st_["dn_done"], prev_hgu[u % 2]] + rep_toks)
            prev_hgu[u % 2] = t_hgu
            usem_u = usems_pool[u % 2]
            ucnt = ucnt_pool[u % 2]
            tok_cnt = dict(k.cnt)
            ustart_tok = [c[0] for c in ucnt]
            nc.tensor.wait_ge(k.sem["t"], k.cnt["t"])
            nc.tensor.sem_inc(k.sem["t"], 1); k.cnt["t"] += 1
            nc.vector.wait_ge(k.sem["v"], k.cnt["v"])
            nc.vector.sem_inc(k.sem["v"], 1); k.cnt["v"] += 1
            nc.scalar.wait_ge(k.sem["s"], k.cnt["s"])
            nc.scalar.sem_inc(k.sem["s"], 1); k.cnt["s"] += 1
            for j in range(4):
                if ucnt[j][0] > 0:
                    nc.sync.wait_ge(usem_u[j], ucnt[j][0])
                nc.sync.sem_inc(usem_u[j], 16)
                ucnt[j][0] += 16
            sv_na = nc.values_load(ui32[0:1, 2 * u + 1:2 * u + 2],
                                   engines=[ET.PE, ET.DVE, ET.Activation, ET.SP],
                                   min_val=0, max_val=1)
            sv_eu = nc.values_load(ui32[0:1, 2 * u:2 * u + 1], engines=[ET.SP],
                                   min_val=0, max_val=E - 1)
            snap_cnt = dict(k.cnt)
            snap_lw = dict(k._lw)
            snap_wl = {kk2: set(vv2) for kk2, vv2 in k.wait_log.items()}
            ustart = [c[0] for c in ucnt]
            yesel = [0]
            old_dma_y = st_["dma_y"]
            for body in nc.Switch(engines=sw_engs, index=[sv_na] * 4, n=2):
                if body == 0:
                    st_["pr"] = [("s", tok_cnt["s"]), ("s", tok_cnt["s"])]
                    st_["gate_done"] = ("t", tok_cnt["t"])
                    st_["up_done"] = ("t", tok_cnt["t"])
                    st_["dn_done"] = ("t", tok_cnt["t"])
                    u_extra = [("v", tok_cnt["v"]), ("s", tok_cnt["s"])]

                    def u_dma_y(**kw):
                        j = 2 + yesel[0]
                        yesel[0] = 1 - yesel[0]
                        kw.setdefault("wait", [])
                        kw["wait"] = list(kw["wait"]) + [(usem_u[j], ustart_tok[j])]
                        return k.dma_sem("y", usem_u[j], ucnt[j], "dma_start", **kw)
                    st_["dma_y"] = u_dma_y

                    def chunk_loads(w_d, nchunk, fdim, dstbuf, first_waits):
                        toks = []
                        for kc in range(nchunk):
                            sj = kc % 2
                            ld = k.dma_sem("y", usem_u[sj], ucnt[sj], "dma_start",
                                           out=dstbuf[:, kc, 0:fdim],
                                           in_=w_d[sv_eu][kc * 128:(kc + 1) * 128, 0:fdim],
                                           wait=[(usem_u[sj], ucnt[sj][0])] + first_waits)
                            toks.append(ld)
                        return toks

                    wg_toks = chunk_loads(wgc_d, KC, FF, wgb[0], [st_["gate_done"]])
                    wu_toks = chunk_loads(wuc_d, KC, FF, wub, [st_["up_done"]])
                    wd_toks = chunk_loads(wdc_d, FC, D, wdb, [st_["dn_done"]])
                    moe_worker(wgb[0], wg_toks, hgb[u % 2], t_hgu, wu_toks, wd_toks,
                               E * CB + u * US, u_extra)
                    st_["dma_y"] = old_dma_y
                else:
                    def comp(eng, sem, lo, hi):
                        if hi <= lo:
                            return
                        pts = sorted(vv2 for vv2 in k.wait_log.get(id(sem), ())
                                     if lo < vv2 <= hi and vv2 not in snap_wl.get(id(sem), ()))
                        cur = lo
                        for p in pts:
                            if p > cur:
                                eng.sem_inc(sem, p - cur)
                                cur = p
                        if hi > cur:
                            eng.sem_inc(sem, hi - cur)
                    comp(nc.tensor, k.sem["t"], snap_cnt["t"], k.cnt["t"])
                    comp(nc.vector, k.sem["v"], snap_cnt["v"], k.cnt["v"])
                    comp(nc.scalar, k.sem["s"], snap_cnt["s"], k.cnt["s"])
                    for j in range(4):
                        if ucnt[j][0] > ustart[j] and ustart[j] > 0:
                            nc.sync.wait_ge(usem_u[j], ustart[j])
                        comp(nc.sync, usem_u[j], ustart[j], ucnt[j][0])
            k._lw = snap_lw
            nc.tensor.wait_ge(k.sem["t"], k.cnt["t"])
            nc.vector.wait_ge(k.sem["v"], k.cnt["v"])
            nc.scalar.wait_ge(k.sem["s"], k.cnt["s"])
            nc.tensor.sem_inc(k.sem["t"], 1); k.cnt["t"] += 1
            nc.vector.sem_inc(k.sem["v"], 1); k.cnt["v"] += 1
            nc.scalar.sem_inc(k.sem["s"], 1); k.cnt["s"] += 1
            for j in range(4):
                nc.sync.wait_ge(usem_u[j], ucnt[j][0])
                nc.sync.sem_inc(usem_u[j], 16)
                ucnt[j][0] += 16
            st_["pr"] = [("s", k.cnt["s"]), ("s", k.cnt["s"])]
            st_["gate_done"] = ("t", k.cnt["t"])
            st_["up_done"] = ("t", k.cnt["t"])
            st_["dn_done"] = ("t", k.cnt["t"])
        e_marks = k.marks()
        st_e_ye = st_["ye_free"]

    if PH <= 5:
        return _finish()
    with ExitStack() as phF:
        bga = phF.enter_context(nc.sbuf_tensor("bga", [128, NT, D], BF16))
        bgb = phF.enter_context(nc.sbuf_tensor("bgb", [128, NT, D], BF16))
        bgf = phF.enter_context(nc.sbuf_tensor("bgf", [16, T // 16], F32))
        bgi = [phF.enter_context(nc.sbuf_tensor(f"bgi{j}", [128, T // 16], I16)) for j in range(2)]
        scr1b = [phF.enter_context(nc.sbuf_tensor(f"scr1{j}", [128, D], F32)) for j in range(2)]
        scr2b = [phF.enter_context(nc.sbuf_tensor(f"scr2{j}", [128, D], F32)) for j in range(2)]
        xrt = [phF.enter_context(nc.sbuf_tensor(f"xrt{j}", [128, D], F32)) for j in range(2)]
        xnt = [phF.enter_context(nc.sbuf_tensor(f"xnt{j}", [128, D], BF16)) for j in range(2)]
        ysht = [phF.enter_context(nc.sbuf_tensor(f"ysht{j}", [128, D], BF16)) for j in range(2)]
        sstt = [phF.enter_context(nc.sbuf_tensor(f"sstt{j}", [128, D], BF16)) for j in range(2)]
        outb = [phF.enter_context(nc.sbuf_tensor(f"outb{j}", [128, D], F32)) for j in range(2)]
        FS = int(os.environ.get("MK_FSUB", "9"))
        gt = []
        mdbg_free = [("v", 0)]
        for kk in range(2 if FS >= 1 else 0):
            t_bls = []
            for qq_ in range(4):
                t_bls.append(k.dma("y", out=bgf[:, qq_ * 32:(qq_ + 1) * 32],
                             in_=bgidx_d[kk, qq_ * 512:(qq_ + 1) * 512].rearrange("(s p) -> p s", p=16),
                             wait=e_marks if qq_ == 0 else [], allow_slow_non_contiguous=True))
            t_bc_ = k.v("tensor_copy", bgi[kk][0:16, :], bgf[:], wait=t_bls)
            reps = [k.dma("y", out=bgi[kk][16 * r:16 * (r + 1), :], in_=bgi[kk][0:16, :], wait=[t_bc_])
                    for r in range(1, 8)]
            dstb = bga if kk == 0 else bgb
            for hh in range(2):
                gt.append(k.gop("dma_gather", dstb[:, hh * 8:(hh + 1) * 8, :], yexp[:],
                                bgi[kk][:, hh * 64:(hh + 1) * 64],
                                T // 2, T // 2, D, transpose=False,
                                wait=[t_bc_] + reps + [st_e_ye[0], st_e_ye[1]]))
        ld_toks = [[None] * 4, [None] * 4]
        st_free = [("v", 0), ("v", 0)]
        war_scr = [("v", 0), ("v", 0)]
        cons = [{}, {}]
        for i in range(NT if FS >= 2 else 0):
            cur = i % 2
            if i < 2:
                ld_toks[cur] = [
                    k.dma("y", out=xrt[cur][:], in_=xo[128 * i:128 * (i + 1), :], wait=e_marks),
                    k.dma("y", out=xnt[cur][:], in_=xnrows[128 * i:128 * (i + 1), :], wait=e_marks),
                    k.dma("y", out=ysht[cur][:], in_=ysh_d[128 * i:128 * (i + 1), :], wait=e_marks),
                    k.dma("y", out=sstt[cur][:], in_=stm_d[128 * i:128 * (i + 1), :], wait=e_marks),
                ]
            scr1 = scr1b[cur]; scr2 = scr2b[cur]
            t_s2 = k.s("activation", scr1[:], bga[:, i, :], AF.Identity, scale=wts[:, i, 0:1],
                       wait=gt + [wts_toks[i], mdbg_free[0], war_scr[cur]])
            t_s2 = k.s("activation", scr2[:], bgb[:, i, :], AF.Identity, scale=wts[:, i, 1:2])
            t_moe = k.v("tensor_tensor", scr1[:], scr1[:], scr2[:], op=ALU.add, wait=[t_s2])
            if debug and "moe" in dbg:
                mdbg_free[0] = k.dma("y", out=dbg["moe"][128 * i:128 * (i + 1), :], in_=scr1[:], wait=[t_moe])
            t_xy = k.v("tensor_tensor", scr2[:], xnt[cur][:], ysht[cur][:], op=ALU.add, wait=ld_toks[cur][1:3])
            k.v("tensor_tensor", scr1[:], scr1[:], scr2[:], op=ALU.add, wait=[mdbg_free[0]])
            t_g = k.s("activation", scr2[:], scr1[:], AF.Silu, wait=[("v", k.cnt["v"])])
            t_sg = k.v("tensor_tensor", scr1[:], sstt[cur][:], scr2[:], op=ALU.mult,
                       wait=[t_g, ld_toks[cur][3]])
            t_out = k.v("tensor_tensor", outb[cur][:], xrt[cur][:], scr1[:], op=ALU.add,
                        wait=[ld_toks[cur][0], st_free[cur]])
            war_scr[cur] = t_out
            st_free[cur] = k.dma("y", out=out_h[128 * i:128 * (i + 1), :], in_=outb[cur][:], wait=[t_out])
            if i + 2 < NT:
                ld_toks[cur] = [
                    k.dma("y", out=xrt[cur][:], in_=xo[128 * (i + 2):128 * (i + 3), :], wait=[t_out]),
                    k.dma("y", out=xnt[cur][:], in_=xnrows[128 * (i + 2):128 * (i + 3), :], wait=[t_xy]),
                    k.dma("y", out=ysht[cur][:], in_=ysh_d[128 * (i + 2):128 * (i + 3), :], wait=[t_xy]),
                    k.dma("y", out=sstt[cur][:], in_=stm_d[128 * (i + 2):128 * (i + 3), :], wait=[t_sg]),
                ]

    if debug:
        k.dma("y", out=dbg["s"][:], in_=stm_d[:])
        k.dma("y", out=dbg["ysh"][:], in_=ysh_d[:])
    return _finish()


def make_consts():
    c = {}
    c["c_ident"] = np.eye(128, dtype=np.float32)
    c["c_identb"] = np.eye(128, dtype=ml_dtypes.bfloat16)
    c["c_triu"] = np.triu(np.ones((128, 128), dtype=np.float32), k=1)
    c["c_iota16"] = np.tile(np.arange(E, dtype=np.float32), (128, 1))
    c["c_t1"] = c["c_iota16"] - BIG
    c["c_iotatok"] = np.arange(128, dtype=np.float32).reshape(128, 1)
    c["c_tokv"] = (np.arange(NT, dtype=np.float32)[None, :] * 128
                   + np.arange(128, dtype=np.float32)[:, None]).astype(np.float32)
    hs = np.zeros((128, KC, E), dtype=ml_dtypes.bfloat16)
    for dl in range(128):
        for kc in range(KC):
            hs[dl, kc, 2 * kc + (dl >= 64)] = 1
    c["c_headsel"] = hs
    hb = np.zeros((16, KC, 128), dtype=ml_dtypes.bfloat16)
    for h in range(16):
        for kc in range(KC):
            for col in range(128):
                if h == 2 * kc + (col >= 64):
                    hb[h, kc, col] = 1
    c["c_headbc"] = hb
    samp = np.linspace(0.0, T - 1, LEN).astype(np.int32)
    sampp = np.zeros(NSAMP, dtype=np.int16)
    sampp[:LEN] = samp
    si = np.zeros((16, NSAMP // 16), dtype=np.int16)
    for j in range(NSAMP):
        si[j % 16, j // 16] = sampp[j]
    c["c_sampidx"] = np.tile(si, (8, 1))
    return c


_NC_CACHE = {}


def kernel(L0_attn_x, L0_attn_Wq, L0_attn_bq, L0_attn_Wk, L0_attn_bk, L0_attn_We,
           L0_attn_Wr, L0_attn_Wc, L0_attn_bc, L0_attn_Wo, L0_attn_bo,
           L0_ffn_router_w, L0_ffn_router_b, L0_ffn_w_gate, L0_ffn_w_up,
           L0_ffn_w_down, L0_ffn_sw_gate, L0_ffn_sw_up, L0_ffn_sw_down,
           debug=False):
    if ("nc", debug) not in _NC_CACHE:
        _NC_CACHE[("nc", debug)] = build(debug=debug)
    nc = _NC_CACHE[("nc", debug)]
    consts = make_consts()
    x = np.ascontiguousarray(np.asarray(L0_attn_x, dtype=np.float32))
    xflat = x.reshape(B * T, D)
    shared = {
        "xfull": xflat,
        "Wq": np.asarray(L0_attn_Wq, np.float32), "bq": np.asarray(L0_attn_bq, np.float32),
        "Wk": np.asarray(L0_attn_Wk, np.float32), "bk": np.asarray(L0_attn_bk, np.float32),
        "We": np.asarray(L0_attn_We, np.float32), "Wr": np.asarray(L0_attn_Wr, np.float32),
        "Wc": np.asarray(L0_attn_Wc, np.float32), "bc": np.asarray(L0_attn_bc, np.float32),
        "Wo": np.asarray(L0_attn_Wo, np.float32), "bo": np.asarray(L0_attn_bo, np.float32),
        "router_w": np.asarray(L0_ffn_router_w, np.float32),
        "router_b": np.asarray(L0_ffn_router_b, np.float32),
        "w_gate": np.asarray(L0_ffn_w_gate, np.float32),
        "w_up": np.asarray(L0_ffn_w_up, np.float32),
        "w_down": np.asarray(L0_ffn_w_down, np.float32),
        "sw_gate": np.asarray(L0_ffn_sw_gate, np.float32),
        "sw_up": np.asarray(L0_ffn_sw_up, np.float32),
        "sw_down": np.asarray(L0_ffn_sw_down, np.float32),
    }
    shared.update(consts)
    in_maps = []
    for c in range(B):
        m = dict(shared)
        m["xown"] = x[c]
        m["c_mask8"] = (np.arange(8) < c).astype(np.float32).reshape(8, 1)
        in_maps.append(m)
    res = run_bass_kernel_spmd(nc, in_maps, core_ids=list(range(B)))
    out = np.stack([res.results[c]["out"] for c in range(B)], axis=0)
    if debug:
        return out, res
    return out


# version 09
# speedup vs baseline: 1.0170x; 1.0170x over previous
import sys
if "/opt/trn_rl_repo" not in sys.path:
    sys.path.insert(0, "/opt/trn_rl_repo")
import numpy as np
import ml_dtypes
import concourse.bacc as bacc_mod
import concourse.bass as bass
import concourse.mybir as mybir
from concourse.bass_utils import run_bass_kernel_spmd
from contextlib import ExitStack

F32 = mybir.dt.float32
BF16 = mybir.dt.bfloat16
I16 = mybir.dt.int16
I32 = mybir.dt.int32
AF = mybir.ActivationFunctionType
ALU = mybir.AluOpType
AX = mybir.AxisListType

B, T, D = 8, 2048, 1024
H, DK, LOWR, LEN = 16, 64, 20, 200
E, FF, CAP = 16, 1408, 2560
CB = 384
NU = 10
US = 384
YROWS = E * CB + NU * US
NT = T // 128
KC = D // 128
FC = FF // 128
NSAMP = 256
BIG = 1024.0
COLS = CB // 16


class K:
    NSLOT = 8

    def __init__(self, nc, stack):
        self.nc = nc
        self.sem = {}
        self.cnt = {}
        for e in ("v", "s", "t"):
            self.sem[e] = stack.enter_context(nc.semaphore("ch_" + e))
            self.cnt[e] = 0
        self.engs = {"v": nc.vector, "s": nc.scalar, "t": nc.tensor,
                     "g": nc.gpsimd, "y": nc.sync}
        self.qsem, self.qcnt, self.qidx = {}, {}, {}
        for q in ("y", "g"):
            self.qsem[q] = [stack.enter_context(nc.semaphore(f"q_{q}{i}")) for i in range(self.NSLOT)]
            self.qcnt[q] = [0] * self.NSLOT
            self.qidx[q] = 0
        self._lw = {}
        self.wait_log = {}

    def _resolve(self, tok):
        se, vv = tok
        return (self.sem[se], vv) if isinstance(se, str) else (se, vv)

    def _wait(self, ekey, eng, tok):
        if tok[1] <= 0:
            return
        s, vv = self._resolve(tok)
        key = (ekey, id(s))
        if self._lw.get(key, -1) >= vv:
            return
        eng.wait_ge(s, vv)
        self._lw[key] = vv
        self.wait_log.setdefault(id(s), set()).add(vv)

    def op(self, e, method, *args, wait=(), inc=True, **kw):
        eng = self.engs[e]
        w = list(wait)
        if e in ("v", "s"):
            w.append((e, self.cnt[e]))
        for tok in w:
            self._wait(e, eng, tok)
        inst = getattr(eng, method)(*args, **kw)
        if inc:
            self.cnt[e] += 1
            inst.then_inc(self.sem[e], 1)
        return (e, self.cnt[e])

    def _q(self, q, method, args, kw, wait):
        eng = self.engs[q]
        i = self.qidx[q] % self.NSLOT
        self.qidx[q] += 1
        s = self.qsem[q][i]
        if self.qcnt[q][i] > 0:
            self._wait(q, eng, (s, self.qcnt[q][i]))
        for tok in wait:
            self._wait(q, eng, tok)
        inst = getattr(eng, method)(*args, **kw)
        self.qcnt[q][i] += 16
        inst.then_inc(s, 16)
        return (s, self.qcnt[q][i])

    def dma(self, q, *args, wait=(), **kw):
        return self._q(q, "dma_start", args, kw, wait)

    def dma_sem(self, q, sem, cnt_ref, method, *args, wait=(), **kw):
        eng = self.engs[q]
        for tok in wait:
            self._wait(q, eng, tok)
        inst = getattr(eng, method)(*args, **kw)
        cnt_ref[0] += 16
        inst.then_inc(sem, 16)
        return (sem, cnt_ref[0])

    def gop(self, method, *args, wait=(), **kw):
        return self._q("g", method, args, kw, wait)

    def v(self, m, *a, **kw): return self.op("v", m, *a, **kw)
    def s(self, m, *a, **kw): return self.op("s", m, *a, **kw)
    def t(self, m, *a, **kw): return self.op("t", m, *a, **kw)

    def marks(self):
        toks = [("v", self.cnt["v"]), ("s", self.cnt["s"]), ("t", self.cnt["t"])]
        for q in ("y", "g"):
            for i in range(self.NSLOT):
                if self.qcnt[q][i] > 0:
                    toks.append((self.qsem[q][i], self.qcnt[q][i]))
        return toks


def build(debug=False):
    import os
    PH = int(os.environ.get("MK_PHASES", "9"))
    CS = int(os.environ.get("MK_CSUB", "9"))
    nc = bacc_mod.Bacc("TRN2")
    dram = lambda nm, shp, dt=F32, kind="ExternalInput": nc.dram_tensor(nm, shp, dt, kind=kind).ap()

    xo = dram("xown", [T, D])
    xf = dram("xfull", [B * T, D])
    wq_h = dram("Wq", [D, D]); bq_h = dram("bq", [D])
    wk_h = dram("Wk", [D, D]); bk_h = dram("bk", [D])
    we_h = dram("We", [H, LEN, LOWR]); wr_h = dram("Wr", [H, LEN, LOWR])
    wc_h = dram("Wc", [2 * LOWR, DK]); bc_h = dram("bc", [DK])
    wo_h = dram("Wo", [D, D]); bo_h = dram("bo", [D])
    rw_h = dram("router_w", [D, E]); rb_h = dram("router_b", [E])
    wg_h = dram("w_gate", [E, D, FF]); wu_h = dram("w_up", [E, D, FF])
    wd_h = dram("w_down", [E, FF, D])
    swg_h = dram("sw_gate", [D, FF]); swu_h = dram("sw_up", [D, FF])
    swd_h = dram("sw_down", [FF, D])
    ident_h = dram("c_ident", [128, 128])
    identb_h = dram("c_identb", [128, 128], BF16)
    triu_h = dram("c_triu", [128, 128])
    iota16_h = dram("c_iota16", [128, E])
    t1_h = dram("c_t1", [128, E])
    iotatok_h = dram("c_iotatok", [128, 1])
    tokv_h = dram("c_tokv", [128, NT])
    mask8_h = dram("c_mask8", [8, 1])
    headsel_h = dram("c_headsel", [128, KC, E], BF16)
    headbc_h = dram("c_headbc", [16, KC, 128], BF16)
    sampidx_h = dram("c_sampidx", [128, NSAMP // 16], I16)
    out_h = dram("out", [T, D], kind="ExternalOutput")
    xnrows = dram("d_xnrows", [T, D], BF16, kind="Internal")
    wgc_d = dram("d_wgc", [E, D, FF], BF16, kind="Internal")
    wuc_d = dram("d_wuc", [E, D, FF], BF16, kind="Internal")
    wdc_d = dram("d_wdc", [E, FF, D], BF16, kind="Internal")
    tabf_d = dram("d_tab", [YROWS], kind="Internal")
    bgidx_d = dram("d_bgidx", [2, T], kind="Internal")
    yexp = dram("d_yexp", [YROWS, D], BF16, kind="Internal")
    ysh_d = dram("d_ysh", [T, D], BF16, kind="Internal")
    stm_d = dram("d_stm", [T, D], BF16, kind="Internal")
    cnt_d = dram("d_cnt", [B * E], kind="Internal")
    uex_d = dram("d_uex", [16], kind="Internal")
    uex2_d = dram("d_uex2", [16], kind="Internal")
    uinfo_d = dram("d_uinfo", [2 * NU], kind="Internal")
    dbg = {}
    if debug:
        dbg["xn"] = dram("dbg_xn", [T, D], BF16, kind="ExternalOutput")
        dbg["logits"] = dram("dbg_logits", [128, NT, E], kind="ExternalOutput")
        dbg["route"] = dram("dbg_route", [128, NT, 8], kind="ExternalOutput")
        dbg["cnt"] = dram("dbg_cnt", [8, E], kind="ExternalOutput")
        dbg["wts"] = dram("dbg_wts", [128, NT, 2], kind="ExternalOutput")
        dbg["flog"] = dram("dbg_flog", [128, B * NT, E], kind="ExternalOutput")
        dbg["s"] = dram("dbg_s", [T, D], BF16, kind="ExternalOutput")
        dbg["ysh"] = dram("dbg_ysh", [T, D], BF16, kind="ExternalOutput")
        dbg["moe"] = dram("dbg_moe", [T, D], kind="ExternalOutput")

    big = ExitStack()
    sbP = lambda nm, shp, dt=F32: big.enter_context(nc.sbuf_tensor(nm, shp, dt))
    ps_ = [big.enter_context(nc.psum_tensor(f"ps{i}", [128, 512], F32)) for i in range(8)]
    k = K(nc, big)

    xnfm = sbP("xnfm", [128, KC, T], BF16)
    possb = sbP("possb", [128, NT, E])
    idxk = sbP("idxk", [128, NT, 2])
    valk = sbP("valk", [128, NT, 2])
    rnkk = sbP("rnkk", [128, NT, 2])
    dstk = sbP("dstk", [128, NT, 2])
    wts = sbP("wts", [128, NT, 2])
    cntT = sbP("cntT", [8, E])
    cnt_row = sbP("cnt_row", [1, 8 * E])
    prior_b = sbP("prior_b", [128, E])
    ident = sbP("ident_s", [128, 128]); identb = sbP("identb_s", [128, 128], BF16)
    triu = sbP("triu_s", [128, 128])
    iota16 = sbP("iota16_s", [128, E]); t1c = sbP("t1_s", [128, E])
    iotatok = sbP("iotatok_s", [128, 1]); mask8 = sbP("mask8_s", [8, 1])
    headsel = sbP("headsel_s", [128, KC, E], BF16)
    headbc = sbP("headbc_s", [16, KC, 128], BF16)
    sampidx = sbP("sampidx_s", [128, NSAMP // 16], I16)
    ones1 = sbP("ones1", [1, 128]); onescol = sbP("onescol", [128, 1])
    epsln = sbP("epsln", [128, 1])
    rw_sb = sbP("rw_sb", [128, KC, E]); rw2 = sbP("rw2", [2, E])
    brow = sbP("brow", [1, E]); cwrow = sbP("cwrow", [1, E])
    wcpad32 = sbP("wcpad32", [128, 128]); wcpad = sbP("wcpad", [128, 128], BF16)
    bqs = sbP("bqs", [128, KC]); bks = sbP("bks", [128, KC]); bos_row = sbP("bos_row", [1, D])
    bcrep = sbP("bcrep", [128, 1]); bbo = sbP("bbo", [128, D], BF16)
    scWe = sbP("scWe", [128, KC, 128], BF16); scWr = sbP("scWr", [128, KC, 128], BF16)
    invq = sbP("invq", [16, 512]); invqb = sbP("invqb", [16, 512], BF16)
    invk_ = sbP("invk", [16, 512]); invkb = sbP("invkb", [16, 512], BF16)
    phAB = ExitStack()
    sbA = lambda nm, shp, dt=F32: phAB.enter_context(nc.sbuf_tensor(nm, shp, dt, side="right"))
    xfb = [sbA(f"xfb{j}", [128, 2, D]) for j in range(2)]
    xs32 = [xfb[0][:, 0, :], xfb[1][:, 0, :]]
    xn32 = sbA("xn32", [128, D]); fmscr = sbA("fmscr", [128, D])
    sqscr = sbA("sqscr", [128, D])
    xnb = [sbP(f"xnb{j}", [128, D], BF16) for j in range(2)]
    musum = sbP("musum", [128, 1]); ssq = sbP("ssq", [128, 1])
    mu = sbP("mu", [128, 1]); var = sbP("var", [128, 1]); var2 = sbP("var2", [128, 1])
    sig = sbP("sig", [128, 1]); inv = sbP("inv", [128, 1]); negmuinv = sbP("negmuinv", [128, 1])
    ltile = sbP("ltile", [128, E])
    m1 = sbP("m1", [128, 1]); m2 = sbP("m2", [128, 1]); negm1 = sbP("negm1", [128, 1])
    etile = sbP("etile", [128, E]); probs = sbP("probs", [128, E])
    oh1 = sbP("oh1", [128, E]); oh2 = sbP("oh2", [128, E]); ohs = sbP("ohs", [128, E])
    selt = sbP("selt", [128, E])
    i1 = sbP("i1", [128, 1]); i2 = sbP("i2", [128, 1])
    p1 = sbP("p1", [128, 1]); p2 = sbP("p2", [128, 1])
    ssum = sbP("ssum", [128, 1]); rcp = sbP("rcp", [128, 1])
    rsel = sbP("rsel", [128, E])
    r1 = sbP("r1", [128, 1]); r2 = sbP("r2", [128, 1])
    d1 = sbP("d1", [128, 1]); d2 = sbP("d2", [128, 1])
    tokval = sbP("tokval", [128, 1])
    a32 = sbP("a32", [128, NT, 2], I32)
    m1a = sbP("m1a", [128, NT, 1]); sa1 = sbP("sa1", [128, NT, 1]); ra1 = sbP("ra1", [128, NT, 1])
    p1a = sbP("p1a", [128, NT, 1]); p2a = sbP("p2a", [128, NT, 1])
    i1a = sbP("i1a", [128, NT, 1]); i2a = sbP("i2a", [128, NT, 1])
    pk16 = sbP("pk16", [128, NT, 1])
    ubs2 = sbP("ubs2", [128, NT, 2]); mlt2 = sbP("mlt2", [128, NT, 2]); rp2 = sbP("rp2", [128, NT, 2])
    j62 = sbP("j62", [128, NT, 2]); tA2 = sbP("tA2", [128, NT, 2]); tB2 = sbP("tB2", [128, NT, 2])
    rem2 = sbP("rem2", [128, NT, 2])
    tokv = sbP("tokv_s", [128, NT])
    offrow = sbP("offrow", [1, E]); tmprow = sbP("tmprow", [1, E])
    musig = sbP("musig", [128, 2]); musigT = sbP("musigT", [2, 128])
    orow = sbP("orow", [1, E]); urow = sbP("urow", [1, E]); trow = sbP("trow", [1, E])
    tot1 = sbP("tot1", [1, 1]); ubrow = sbP("ubrow", [1, E])
    ubcol = sbP("ubcol", [16, 1]); ucol = sbP("ucol", [16, 1]); totcol = sbP("totcol", [16, 1])
    eucol = sbP("eucol", [16, 1]); nacol = sbP("nacol", [16, 1]); m16 = sbP("m16", [16, E])
    uinf2 = sbP("uinf2", [16, 2]); uirow = sbP("uirow", [1, 2 * NU]); ui32 = sbP("ui32", [1, 2 * NU], I32)
    ubb = sbP("ubb", [128, E])
    rtile = sbP("rtile", [128, 1]); mlt = sbP("mlt", [128, 1]); rp_ = sbP("rp_", [128, 1])
    j6 = sbP("j6", [128, 1]); rem_ = sbP("rem_", [128, 1]); ubsel = sbP("ubsel", [128, 1])
    tA = sbP("tA", [128, 1]); tB = sbP("tB", [128, 1])
    dbgst = sbP("dbgst", [128, 8]); dbg_free = [("v", 0)]; ldbg_free = [("v", 0)]
    prow = sbP("prow", [1, E])
    idx_all = sbP("idx_all", [128, (E + NU) * COLS], I16)
    tabf16 = sbP("tabf16", [16, (E + NU) * COLS])
    zz = sbP("zz", [16, YROWS // 16])

    cl = {}
    for nm, dst, src in [("id", ident, ident_h), ("idb", identb, identb_h), ("tr", triu, triu_h),
                         ("io", iota16, iota16_h), ("t1", t1c, t1_h), ("it", iotatok, iotatok_h),
                         ("m8", mask8, mask8_h), ("hs", headsel, headsel_h), ("hb", headbc, headbc_h),
                         ("si", sampidx, sampidx_h), ("tv", tokv, tokv_h)]:
        cl[nm] = k.dma("y", out=dst[:], in_=src[:])
    t_ones = k.v("memset", ones1[:], 1.0)
    t_eps = k.v("memset", epsln[:], 1e-5)
    t_onc = k.v("memset", onescol[:], 1.0)
    t_zz = k.v("memset", zz[:], 0.0)
    rw_ld = [k.dma("y", out=rw_sb[:, kc, :], in_=rw_h[kc * 128:(kc + 1) * 128, :]) for kc in range(KC)]
    t_br = k.dma("y", out=brow[:], in_=rb_h.rearrange("(a b) -> a b", a=1))
    for kc in range(KC):
        k.t("matmul", ps_[7][0:1, 0:E], onescol[:], rw_sb[:, kc, :],
            start=(kc == 0), stop=(kc == KC - 1), wait=[rw_ld[kc], t_onc], inc=(kc == KC - 1))
    t_cw = ("t", k.cnt["t"])
    t_cwev = k.s("copy", cwrow[:], ps_[7][0:1, 0:E], wait=[t_cw])
    t_rw2a = k.v("tensor_scalar", rw2[0:1, :], cwrow[:], -1.0, None, op0=ALU.mult, wait=[t_cwev])
    t_rw2b = k.dma("y", out=rw2[1:2, :], in_=rb_h.rearrange("(a b) -> a b", a=1))
    t_tz = k.dma("y", out=tabf_d.rearrange("(p c) -> p c", p=16), in_=zz[:], wait=[t_zz])
    zbf = sbP("zbf", [128, D], BF16)
    t_zb = k.v("memset", zbf[:], 0.0)
    lt2 = sbP("ltile2", [128, E])
    mu2 = sbP("mu2", [128, 1]); sg2 = sbP("sig2", [128, 1])
    iv2b = [sbP("inv2a", [128, 1]), sbP("inv2b", [128, 1])]
    ms2 = sbP("musum2", [128, 1]); sq2 = sbP("ssq2", [128, 1]); vv2 = sbP("var2b", [128, 1])
    mm2t = sbP("m2t", [128, 1]); o1t = sbP("o1t", [128, E]); ost = sbP("ost", [128, E])
    musig2 = [sbP("musig2a", [128, 2]), sbP("musig2b", [128, 2])]
    musigT2 = [sbP("musigT2a", [2, 128]), sbP("musigT2b", [2, 128])]
    phDw = ExitStack()
    swg = phDw.enter_context(nc.sbuf_tensor("swg", [128, KC, FF], BF16))
    swu = phDw.enter_context(nc.sbuf_tensor("swu", [128, KC, FF], BF16))
    swd = phDw.enter_context(nc.sbuf_tensor("swd", [128, FC, D], BF16))
    t_swg = k.dma("g", out=swg[:], in_=swg_h.rearrange("(c p) f -> p c f", p=128))
    t_swu = k.dma("g", out=swu[:], in_=swu_h.rearrange("(c p) f -> p c f", p=128))
    t_swd = k.dma("g", out=swd[:], in_=swd_h.rearrange("(c p) f -> p c f", p=128))
    yexp_zero_toks = []
    for zi in range(YROWS // 128):
        yexp_zero_toks.append(k.dma("g", out=yexp[zi * 128:(zi + 1) * 128, :], in_=zbf[:], wait=[t_zb]))

    t_bq = [k.dma("y", out=bqs[:, kc:kc + 1], in_=bq_h[kc * 128:(kc + 1) * 128].rearrange("(p a) -> p a", a=1)) for kc in range(KC)]
    t_bk = [k.dma("y", out=bks[:, kc:kc + 1], in_=bk_h[kc * 128:(kc + 1) * 128].rearrange("(p a) -> p a", a=1)) for kc in range(KC)]
    t_bc1 = k.dma("y", out=bcrep[0:64, :], in_=bc_h.rearrange("(p a) -> p a", a=1))
    t_bc2 = k.dma("y", out=bcrep[64:128, :], in_=bc_h.rearrange("(p a) -> p a", a=1))
    t_bo = k.dma("y", out=bos_row[:], in_=bo_h.rearrange("(a b) -> a b", a=1))
    t_wz = k.v("memset", wcpad32[:], 0.0)
    t_wc = []
    t_wc.append(k.dma("y", out=wcpad32[0:20, 0:64], in_=wc_h[0:20, :], wait=[t_wz]))
    t_wc.append(k.dma("y", out=wcpad32[32:52, 0:64], in_=wc_h[20:40, :], wait=[t_wz]))
    t_wc.append(k.dma("y", out=wcpad32[64:84, 64:128], in_=wc_h[0:20, :], wait=[t_wz]))
    t_wc.append(k.dma("y", out=wcpad32[96:116, 64:128], in_=wc_h[20:40, :], wait=[t_wz]))
    t_wcc = k.v("tensor_copy", wcpad[:], wcpad32[:], wait=t_wc)
    bbo_toks = []
    for nt in range(2):
        tt = k.t("matmul", ps_[7][:, 0:512], ones1[0:1, 0:128], bos_row[0:1, nt * 512:(nt + 1) * 512],
                 start=True, stop=True, wait=[t_bo, t_ones, t_cwev] + bbo_toks)
        bbo_toks.append(k.s("copy", bbo[:, nt * 512:(nt + 1) * 512], ps_[7][:, 0:512], wait=[tt]))

    lall = sbA("lall", [128, NT, E])
    da = sbA("da", [128, NT, E])
    ea = sbA("ea", [128, NT, E])
    oh1a = sbA("oh1a", [128, NT, E])
    oh2a = sbA("oh2a", [128, NT, E])
    ohsa = sbA("ohsa", [128, NT, E])
    log_bank = ps_[2]
    off_bank = ps_[6]
    pos_bank = ps_[3]
    tp = [ps_[0], ps_[1]]

    a_done = []
    scat_toks = []
    own_ld = [k.dma("y", out=xfb[0][:, 0, :], in_=xo[0:128, :]), None]
    xnb_free = [[("v", 0)], [("v", 0)]]
    prev_log_evac = ("s", 0)
    prev_pos_evac = ("s", 0)
    prev_off_read = ("s", 0)
    prev_tmp_ev = ("s", 0)
    prev_fm_users = [("t", 0), ("v", 0)]
    prev_bg = [("v", 0), ("v", 0)]
    prev_g2 = ("v", 0)
    prev_tp_evac = [("s", 0), ("s", 0)]

    for i in range(NT):
        buf = xfb[i % 2][:, 0, :]
        ld_tok = own_ld[i % 2]
        if i + 1 < NT:
            w = [a_done[i - 1]] if i >= 1 else []
            own_ld[(i + 1) % 2] = k.dma("y", out=xfb[(i + 1) % 2][:, 0, :], in_=xo[128 * (i + 1):128 * (i + 2), :], wait=w)
        k.s("activation", sqscr[:], buf[:], AF.Copy, accum_out=musum[:], wait=[ld_tok])
        k.v("tensor_scalar", mu[:], musum[:], 1.0 / D, None, op0=ALU.mult, wait=[("s", k.cnt["s"])])
        t_nmu = k.v("tensor_scalar", negmuinv[:], mu[:], -1.0, None, op0=ALU.mult)
        k.s("activation", sqscr[:], buf[:], AF.Square, bias=negmuinv[:], scale=1.0,
            accum_out=ssq[:], wait=[t_nmu])
        k.v("tensor_scalar", var[:], ssq[:], 1.0 / D, None, op0=ALU.mult, wait=[("s", k.cnt["s"])])
        t_sig = k.s("activation", sig[:], var[:], AF.Sqrt, bias=epsln[:], scale=1.0, wait=[("v", k.cnt["v"]), t_eps])
        t_inv = k.v("reciprocal", inv[:], sig[:], wait=[t_sig])
        t_nmi = k.v("tensor_scalar", negmuinv[:], mu[:], inv[:], -1.0, op0=ALU.mult, op1=ALU.mult)
        t_xn = k.s("activation", xn32[:], buf[:], AF.Identity, bias=negmuinv[:], scale=inv[:],
                   wait=[t_nmi, prev_fm_users[0], prev_fm_users[1]])
        t_xnb = k.v("tensor_copy", xnb[i % 2][:], xn32[:], wait=[t_xn] + xnb_free[i % 2])
        xnb_free[i % 2] = [k.dma("y", out=xnrows[128 * i:128 * (i + 1), :], in_=xnb[i % 2][:], wait=[t_xnb])]
        if debug:
            xnb_free[i % 2].append(k.dma("y", out=dbg["xn"][128 * i:128 * (i + 1), :], in_=xnb[i % 2][:], wait=[t_xnb]))
        tt_toks = []
        for h2 in range(2):
            for j in range(4):
                kc = h2 * 4 + j
                tt_toks.append(k.t("transpose", tp[h2][:, j * 128:(j + 1) * 128],
                                   xn32[:, kc * 128:(kc + 1) * 128], ident[:],
                                   wait=[t_xn, prev_tp_evac[h2], cl["id"]]))
        ev0 = k.s("copy", fmscr[:, 0:512], tp[0][:, 0:512], wait=[tt_toks[3], prev_fm_users[0], prev_fm_users[1]])
        ev1 = k.s("copy", fmscr[:, 512:1024], tp[1][:, 0:512], wait=[tt_toks[7]])
        prev_tp_evac = [ev0, ev1]
        t_fmv = k.v("tensor_copy",
                    xnfm[:, :, 128 * i:128 * (i + 1)],
                    fmscr.rearrange("p (c t) -> p c t", c=KC), wait=[ev1])
        for kc in range(KC):
            k.t("matmul", log_bank[:, 16 * i:16 * i + 16], fmscr[:, kc * 128:(kc + 1) * 128],
                rw_sb[:, kc, :], start=(kc == 0), stop=False,
                wait=([ev0] if kc < 4 else [ev1]) + ([rw_ld[kc]] if kc == 0 else []), inc=False)
        t_rb = k.t("matmul", log_bank[:, 16 * i:16 * i + 16], ones1[0:1, 0:128], brow[:],
                   start=False, stop=True, wait=[t_br, t_ones])
        prev_fm_users = [t_rb, t_fmv]
        a_done.append(t_xn)

    t_lev = k.s("copy", lall[:], log_bank[:, 0:NT * E].rearrange("p (t e) -> p t e", e=E),
                wait=[prev_fm_users[0]])
    if debug:
        ldbg_free[0] = k.dma("y", out=dbg["logits"][:], in_=lall[:], wait=[t_lev])
    io3 = iota16[:].rearrange("p (a e) -> p a e", a=1).to_broadcast([128, NT, E])
    t13 = t1c[:].rearrange("p (a e) -> p a e", a=1).to_broadcast([128, NT, E])
    pa = da
    sel = lall
    pa2 = ea
    k.v("tensor_reduce", m1a[:], lall[:], axis=AX.X, op=ALU.max, wait=[t_lev])
    t_d = k.v("tensor_tensor", da[:], lall[:], m1a[:].to_broadcast([128, NT, E]), op=ALU.subtract)
    t_exp = k.s("activation", ea[:], da[:], AF.Exp, wait=[t_d])
    k.v("tensor_reduce", sa1[:], ea[:], axis=AX.X, op=ALU.add, wait=[t_exp])
    k.v("reciprocal", ra1[:], sa1[:])
    k.v("tensor_tensor", pa[:], ea[:], ra1[:].to_broadcast([128, NT, E]), op=ALU.mult)
    k.v("tensor_reduce", p1a[:], pa[:], axis=AX.X, op=ALU.max)
    k.v("tensor_tensor", sel[:], pa[:], p1a[:].to_broadcast([128, NT, E]), op=ALU.is_equal,
        wait=[ldbg_free[0]])
    k.v("tensor_tensor", sel[:], sel[:], t13, op=ALU.mult, wait=[cl["t1"]])
    k.v("tensor_scalar", sel[:], sel[:], BIG, None, op0=ALU.add)
    k.v("tensor_reduce", i1a[:], sel[:], axis=AX.X, op=ALU.min)
    k.v("tensor_tensor", oh1a[:], io3, i1a[:].to_broadcast([128, NT, E]), op=ALU.is_equal,
        wait=[cl["io"]])
    k.v("tensor_scalar", sel[:], oh1a[:], 2.0, None, op0=ALU.mult)
    k.v("tensor_tensor", pa2[:], pa[:], sel[:], op=ALU.subtract)
    k.v("tensor_reduce", p2a[:], pa2[:], axis=AX.X, op=ALU.max)
    k.v("tensor_tensor", sel[:], pa2[:], p2a[:].to_broadcast([128, NT, E]), op=ALU.is_equal)
    k.v("tensor_tensor", sel[:], sel[:], t13, op=ALU.mult)
    k.v("tensor_scalar", sel[:], sel[:], BIG, None, op0=ALU.add)
    k.v("tensor_reduce", i2a[:], sel[:], axis=AX.X, op=ALU.min)
    k.v("tensor_tensor", oh2a[:], io3, i2a[:].to_broadcast([128, NT, E]), op=ALU.is_equal)
    t_ohs = k.v("tensor_tensor", ohsa[:], oh1a[:], oh2a[:], op=ALU.add)
    k.v("tensor_tensor", sa1[:], p1a[:], p2a[:], op=ALU.add)
    k.v("tensor_scalar", sa1[:], sa1[:], 1e-9, None, op0=ALU.add)
    k.v("reciprocal", ra1[:], sa1[:])
    k.v("tensor_tensor", valk[:, :, 0:1], p1a[:], ra1[:], op=ALU.mult)
    k.v("tensor_tensor", valk[:, :, 1:2], p2a[:], ra1[:], op=ALU.mult)
    k.v("tensor_copy", idxk[:, :, 0:1], i1a[:])
    t_idxa = k.v("tensor_copy", idxk[:, :, 1:2], i2a[:])

    for i in range(NT):
        if i == 0:
            t_or = k.v("memset", offrow[:], 0.0)
        else:
            t_or = prev_off_read
        k.t("matmul", pos_bank[:, 0:16], triu[:], ohsa[:, i, :], start=True, stop=False,
            wait=[t_ohs, t_or, prev_pos_evac, cl["tr"]], inc=False)
        t_pm = k.t("matmul", pos_bank[:, 0:16], ones1[0:1, 0:128], offrow[:], start=False, stop=True)
        t_ou = k.t("matmul", off_bank[0:1, 0:16], onescol[:], ohsa[:, i, :],
                   start=True, stop=True, wait=[t_or, prev_tmp_ev])
        prev_tmp_ev = k.s("copy", tmprow[:], off_bank[0:1, 0:16], wait=[t_ou])
        prev_off_read = k.v("tensor_tensor", offrow[:], offrow[:], tmprow[:], op=ALU.add,
                            wait=[prev_tmp_ev, t_pm])
        t_pev = k.s("copy", possb[:, i, :], pos_bank[:, 0:16], wait=[t_pm])
        prev_pos_evac = t_pev

    k.v("tensor_tensor", da[:], possb[:], oh1a[:], op=ALU.mult, wait=[prev_pos_evac])
    k.v("tensor_reduce", rnkk[:, :, 0:1], da[:], axis=AX.X, op=ALU.add)
    k.v("tensor_tensor", da[:], possb[:], oh2a[:], op=ALU.mult)
    t_r2 = k.v("tensor_reduce", rnkk[:, :, 1:2], da[:], axis=AX.X, op=ALU.add)
    if debug:
        for i in range(NT):
            for col, srcb in [(0, i1a), (1, i2a), (2, p1a), (3, p2a)]:
                k.v("tensor_copy", dbgst[:, col:col + 1], srcb[:, i, :],
                    wait=[dbg_free[0]] if col == 0 else [])
            for col, kkk in [(4, 0), (5, 1), (6, 0), (7, 1)]:
                k.v("tensor_copy", dbgst[:, col:col + 1], rnkk[:, i, kkk:kkk + 1], wait=[t_r2])
            dbg_free[0] = k.dma("y", out=dbg["route"][:, i, :], in_=dbgst[:], wait=[("v", k.cnt["v"])])

    a_marks = k.marks()

    t_o = k.v("tensor_scalar", orow[:], offrow[:], -float(CB), 0.0, op0=ALU.add, op1=ALU.max,
              wait=[prev_off_read])
    t_u = k.v("memset", urow[:], 0.0)
    for j in range(5):
        k.v("tensor_scalar", trow[:], orow[:], float(US * j), None, op0=ALU.is_gt)
        t_u = k.v("tensor_tensor", urow[:], urow[:], trow[:], op=ALU.add)
    t_tot = k.v("reduce_sum", tot1[:], urow[:], axis=AX.X)
    y_u1 = k.dma("y", out=uex_d.rearrange("(a b) -> a b", a=1), in_=urow[:], wait=[t_u])
    y_u2 = k.dma("y", out=ucol[:], in_=uex_d.rearrange("(p a) -> p a", a=1), wait=[y_u1])
    t_ubm = k.t("matmul", pos_bank[0:16, 0:1], triu[0:16, 0:16], ucol[:], start=True, stop=True,
                wait=[y_u2, prev_pos_evac])
    t_ube = k.s("copy", ubcol[:], pos_bank[0:16, 0:1], wait=[t_ubm])
    y_u3 = k.dma("y", out=uex2_d.rearrange("(a b) -> a b", a=1),
                 in_=ubcol.rearrange("p a -> a (p a)"), wait=[t_ube]) if False else None
    y_u3 = k.dma("y", out=uex2_d.rearrange("(p a) -> p a", a=1), in_=ubcol[:], wait=[t_ube])
    y_u4 = k.dma("y", out=ubrow[:], in_=uex2_d.rearrange("(a b) -> a b", a=1), wait=[y_u3])
    t_ubbm = k.t("matmul", pos_bank[:, 1:17], ones1[0:1, 0:128], ubrow[:], start=True, stop=True,
                 wait=[y_u4, t_ube])
    t_ubbe = k.s("copy", ubb[:], pos_bank[:, 1:17], wait=[t_ubbm])
    k.v("tensor_scalar", m16[:], ubb[0:16, :] if False else ubb[0:16, :], None, None, op0=ALU.bypass) if False else None
    k.v("tensor_scalar", m16[:], ubb[0:16, :], iotatok[0:16, 0:1], None, op0=ALU.is_le, wait=[t_ubbe])
    k.v("reduce_sum", eucol[:], m16[:], axis=AX.X)
    t_eu = k.v("tensor_scalar", eucol[:], eucol[:], -1.0, 0.0, op0=ALU.add, op1=ALU.max)
    t_tc2 = k.t("matmul", pos_bank[0:16, 17:18], ones1[0:1, 0:16], tot1[:], start=True, stop=True,
                wait=[t_tot, t_ubbe])
    t_tce = k.s("copy", totcol[:], pos_bank[0:16, 17:18], wait=[t_tc2])
    k.v("tensor_scalar", nacol[:], iotatok[0:16, 0:1], totcol[:], None, op0=ALU.is_ge, wait=[t_tce])
    k.v("tensor_copy", uinf2[:, 0:1], eucol[:])
    t_ui = k.v("tensor_copy", uinf2[:, 1:2], nacol[:])
    y_u5 = k.dma("y", out=uinfo_d.rearrange("(p a) -> p a", a=2), in_=uinf2[0:NU, :], wait=[t_ui])
    y_u6 = k.dma("y", out=uirow[:], in_=uinfo_d.rearrange("(a b) -> a b", a=1), wait=[y_u5])
    t_ui32 = k.v("tensor_copy", ui32[:], uirow[:], wait=[y_u6])
    ubb3 = ubb[:].rearrange("p (a e) -> p a e", a=1).to_broadcast([128, NT, E])
    k.v("tensor_tensor", da[:], oh1a[:], ubb3, op=ALU.mult, wait=[t_ubbe])
    k.v("tensor_reduce", ubs2[:, :, 0:1], da[:], axis=AX.X, op=ALU.add)
    k.v("tensor_tensor", da[:], oh2a[:], ubb3, op=ALU.mult)
    k.v("tensor_reduce", ubs2[:, :, 1:2], da[:], axis=AX.X, op=ALU.add)
    k.v("tensor_scalar", mlt2[:], rnkk[:], float(CB), None, op0=ALU.is_lt, wait=[t_r2])
    k.v("tensor_scalar", rp2[:], rnkk[:], -float(CB), 0.0, op0=ALU.add, op1=ALU.max)
    k.v("memset", j62[:], 0.0)
    for mm in range(1, 5):
        k.v("tensor_scalar", tA2[:], rp2[:], float(US * mm), None, op0=ALU.is_ge)
        k.v("tensor_tensor", j62[:], j62[:], tA2[:], op=ALU.add)
    k.v("tensor_scalar", tA2[:], j62[:], float(US), None, op0=ALU.mult)
    k.v("tensor_tensor", rem2[:], rp2[:], tA2[:], op=ALU.subtract)
    k.v("tensor_tensor", tB2[:], ubs2[:], j62[:], op=ALU.add)
    k.v("tensor_scalar", tB2[:], tB2[:], float(US), float(E * CB), op0=ALU.mult, op1=ALU.add)
    k.v("tensor_tensor", tB2[:], tB2[:], rem2[:], op=ALU.add)
    k.v("tensor_scalar", tA2[:], idxk[:], float(CB), None, op0=ALU.mult, wait=[t_idxa])
    k.v("tensor_tensor", tA2[:], tA2[:], rnkk[:], op=ALU.add)
    k.v("tensor_tensor", tA2[:], tA2[:], tB2[:], op=ALU.subtract)
    k.v("tensor_tensor", tA2[:], tA2[:], mlt2[:], op=ALU.mult)
    k.v("tensor_tensor", dstk[:], tB2[:], tA2[:], op=ALU.add)
    t_dst = k.v("tensor_scalar", dstk[:], dstk[:], float(YROWS - 1), None, op0=ALU.min)
    t_a32 = k.v("tensor_copy", a32[:], dstk[:])
    prev_g2 = ("v", 0)
    a2_ps3 = t_ubbe
    for i in range(NT):
        for kk in range(2):
            prev_g2 = k.gop("indirect_dma_start", tabf_d.rearrange("(a b) -> a b", b=1),
                            bass.IndirectOffsetOnAxis(a32[:, i, kk:kk + 1], 0), tokv[:, i:i + 1], None,
                            wait=([t_a32, cl["tv"], t_tz] if (i == 0 and kk == 0) else [prev_g2]))
            scat_toks.append(prev_g2)
            k.dma("y", out=bgidx_d[kk, 128 * i:128 * (i + 1)].rearrange("(p a) -> p a", a=1),
                  in_=dstk[:, i, kk:kk + 1], wait=[t_dst] if (i == 0 and kk == 0) else [])
    a_bank_guard = {0: prev_tp_evac[0], 1: prev_tp_evac[1], 3: t_tce, 7: bbo_toks[-1]}
    a_fm_tok = prev_fm_users[1]
    a_marks = k.marks()

    def _finish():
        for i in range(K.NSLOT):
            if k.qcnt["y"][i] > 0:
                nc.sync.wait_ge(k.qsem["y"][i], k.qcnt["y"][i])
            if k.qcnt["g"][i] > 0:
                nc.sync.wait_ge(k.qsem["g"][i], k.qcnt["g"][i])
        nc.compile()
        return nc
    if PH <= 1:
        return _finish()

    bf_state = {"step": 0, "ld": [None, None],
                "prev_ev": ("s", 0), "prev_tp": ("s", 0), "prev_users": [("t", 0), ("t", 0)],
                "prev_lev": ("s", 0), "prev_cnt": ("t", 0), "guard": list(a_marks[:3]),
                "pending": None, "started": False,
                "msev": [("s", 0), ("s", 0)], "iv": [("v", 0), ("v", 0)]}
    fm2 = fmscr
    sq2scr = sqscr
    bf_tp = ps_[4]
    bf_log = ps_[5]
    NPAIR = NT // 2

    def bf_load(g):
        st = bf_state
        b2, p2 = divmod(g, NPAIR)
        st["ld"][g % 2] = k.dma("y", out=xfb[g % 2][:],
                                in_=xf[b2 * T + 256 * p2: b2 * T + 256 * (p2 + 1), :],
                                wait=[st["prev_users"][g % 2]] + st["guard"])
        st["guard"] = []

    def bf_stats(s2):
        st = bf_state
        par = s2 % 2
        j2 = s2 % NT
        buf = xfb[(j2 // 2) % 2][:, j2 % 2, :]
        ld_tok = st["ld"][(j2 // 2) % 2]
        k.s("activation", sq2scr[:], buf, AF.Copy, accum_out=ms2[:], wait=[ld_tok])
        t_s2 = k.s("activation", sq2scr[:], buf, AF.Square, accum_out=sq2[:])
        k.v("tensor_scalar", mu2[:], ms2[:], 1.0 / D, None, op0=ALU.mult, wait=[t_s2])
        k.v("tensor_tensor", mm2t[:], mu2[:], mu2[:], op=ALU.mult)
        k.v("tensor_scalar", vv2[:], sq2[:], 1.0 / D, None, op0=ALU.mult)
        t_vv = k.v("tensor_tensor", vv2[:], vv2[:], mm2t[:], op=ALU.subtract)
        t_sg = k.s("activation", sg2[:], vv2[:], AF.Sqrt, bias=epsln[:], scale=1.0, wait=[t_vv, t_eps])
        st["iv"][par] = k.v("reciprocal", iv2b[par][:], sg2[:], wait=[t_sg, st["prev_lev"]])
        k.v("tensor_copy", musig2[par][:, 0:1], mu2[:])
        t_msg = k.v("tensor_copy", musig2[par][:, 1:2], sg2[:])
        t_mst = k.t("transpose", bf_log[0:2, 128 + 128 * par:256 + 128 * par], musig2[par][:],
                    ident[:], wait=[t_msg, st["prev_lev"]])
        st["msev"][par] = k.s("copy", musigT2[par][:], bf_log[0:2, 128 + 128 * par:256 + 128 * par],
                              wait=[t_mst])

    def bf_pump(n):
        st = bf_state
        for _ in range(n):
            step = st["step"]
            if step >= B * NT:
                return
            b, j = divmod(step, NT)
            pair, jj = divmod(j, 2)
            bufi = pair % 2
            par = step % 2
            g = step // 2
            if not st["started"]:
                bf_load(g)
                bf_stats(0)
                st["started"] = True
            if st["pending"] is not None:
                pb, pj, p_os = st["pending"]
                st["prev_cnt"] = k.t("matmul", off_bank[0:1, 16 + 16 * pb:32 + 16 * pb], onescol[:],
                                     ost[:], start=(pj == 0), stop=(pj == NT - 1),
                                     wait=[p_os] + (a_marks[:3] if (pb == 0 and pj == 0) else []))
                st["pending"] = None
            buf = xfb[bufi][:, jj, :]
            ld_tok = st["ld"][bufi]
            if jj == 0 and g + 1 < B * NPAIR:
                bf_load(g + 1)
            tts = []
            for h2 in range(2):
                for jb in range(4):
                    kc = h2 * 4 + jb
                    tts.append(k.t("transpose", bf_tp[:, jb * 128:(jb + 1) * 128],
                                   buf[:, kc * 128:(kc + 1) * 128],
                                   ident[:], wait=[ld_tok, st["prev_tp"], st["prev_ev"]]))
                ev = k.s("copy", fm2[:, h2 * 512:(h2 + 1) * 512], bf_tp[:, 0:512], wait=[tts[-1]])
                st["prev_tp"] = ev
            st["prev_ev"] = st["prev_tp"]
            for kc in range(KC):
                k.t("matmul", bf_log[:, 0:16], fm2[:, kc * 128:(kc + 1) * 128], rw_sb[:, kc, :],
                    start=(kc == 0), stop=False,
                    wait=[st["prev_ev"], st["prev_lev"]] if kc == 0 else [], inc=False)
            t_rmm = k.t("matmul", bf_log[:, 0:16], musigT2[par][:], rw2[:], start=False, stop=True,
                        wait=[st["msev"][par], t_rw2a, t_rw2b])
            st["prev_users"][bufi] = t_rmm
            t_lev = k.s("activation", lt2[:], bf_log[:, 0:16], AF.Identity, scale=iv2b[par][:],
                        wait=[t_rmm, st["iv"][par], st.get("flog_free", ("v", 0))])
            st["prev_lev"] = t_lev
            if debug:
                st["flog_free"] = k.dma("y", out=dbg["flog"][:, b * NT + j, :], in_=lt2[:], wait=[t_lev])
            k.v("reduce_max", m1[:], lt2[:], axis=AX.X, wait=[t_lev, st["prev_cnt"]])
            k.v("tensor_scalar", o1t[:], lt2[:], m1[:], None, op0=ALU.is_equal)
            k.v("tensor_scalar", o1t[:], o1t[:], BIG, None, op0=ALU.mult)
            k.v("tensor_tensor", o1t[:], lt2[:], o1t[:], op=ALU.subtract)
            k.v("reduce_max", m2[:], o1t[:], axis=AX.X)
            t_os = k.v("tensor_scalar", ost[:], lt2[:], m2[:], None, op0=ALU.is_ge)
            st["pending"] = (b, j, t_os)
            if step + 1 < B * NT:
                bf_stats(step + 1)
            st["step"] += 1

    cast_state = {"step": 0}
    cast_toks = [[None, None, None] for _ in range(E)]

    def cast_pump(n):
        for _ in range(n):
            s3 = cast_state["step"]
            if s3 >= 3 * E:
                return
            e3, m3 = divmod(s3, 3)
            src3, dst3 = [(wg_h, wgc_d), (wu_h, wuc_d), (wd_h, wdc_d)][m3]
            cast_toks[e3][m3] = k.dma("g", out=dst3[e3], in_=src3[e3], wait=[bf_state["prev_lev"]])
            cast_state["step"] += 1

    def bf_flush():
        st = bf_state
        if st["pending"] is not None:
            pb, pj, p_os = st["pending"]
            st["prev_cnt"] = k.t("matmul", off_bank[0:1, 16 + 16 * pb:32 + 16 * pb], onescol[:],
                                 ost[:], start=(pj == 0), stop=(pj == NT - 1), wait=[p_os])
            st["pending"] = None

    with ExitStack() as phD:
        gs8 = phD.enter_context(nc.sbuf_tensor("gs8", [128, FC, 512], BF16))
        hs8 = phD.enter_context(nc.sbuf_tensor("hs8", [128, FC, 512], BF16))
        yst = [phD.enter_context(nc.sbuf_tensor(f"yst{j}", [128, D], BF16)) for j in range(2)]
        pr = [("s", 0), ("s", 0)]
        prd = ("s", 0)
        ys_free = [("y", 0), ("y", 0)]
        nbank = [0, 1, 3, 7]
        pr4 = [("s", 0)] * 4
        prd2 = [("s", 0), ("s", 0)]
        for g8 in range(4):
            c0 = g8 * 512
            for mt in range(FC):
                bank = ps_[nbank[mt % 4]]
                for kc in range(KC):
                    k.t("matmul", bank[:, 0:512], swg[:, kc, mt * 128:(mt + 1) * 128],
                        xnfm[:, kc, c0:c0 + 512], start=(kc == 0), stop=(kc == KC - 1),
                        wait=[t_swg, pr4[mt % 4]] + (prd2 if mt < 4 else [])
                             + ([a_bank_guard[nbank[mt % 4]], a_fm_tok] if (g8 == 0 and mt < 4) else []),
                        inc=(kc == KC - 1))
                tmm = ("t", k.cnt["t"])
                pr4[mt % 4] = k.s("activation", gs8[:, mt, :], bank[:, 0:512], AF.Silu, wait=[tmm])
            for mt in range(FC):
                bank = ps_[nbank[mt % 4]]
                for kc in range(KC):
                    k.t("matmul", bank[:, 0:512], swu[:, kc, mt * 128:(mt + 1) * 128],
                        xnfm[:, kc, c0:c0 + 512], start=(kc == 0), stop=(kc == KC - 1),
                        wait=[t_swu, pr4[mt % 4]], inc=(kc == KC - 1))
                tmm = ("t", k.cnt["t"])
                pr4[mt % 4] = k.v("tensor_tensor", hs8[:, mt, :], bank[:, 0:512], gs8[:, mt, :],
                                 op=ALU.mult, wait=[tmm, pr4[mt % 4]])
            for it in range(4):
                ybuf = yst[it % 2]
                bank = ps_[2 if it % 2 == 0 else 0]
                for nt2 in range(2):
                    for kc in range(FC):
                        k.t("matmul", bank[:, 0:512], hs8[:, kc, it * 128:(it + 1) * 128],
                            swd[:, kc, nt2 * 512:(nt2 + 1) * 512], start=(kc == 0), stop=(kc == FC - 1),
                            wait=([t_swd, prd2[it % 2], pr4[0], pr4[1], pr4[2], pr4[3]]
                                  + (a_marks if g8 == 0 else [])) if kc == 0 else [],
                            inc=(kc == FC - 1))
                    tmm = ("t", k.cnt["t"])
                    prd2[it % 2] = k.s("copy", ybuf[:, nt2 * 512:(nt2 + 1) * 512], bank[:, 0:512],
                              wait=[tmm, ys_free[it % 2]])
                ys_free[it % 2] = k.dma("y", out=ysh_d[c0 + it * 128:c0 + (it + 1) * 128, :],
                                        in_=ybuf[:], wait=[prd2[it % 2]])
            bf_pump(16)
            cast_pump(6)
        d_marks = k.marks()

    if PH <= 2:
        return _finish()
    phDw.close()
    t_tls = []
    for e in range(E + NU):
        t_tls.append(k.dma("y", out=tabf16[:, e * COLS:(e + 1) * COLS],
                           in_=tabf_d[e * CB:(e + 1) * CB].rearrange("(s p) -> p s", p=16),
                           wait=scat_toks if e == 0 else [], allow_slow_non_contiguous=True))
    t_tc = k.v("tensor_copy", idx_all[0:16, :], tabf16[:], wait=t_tls)
    rep_toks = []
    for r in range(1, 8):
        rep_toks.append(k.dma("y", out=idx_all[16 * r:16 * (r + 1), :], in_=idx_all[0:16, :], wait=[t_tc]))

    with ExitStack() as phC:
        wqsb = phC.enter_context(nc.sbuf_tensor("wqsb", [128, KC, D], BF16))
        wksb = phC.enter_context(nc.sbuf_tensor("wksb", [128, KC, D], BF16))
        wosb = phC.enter_context(nc.sbuf_tensor("wosb", [128, KC, D], BF16))
        qsb = phC.enter_context(nc.sbuf_tensor("qsb", [128, KC, 512], BF16))
        ksb = phC.enter_context(nc.sbuf_tensor("ksb", [128, KC, 512], BF16))
        scsb = phC.enter_context(nc.sbuf_tensor("scsb", [128, KC, 512], BF16))
        qq = phC.enter_context(nc.sbuf_tensor("qq", [128, 512], BF16))
        xsg = phC.enter_context(nc.sbuf_tensor("xsg", [128, KC, NSAMP], BF16))
        xst = phC.enter_context(nc.sbuf_tensor("xst", [128, 2, D], BF16))
        sst = [phC.enter_context(nc.sbuf_tensor(f"sst{j}", [128, D], BF16)) for j in range(2)]
        t_wq = k.dma("g", out=wqsb[:], in_=wq_h.rearrange("(c p) f -> p c f", p=128), wait=d_marks[:1])
        t_wk = k.dma("g", out=wksb[:], in_=wk_h.rearrange("(c p) f -> p c f", p=128))
        t_wo = k.dma("g", out=wosb[:], in_=wo_h.rearrange("(c p) f -> p c f", p=128))
        t_wes, t_wrs = [], []
        wesb0 = phC.enter_context(nc.sbuf_tensor("wesb0", [128, H, LOWR], BF16))
        wesb1 = phC.enter_context(nc.sbuf_tensor("wesb1", [72, H, LOWR], BF16))
        wrsb0 = phC.enter_context(nc.sbuf_tensor("wrsb0", [128, H, LOWR], BF16))
        wrsb1 = phC.enter_context(nc.sbuf_tensor("wrsb1", [72, H, LOWR], BF16))
        for h in range(H if CS >= 3 else 0):
            t_wes.append(k.dma("g", out=wesb0[:, h, :], in_=we_h[h, 0:128, :]))
            t_wes.append(k.dma("g", out=wesb1[:, h, :], in_=we_h[h, 128:200, :]))
            t_wrs.append(k.dma("g", out=wrsb0[:, h, :], in_=wr_h[h, 0:128, :]))
            t_wrs.append(k.dma("g", out=wrsb1[:, h, :], in_=wr_h[h, 128:200, :]))
        if CS < 2:
            t_z1 = k.v("memset", scWe[:], 0.0)
            t_z2 = k.v("memset", scWr[:], 0.0)
            c_prep = k.marks()
        t_xsg = None if CS < 2 else k.gop("dma_gather", xsg[:], xnrows[:], sampidx[:], NSAMP, NSAMP, D,
                      transpose=True, wait=xnb_free[0] + xnb_free[1] + [cl["si"]])
        t_xstv = []
        if CS < 2:
            pass
        else:
            ps3b = ps_[3][:].bitcast(BF16)
            prev_cp = ("v", 0)
            for h2 in range(2):
                for g4 in range(2):
                    tt = None
                    for j4 in range(4):
                        kc = g4 * 4 + j4
                        tt = k.t("transpose", ps3b[:, j4 * 128:(j4 + 1) * 128],
                                 xsg[:, kc, h2 * 128:(h2 + 1) * 128], identb[:],
                                 wait=[t_xsg, cl["idb"], prev_cp])
                    prev_cp = k.v("tensor_copy", xst[:, h2, g4 * 512:(g4 + 1) * 512],
                                  ps3b[:, 0:512], wait=[tt])
                    t_xstv.append(prev_cp)
            t_z1 = k.v("memset", scWe[:], 0.0)
            t_z2 = k.v("memset", scWr[:], 0.0)
            prev_we_ev = ("s", 0)
            for h in range(H if CS >= 3 else 0):
                kc, par = h // 2, h % 2
                for (w0, w1, dstt, colbase, wtoks) in [(wesb0, wesb1, scWe, 0, t_wes), (wrsb0, wrsb1, scWr, 32, t_wrs)]:
                    k.t("matmul", ps_[7][0:64, 0:LOWR], xst[:, 0, h * 64:(h + 1) * 64], w0[:, h, :],
                        start=True, stop=False,
                        wait=t_xstv[-1:] + [prev_we_ev, wtoks[2 * h]], inc=False)
                    tmm = k.t("matmul", ps_[7][0:64, 0:LOWR], xst[0:72, 1, h * 64:(h + 1) * 64], w1[:, h, :],
                              start=False, stop=True, wait=[wtoks[2 * h + 1]])
                    prev_we_ev = k.s("copy",
                                     dstt[par * 64:(par + 1) * 64, kc, par * 64 + colbase:par * 64 + colbase + LOWR],
                                     ps_[7][0:64, 0:LOWR], wait=[tmm, t_z1, t_z2])
        c_prep = k.marks()
        pr = [("s", 0), ("s", 0)]
        prs = ("v", 0)
        prq = ("v", 0)
        prqL = [("v", 0), ("v", 0)]
        prm = ("s", 0)
        ss_free = [("y", 0), ("y", 0)]
        sq_ev = ("s", 0)
        binv_ev = ("v", 0)
        nbank = [0, 1]
        for g8 in range(4):
            c0 = g8 * 512
            for (wsb, t_w, bias_t, qdst, invdst, invbdst) in [
                    (wqsb, t_wq, bqs, qsb, invq, invqb), (wksb, t_wk, bks, ksb, invk_, invkb)]:
                for mt in range(KC):
                    bank = ps_[nbank[mt % 2]]
                    for kc in range(KC):
                        k.t("matmul", bank[:, 0:512], wsb[:, kc, mt * 128:(mt + 1) * 128],
                            xnfm[:, kc, c0:c0 + 512], start=(kc == 0), stop=(kc == KC - 1),
                            wait=[t_w, pr[mt % 2]] + (prqL if mt < 2 else [])
                                 + (d_marks if (g8 == 0 and mt < 2) else []),
                            inc=(kc == KC - 1))
                    tmm = ("t", k.cnt["t"])
                    t_ev = k.s("activation", qdst[:, mt, :], bank[:, 0:512], AF.Identity,
                               bias=bias_t[:, mt:mt + 1], scale=1.0, wait=[tmm])
                    t_sq = k.s("activation", qq[:], qdst[:, mt, :], AF.Square, wait=[t_ev])
                    pr[mt % 2] = t_sq
                    k.t("matmul", ps_[3][0:16, 0:512], headsel[:, mt, :], qq[:],
                        start=(mt == 0), stop=(mt == KC - 1),
                        wait=[t_sq, cl["hs"]] + (([sq_ev, binv_ev, prm] + (c_prep if g8 == 0 else [])) if mt == 0 else []),
                        inc=(mt == KC - 1))
                t_ssq = ("t", k.cnt["t"])
                t_sqr = k.s("activation", invdst[0:16, :],
                            ps_[3][0:16, 0:512], AF.Sqrt, wait=[t_ssq])
                sq_ev = t_sqr
                k.v("tensor_scalar", invdst[0:16, :], invdst[0:16, :],
                    1e-12, None, op0=ALU.max, wait=[t_sqr])
                k.v("reciprocal", invdst[0:16, :], invdst[0:16, :])
                t_ivb = k.v("tensor_copy", invbdst[0:16, :], invdst[0:16, :])
                for kc in range(KC):
                    tbm = k.t("matmul", ps_[7][:, 0:512], headbc[:, kc, :], invbdst[0:16, :],
                              start=True, stop=True, wait=[t_ivb, cl["hb"], binv_ev])
                    binv_ev = k.v("tensor_tensor", qdst[:, kc, :], qdst[:, kc, :], ps_[7][:, 0:512],
                                  op=ALU.mult, wait=[tbm, pr[0], pr[1]])
            for kc in range(KC if CS >= 4 else 0):
                k.t("matmul", ps_[2][:, 0:512], scWe[:, kc, :], qsb[:, kc, :], start=True, stop=False,
                    wait=[binv_ev, prs, prm] + (c_prep if (g8 == 0 and kc == 0) else []), inc=False)
                tmm = k.t("matmul", ps_[2][:, 0:512], scWr[:, kc, :], ksb[:, kc, :], start=False, stop=True)
                prs = k.v("tensor_copy", scsb[:, kc, :], ps_[2][:, 0:512], wait=[tmm])
                tmm2 = k.t("matmul", ps_[3][:, 0:512], wcpad[:], scsb[:, kc, :], start=True, stop=True,
                           wait=[prs, t_wcc, prm] + ([sq_ev] if kc == 0 else []))
                prm = k.s("activation", scsb[:, kc, :], ps_[3][:, 0:512], AF.Identity,
                          bias=bcrep[:], scale=1.0, wait=[tmm2, t_bc1, t_bc2])
            for it in range(4 if CS >= 5 else 0):
                sbuf_ = sst[it % 2]
                bank = ps_[0 if it % 2 == 0 else 1]
                for nt2 in range(2):
                    for kc in range(KC):
                        k.t("matmul", bank[:, 0:512], scsb[:, kc, it * 128:(it + 1) * 128],
                            wosb[:, kc, nt2 * 512:(nt2 + 1) * 512], start=(kc == 0), stop=(kc == KC - 1),
                            wait=[t_wo, prm, prq, binv_ev, pr[0], pr[1]] if kc == 0 else [], inc=(kc == KC - 1))
                    tmm = ("t", k.cnt["t"])
                    prq = k.v("tensor_tensor", sbuf_[:, nt2 * 512:(nt2 + 1) * 512], bank[:, 0:512],
                              bbo[:, nt2 * 512:(nt2 + 1) * 512], op=ALU.add,
                              wait=[tmm, bbo_toks[nt2], ss_free[it % 2]])
                    prqL[it % 2] = prq
                ss_free[it % 2] = k.dma("y", out=stm_d[c0 + it * 128:c0 + (it + 1) * 128, :],
                                        in_=sbuf_[:], wait=[prq])
            bf_pump(16)
            cast_pump(6)
        c_marks = k.marks()

    if PH <= 3:
        return _finish()
    bf_pump(B * NT)
    cast_pump(3 * E)
    bf_flush()
    phAB.close()

    cnt_evs = []
    for b in range(B):
        cnt_evs.append(k.s("copy", cnt_row[0:1, E * b:E * (b + 1)], off_bank[0:1, 16 + 16 * b:32 + 16 * b],
                           wait=[bf_state["prev_cnt"]]))
    cnt_evs = [k.dma("y", out=cnt_d.rearrange("(a b) -> a b", a=1), in_=cnt_row[:], wait=cnt_evs)]
    cnt_evs = [k.dma("y", out=cntT[:], in_=cnt_d.rearrange("(p e) -> p e", e=E), wait=cnt_evs)]
    if debug:
        k.dma("y", out=dbg["cnt"][:], in_=cntT[:], wait=cnt_evs)
    t_pm = k.t("matmul", ps_[5][0:1, 0:16], mask8[:], cntT[:], start=True, stop=True,
               wait=cnt_evs + [cl["m8"]])
    t_pev = k.s("copy", prow[:], ps_[5][0:1, 0:16], wait=[t_pm])
    t_pb = k.t("matmul", ps_[5][:, 16:32], ones1[0:1, 0:128], prow[:], start=True, stop=True, wait=[t_pev])
    t_pbe = k.s("copy", prior_b[:], ps_[5][:, 16:32], wait=[t_pb])

    io3c = iota16[:].rearrange("p (a e) -> p a e", a=1).to_broadcast([128, NT, E])
    pb3 = prior_b[:].rearrange("p (a e) -> p a e", a=1).to_broadcast([128, NT, E])
    for kkk in range(2):
        k.v("tensor_tensor", possb[:], io3c, idxk[:, :, kkk:kkk + 1].to_broadcast([128, NT, E]),
            op=ALU.is_equal, wait=[t_pbe, cl["io"]] if kkk == 0 else [])
        k.v("tensor_tensor", possb[:], possb[:], pb3, op=ALU.mult)
        k.v("tensor_reduce", pk16[:], possb[:], axis=AX.X, op=ALU.add)
        k.v("tensor_tensor", pk16[:], pk16[:], rnkk[:, :, kkk:kkk + 1], op=ALU.add)
        k.v("tensor_scalar", pk16[:], pk16[:], float(CAP), None, op0=ALU.is_lt)
        t_w = k.v("tensor_tensor", wts[:, :, kkk:kkk + 1], valk[:, :, kkk:kkk + 1], pk16[:], op=ALU.mult)
    wts_toks = [t_w] * NT
    if debug:
        k.dma("y", out=dbg["wts"][:], in_=wts[:], wait=[t_w])

    if PH <= 4:
        return _finish()
    with ExitStack() as phE:
        wgb = [phE.enter_context(nc.sbuf_tensor(f"wgb{j}", [128, KC, FF], BF16)) for j in range(2)]
        wub = phE.enter_context(nc.sbuf_tensor("wub", [128, KC, FF], BF16))
        wdb = phE.enter_context(nc.sbuf_tensor("wdb", [128, FC, D], BF16))
        hgb = [phE.enter_context(nc.sbuf_tensor(f"hgb{j}", [128, KC, CB], BF16)) for j in range(2)]
        gatb = phE.enter_context(nc.sbuf_tensor("gatb", [128, FC, CB], BF16))
        hbb = gatb
        yeb = xnb
        guard = c_marks
        st_ = {"gate_done": ("t", 0), "up_done": ("t", 0), "dn_done": ("t", 0),
               "pr": [("s", 0), ("s", 0)],
               "ye_free": [yexp_zero_toks[-2], yexp_zero_toks[-1]],
               "dma_y": lambda **kw: k.dma("y", **kw)}

        def moe_worker(wgbuf, wg_toks, hgbuf, t_hg_tok, wu_toks, wd_toks, yrow0, extra0):
            pr = st_["pr"]
            for mt in range(FC):
                bank = ps_[mt % 2]
                for kc in range(KC):
                    k.t("matmul", bank[:, 0:CB], wgbuf[:, kc, mt * 128:(mt + 1) * 128],
                        hgbuf[:, kc, :], start=(kc == 0), stop=(kc == KC - 1),
                        wait=[wg_toks[kc], t_hg_tok, pr[mt % 2]] + (extra0 if mt < 2 else []),
                        inc=(kc == KC - 1))
                tmm = ("t", k.cnt["t"])
                pr[mt % 2] = k.s("activation", gatb[:, mt, :], bank[:, 0:CB], AF.Silu, wait=[tmm])
            st_["gate_done"] = ("t", k.cnt["t"])
            for mt in range(FC):
                bank = ps_[mt % 2]
                for kc in range(KC):
                    k.t("matmul", bank[:, 0:CB], wub[:, kc, mt * 128:(mt + 1) * 128],
                        hgbuf[:, kc, :], start=(kc == 0), stop=(kc == KC - 1),
                        wait=[wu_toks[kc], pr[mt % 2]], inc=(kc == KC - 1))
                tmm = ("t", k.cnt["t"])
                pr[mt % 2] = k.v("tensor_tensor", hbb[:, mt, :], bank[:, 0:CB], gatb[:, mt, :],
                                 op=ALU.mult, wait=[tmm, pr[mt % 2]])
            st_["up_done"] = ("t", k.cnt["t"])
            prd = ("s", 0)
            for i3 in range(CB // 128):
                ybuf = yeb[i3 % 2]
                for nt2 in range(2):
                    for kc in range(FC):
                        k.t("matmul", ps_[2][:, 0:512], hbb[:, kc, i3 * 128:(i3 + 1) * 128],
                            wdb[:, kc, nt2 * 512:(nt2 + 1) * 512], start=(kc == 0), stop=(kc == FC - 1),
                            wait=([wd_toks[kc], prd, pr[0], pr[1]] + extra0) if kc == 0 else [wd_toks[kc]],
                            inc=(kc == FC - 1))
                    tmm = ("t", k.cnt["t"])
                    prd = k.s("copy", ybuf[:, nt2 * 512:(nt2 + 1) * 512], ps_[2][:, 0:512],
                              wait=[tmm, st_["ye_free"][i3 % 2]])
                st_["ye_free"][i3 % 2] = st_["dma_y"](out=yexp[yrow0 + i3 * 128:yrow0 + (i3 + 1) * 128, :],
                                                      in_=ybuf[:], wait=[prd])
            st_["dn_done"] = ("t", k.cnt["t"])

        t_wg = [None, None]
        t_hg = [None, None]
        t_wg[0] = k.dma("g", out=wgb[0][:], in_=wgc_d[0].rearrange("(c p) f -> p c f", p=128), wait=guard + [cast_toks[0][0]])
        t_hg[0] = k.gop("dma_gather", hgb[0][:], xnrows[:], idx_all[:, 0:COLS], CB, CB, D,
                        transpose=True, wait=[t_tc] + rep_toks + xnb_free[0] + xnb_free[1])
        for e in range(E):
            cur = e % 2
            if e + 1 < E:
                t_wg[1 - cur] = k.dma("g", out=wgb[1 - cur][:],
                                      in_=wgc_d[e + 1].rearrange("(c p) f -> p c f", p=128),
                                      wait=[st_["gate_done"], cast_toks[e + 1][0]])
                t_hg[1 - cur] = k.gop("dma_gather", hgb[1 - cur][:], xnrows[:],
                                      idx_all[:, (e + 1) * COLS:(e + 2) * COLS], CB, CB, D,
                                      transpose=True,
                                      wait=[st_["gate_done"], st_["up_done"], st_["dn_done"]] + rep_toks)
            t_wu = k.dma("g", out=wub[:], in_=wuc_d[e].rearrange("(c p) f -> p c f", p=128),
                         wait=[st_["up_done"], cast_toks[e][1]] + (guard if e == 0 else []))
            t_wd = k.dma("g", out=wdb[:], in_=wdc_d[e].rearrange("(c p) f -> p c f", p=128),
                         wait=[st_["dn_done"], cast_toks[e][2]] + (guard if e == 0 else []))
            moe_worker(wgb[cur], [t_wg[cur]] * KC, hgb[cur], t_hg[cur], [t_wu] * KC, [t_wd] * FC,
                       e * CB, guard if e == 0 else [])
        ET = mybir.EngineType
        sw_engs = [nc.tensor, nc.vector, nc.scalar, nc.sync]
        usems_pool = [[phE.enter_context(nc.semaphore(f"u{g}_{j}")) for j in range(4)] for g in range(2)]
        ucnt_pool = [[[0] for _ in range(4)] for g in range(2)]
        prev_hgu = [t_hg[0], t_hg[1]]
        for u in range(NU):
            t_hgu = k.gop("dma_gather", hgb[u % 2][:], xnrows[:],
                          idx_all[:, (E + u) * COLS:(E + u + 1) * COLS], CB, CB, D,
                          transpose=True,
                          wait=[st_["gate_done"], st_["up_done"], st_["dn_done"], prev_hgu[u % 2]] + rep_toks)
            prev_hgu[u % 2] = t_hgu
            usem_u = usems_pool[u % 2]
            ucnt = ucnt_pool[u % 2]
            tok_cnt = dict(k.cnt)
            ustart_tok = [c[0] for c in ucnt]
            nc.tensor.wait_ge(k.sem["t"], k.cnt["t"])
            nc.tensor.sem_inc(k.sem["t"], 1); k.cnt["t"] += 1
            nc.vector.wait_ge(k.sem["v"], k.cnt["v"])
            nc.vector.sem_inc(k.sem["v"], 1); k.cnt["v"] += 1
            nc.scalar.wait_ge(k.sem["s"], k.cnt["s"])
            nc.scalar.sem_inc(k.sem["s"], 1); k.cnt["s"] += 1
            for j in range(4):
                if ucnt[j][0] > 0:
                    nc.sync.wait_ge(usem_u[j], ucnt[j][0])
                nc.sync.sem_inc(usem_u[j], 16)
                ucnt[j][0] += 16
            sv_na = nc.values_load(ui32[0:1, 2 * u + 1:2 * u + 2],
                                   engines=[ET.PE, ET.DVE, ET.Activation, ET.SP],
                                   min_val=0, max_val=1)
            sv_eu = nc.values_load(ui32[0:1, 2 * u:2 * u + 1], engines=[ET.SP],
                                   min_val=0, max_val=E - 1)
            snap_cnt = dict(k.cnt)
            snap_lw = dict(k._lw)
            snap_wl = {kk2: set(vv2) for kk2, vv2 in k.wait_log.items()}
            ustart = [c[0] for c in ucnt]
            yesel = [0]
            old_dma_y = st_["dma_y"]
            for body in nc.Switch(engines=sw_engs, index=[sv_na] * 4, n=2):
                if body == 0:
                    st_["pr"] = [("s", tok_cnt["s"]), ("s", tok_cnt["s"])]
                    st_["gate_done"] = ("t", tok_cnt["t"])
                    st_["up_done"] = ("t", tok_cnt["t"])
                    st_["dn_done"] = ("t", tok_cnt["t"])
                    u_extra = [("v", tok_cnt["v"]), ("s", tok_cnt["s"])]

                    def u_dma_y(**kw):
                        j = 2 + yesel[0]
                        yesel[0] = 1 - yesel[0]
                        kw.setdefault("wait", [])
                        kw["wait"] = list(kw["wait"]) + [(usem_u[j], ustart_tok[j])]
                        return k.dma_sem("y", usem_u[j], ucnt[j], "dma_start", **kw)
                    st_["dma_y"] = u_dma_y

                    def chunk_loads(w_d, nchunk, fdim, dstbuf, first_waits):
                        toks = []
                        for kc in range(nchunk):
                            sj = kc % 2
                            ld = k.dma_sem("y", usem_u[sj], ucnt[sj], "dma_start",
                                           out=dstbuf[:, kc, 0:fdim],
                                           in_=w_d[sv_eu][kc * 128:(kc + 1) * 128, 0:fdim],
                                           wait=[(usem_u[sj], ucnt[sj][0])] + first_waits)
                            toks.append(ld)
                        return toks

                    wg_toks = chunk_loads(wgc_d, KC, FF, wgb[0], [st_["gate_done"]])
                    wu_toks = chunk_loads(wuc_d, KC, FF, wub, [st_["up_done"]])
                    wd_toks = chunk_loads(wdc_d, FC, D, wdb, [st_["dn_done"]])
                    moe_worker(wgb[0], wg_toks, hgb[u % 2], t_hgu, wu_toks, wd_toks,
                               E * CB + u * US, u_extra)
                    st_["dma_y"] = old_dma_y
                else:
                    def comp(eng, sem, lo, hi):
                        if hi <= lo:
                            return
                        pts = sorted(vv2 for vv2 in k.wait_log.get(id(sem), ())
                                     if lo < vv2 <= hi and vv2 not in snap_wl.get(id(sem), ()))
                        cur = lo
                        for p in pts:
                            if p > cur:
                                eng.sem_inc(sem, p - cur)
                                cur = p
                        if hi > cur:
                            eng.sem_inc(sem, hi - cur)
                    comp(nc.tensor, k.sem["t"], snap_cnt["t"], k.cnt["t"])
                    comp(nc.vector, k.sem["v"], snap_cnt["v"], k.cnt["v"])
                    comp(nc.scalar, k.sem["s"], snap_cnt["s"], k.cnt["s"])
                    for j in range(4):
                        if ucnt[j][0] > ustart[j] and ustart[j] > 0:
                            nc.sync.wait_ge(usem_u[j], ustart[j])
                        comp(nc.sync, usem_u[j], ustart[j], ucnt[j][0])
            k._lw = snap_lw
            nc.tensor.wait_ge(k.sem["t"], k.cnt["t"])
            nc.vector.wait_ge(k.sem["v"], k.cnt["v"])
            nc.scalar.wait_ge(k.sem["s"], k.cnt["s"])
            nc.tensor.sem_inc(k.sem["t"], 1); k.cnt["t"] += 1
            nc.vector.sem_inc(k.sem["v"], 1); k.cnt["v"] += 1
            nc.scalar.sem_inc(k.sem["s"], 1); k.cnt["s"] += 1
            for j in range(4):
                nc.sync.wait_ge(usem_u[j], ucnt[j][0])
                nc.sync.sem_inc(usem_u[j], 16)
                ucnt[j][0] += 16
            st_["pr"] = [("s", k.cnt["s"]), ("s", k.cnt["s"])]
            st_["gate_done"] = ("t", k.cnt["t"])
            st_["up_done"] = ("t", k.cnt["t"])
            st_["dn_done"] = ("t", k.cnt["t"])
        e_marks = k.marks()
        st_e_ye = st_["ye_free"]

    if PH <= 5:
        return _finish()
    with ExitStack() as phF:
        bga = phF.enter_context(nc.sbuf_tensor("bga", [128, NT, D], BF16))
        bgb = phF.enter_context(nc.sbuf_tensor("bgb", [128, NT, D], BF16))
        bgf = phF.enter_context(nc.sbuf_tensor("bgf", [16, T // 16], F32))
        bgi = [phF.enter_context(nc.sbuf_tensor(f"bgi{j}", [128, T // 16], I16)) for j in range(2)]
        scr1b = [phF.enter_context(nc.sbuf_tensor(f"scr1{j}", [128, D], F32)) for j in range(2)]
        scr2b = [phF.enter_context(nc.sbuf_tensor(f"scr2{j}", [128, D], F32)) for j in range(2)]
        xrt = [phF.enter_context(nc.sbuf_tensor(f"xrt{j}", [128, D], F32)) for j in range(2)]
        xnt = [phF.enter_context(nc.sbuf_tensor(f"xnt{j}", [128, D], BF16)) for j in range(2)]
        ysht = [phF.enter_context(nc.sbuf_tensor(f"ysht{j}", [128, D], BF16)) for j in range(2)]
        sstt = [phF.enter_context(nc.sbuf_tensor(f"sstt{j}", [128, D], BF16)) for j in range(2)]
        outb = [phF.enter_context(nc.sbuf_tensor(f"outb{j}", [128, D], F32)) for j in range(2)]
        FS = int(os.environ.get("MK_FSUB", "9"))
        gt = []
        mdbg_free = [("v", 0)]
        for kk in range(2 if FS >= 1 else 0):
            t_bls = []
            for qq_ in range(4):
                t_bls.append(k.dma("y", out=bgf[:, qq_ * 32:(qq_ + 1) * 32],
                             in_=bgidx_d[kk, qq_ * 512:(qq_ + 1) * 512].rearrange("(s p) -> p s", p=16),
                             wait=e_marks if qq_ == 0 else [], allow_slow_non_contiguous=True))
            t_bc_ = k.v("tensor_copy", bgi[kk][0:16, :], bgf[:], wait=t_bls)
            reps = [k.dma("y", out=bgi[kk][16 * r:16 * (r + 1), :], in_=bgi[kk][0:16, :], wait=[t_bc_])
                    for r in range(1, 8)]
            dstb = bga if kk == 0 else bgb
            for hh in range(2):
                gt.append(k.gop("dma_gather", dstb[:, hh * 8:(hh + 1) * 8, :], yexp[:],
                                bgi[kk][:, hh * 64:(hh + 1) * 64],
                                T // 2, T // 2, D, transpose=False,
                                wait=[t_bc_] + reps + [st_e_ye[0], st_e_ye[1]]))
        ld_toks = [[None] * 4, [None] * 4]
        st_free = [("v", 0), ("v", 0)]
        war_scr = [("v", 0), ("v", 0)]
        cons = [{}, {}]
        for i in range(NT if FS >= 2 else 0):
            cur = i % 2
            if i < 2:
                ld_toks[cur] = [
                    k.dma("y", out=xrt[cur][:], in_=xo[128 * i:128 * (i + 1), :], wait=e_marks),
                    k.dma("y", out=xnt[cur][:], in_=xnrows[128 * i:128 * (i + 1), :], wait=e_marks),
                    k.dma("y", out=ysht[cur][:], in_=ysh_d[128 * i:128 * (i + 1), :], wait=e_marks),
                    k.dma("y", out=sstt[cur][:], in_=stm_d[128 * i:128 * (i + 1), :], wait=e_marks),
                ]
            scr1 = scr1b[cur]; scr2 = scr2b[cur]
            t_s2 = k.s("activation", scr1[:], bga[:, i, :], AF.Identity, scale=wts[:, i, 0:1],
                       wait=gt + [wts_toks[i], mdbg_free[0], war_scr[cur]])
            t_s2 = k.s("activation", scr2[:], bgb[:, i, :], AF.Identity, scale=wts[:, i, 1:2])
            t_moe = k.v("tensor_tensor", scr1[:], scr1[:], scr2[:], op=ALU.add, wait=[t_s2])
            if debug and "moe" in dbg:
                mdbg_free[0] = k.dma("y", out=dbg["moe"][128 * i:128 * (i + 1), :], in_=scr1[:], wait=[t_moe])
            t_xy = k.v("tensor_tensor", scr2[:], xnt[cur][:], ysht[cur][:], op=ALU.add, wait=ld_toks[cur][1:3])
            k.v("tensor_tensor", scr1[:], scr1[:], scr2[:], op=ALU.add, wait=[mdbg_free[0]])
            t_g = k.s("activation", scr2[:], scr1[:], AF.Silu, wait=[("v", k.cnt["v"])])
            t_sg = k.v("tensor_tensor", scr1[:], sstt[cur][:], scr2[:], op=ALU.mult,
                       wait=[t_g, ld_toks[cur][3]])
            t_out = k.v("tensor_tensor", outb[cur][:], xrt[cur][:], scr1[:], op=ALU.add,
                        wait=[ld_toks[cur][0], st_free[cur]])
            war_scr[cur] = t_out
            st_free[cur] = k.dma("y", out=out_h[128 * i:128 * (i + 1), :], in_=outb[cur][:], wait=[t_out])
            if i + 2 < NT:
                ld_toks[cur] = [
                    k.dma("y", out=xrt[cur][:], in_=xo[128 * (i + 2):128 * (i + 3), :], wait=[t_out]),
                    k.dma("y", out=xnt[cur][:], in_=xnrows[128 * (i + 2):128 * (i + 3), :], wait=[t_xy]),
                    k.dma("y", out=ysht[cur][:], in_=ysh_d[128 * (i + 2):128 * (i + 3), :], wait=[t_xy]),
                    k.dma("y", out=sstt[cur][:], in_=stm_d[128 * (i + 2):128 * (i + 3), :], wait=[t_sg]),
                ]

    if debug:
        k.dma("y", out=dbg["s"][:], in_=stm_d[:])
        k.dma("y", out=dbg["ysh"][:], in_=ysh_d[:])
    return _finish()


def make_consts():
    c = {}
    c["c_ident"] = np.eye(128, dtype=np.float32)
    c["c_identb"] = np.eye(128, dtype=ml_dtypes.bfloat16)
    c["c_triu"] = np.triu(np.ones((128, 128), dtype=np.float32), k=1)
    c["c_iota16"] = np.tile(np.arange(E, dtype=np.float32), (128, 1))
    c["c_t1"] = c["c_iota16"] - BIG
    c["c_iotatok"] = np.arange(128, dtype=np.float32).reshape(128, 1)
    c["c_tokv"] = (np.arange(NT, dtype=np.float32)[None, :] * 128
                   + np.arange(128, dtype=np.float32)[:, None]).astype(np.float32)
    hs = np.zeros((128, KC, E), dtype=ml_dtypes.bfloat16)
    for dl in range(128):
        for kc in range(KC):
            hs[dl, kc, 2 * kc + (dl >= 64)] = 1
    c["c_headsel"] = hs
    hb = np.zeros((16, KC, 128), dtype=ml_dtypes.bfloat16)
    for h in range(16):
        for kc in range(KC):
            for col in range(128):
                if h == 2 * kc + (col >= 64):
                    hb[h, kc, col] = 1
    c["c_headbc"] = hb
    samp = np.linspace(0.0, T - 1, LEN).astype(np.int32)
    sampp = np.zeros(NSAMP, dtype=np.int16)
    sampp[:LEN] = samp
    si = np.zeros((16, NSAMP // 16), dtype=np.int16)
    for j in range(NSAMP):
        si[j % 16, j // 16] = sampp[j]
    c["c_sampidx"] = np.tile(si, (8, 1))
    return c


_NC_CACHE = {}


def kernel(L0_attn_x, L0_attn_Wq, L0_attn_bq, L0_attn_Wk, L0_attn_bk, L0_attn_We,
           L0_attn_Wr, L0_attn_Wc, L0_attn_bc, L0_attn_Wo, L0_attn_bo,
           L0_ffn_router_w, L0_ffn_router_b, L0_ffn_w_gate, L0_ffn_w_up,
           L0_ffn_w_down, L0_ffn_sw_gate, L0_ffn_sw_up, L0_ffn_sw_down,
           debug=False):
    if ("nc", debug) not in _NC_CACHE:
        _NC_CACHE[("nc", debug)] = build(debug=debug)
    nc = _NC_CACHE[("nc", debug)]
    consts = make_consts()
    x = np.ascontiguousarray(np.asarray(L0_attn_x, dtype=np.float32))
    xflat = x.reshape(B * T, D)
    shared = {
        "xfull": xflat,
        "Wq": np.asarray(L0_attn_Wq, np.float32), "bq": np.asarray(L0_attn_bq, np.float32),
        "Wk": np.asarray(L0_attn_Wk, np.float32), "bk": np.asarray(L0_attn_bk, np.float32),
        "We": np.asarray(L0_attn_We, np.float32), "Wr": np.asarray(L0_attn_Wr, np.float32),
        "Wc": np.asarray(L0_attn_Wc, np.float32), "bc": np.asarray(L0_attn_bc, np.float32),
        "Wo": np.asarray(L0_attn_Wo, np.float32), "bo": np.asarray(L0_attn_bo, np.float32),
        "router_w": np.asarray(L0_ffn_router_w, np.float32),
        "router_b": np.asarray(L0_ffn_router_b, np.float32),
        "w_gate": np.asarray(L0_ffn_w_gate, np.float32),
        "w_up": np.asarray(L0_ffn_w_up, np.float32),
        "w_down": np.asarray(L0_ffn_w_down, np.float32),
        "sw_gate": np.asarray(L0_ffn_sw_gate, np.float32),
        "sw_up": np.asarray(L0_ffn_sw_up, np.float32),
        "sw_down": np.asarray(L0_ffn_sw_down, np.float32),
    }
    shared.update(consts)
    in_maps = []
    for c in range(B):
        m = dict(shared)
        m["xown"] = x[c]
        m["c_mask8"] = (np.arange(8) < c).astype(np.float32).reshape(8, 1)
        in_maps.append(m)
    res = run_bass_kernel_spmd(nc, in_maps, core_ids=list(range(B)))
    out = np.stack([res.results[c]["out"] for c in range(B)], axis=0)
    if debug:
        return out, res
    return out


# version 10
# speedup vs baseline: 1.0381x; 1.0208x over previous
import sys
if "/opt/trn_rl_repo" not in sys.path:
    sys.path.insert(0, "/opt/trn_rl_repo")
import numpy as np
import ml_dtypes
import concourse.bacc as bacc_mod
import concourse.bass as bass
import concourse.mybir as mybir
from concourse.bass_utils import run_bass_kernel_spmd
from contextlib import ExitStack

F32 = mybir.dt.float32
BF16 = mybir.dt.bfloat16
I16 = mybir.dt.int16
I32 = mybir.dt.int32
AF = mybir.ActivationFunctionType
ALU = mybir.AluOpType
AX = mybir.AxisListType

B, T, D = 8, 2048, 1024
H, DK, LOWR, LEN = 16, 64, 20, 200
E, FF, CAP = 16, 1408, 2560
CB = 384
NU = 10
US = 384
YROWS = E * CB + NU * US
NT = T // 128
KC = D // 128
FC = FF // 128
NSAMP = 256
BIG = 1024.0
COLS = CB // 16


class K:
    NSLOT = 8

    def __init__(self, nc, stack):
        self.nc = nc
        self.sem = {}
        self.cnt = {}
        for e in ("v", "s", "t"):
            self.sem[e] = stack.enter_context(nc.semaphore("ch_" + e))
            self.cnt[e] = 0
        self.engs = {"v": nc.vector, "s": nc.scalar, "t": nc.tensor,
                     "g": nc.gpsimd, "y": nc.sync}
        self.qsem, self.qcnt, self.qidx = {}, {}, {}
        for q in ("y", "g"):
            self.qsem[q] = [stack.enter_context(nc.semaphore(f"q_{q}{i}")) for i in range(self.NSLOT)]
            self.qcnt[q] = [0] * self.NSLOT
            self.qidx[q] = 0
        self._lw = {}
        self.wait_log = {}

    def _resolve(self, tok):
        se, vv = tok
        return (self.sem[se], vv) if isinstance(se, str) else (se, vv)

    def _wait(self, ekey, eng, tok):
        if tok[1] <= 0:
            return
        s, vv = self._resolve(tok)
        key = (ekey, id(s))
        if self._lw.get(key, -1) >= vv:
            return
        eng.wait_ge(s, vv)
        self._lw[key] = vv
        self.wait_log.setdefault(id(s), set()).add(vv)

    def op(self, e, method, *args, wait=(), inc=True, **kw):
        eng = self.engs[e]
        w = list(wait)
        if e in ("v", "s"):
            w.append((e, self.cnt[e]))
        for tok in w:
            self._wait(e, eng, tok)
        inst = getattr(eng, method)(*args, **kw)
        if inc:
            self.cnt[e] += 1
            inst.then_inc(self.sem[e], 1)
        return (e, self.cnt[e])

    def _q(self, q, method, args, kw, wait):
        eng = self.engs[q]
        i = self.qidx[q] % self.NSLOT
        self.qidx[q] += 1
        s = self.qsem[q][i]
        if self.qcnt[q][i] > 0:
            self._wait(q, eng, (s, self.qcnt[q][i]))
        for tok in wait:
            self._wait(q, eng, tok)
        inst = getattr(eng, method)(*args, **kw)
        self.qcnt[q][i] += 16
        inst.then_inc(s, 16)
        return (s, self.qcnt[q][i])

    def dma(self, q, *args, wait=(), **kw):
        return self._q(q, "dma_start", args, kw, wait)

    def dma_sem(self, q, sem, cnt_ref, method, *args, wait=(), **kw):
        eng = self.engs[q]
        for tok in wait:
            self._wait(q, eng, tok)
        inst = getattr(eng, method)(*args, **kw)
        cnt_ref[0] += 16
        inst.then_inc(sem, 16)
        return (sem, cnt_ref[0])

    def gop(self, method, *args, wait=(), **kw):
        return self._q("g", method, args, kw, wait)

    def v(self, m, *a, **kw): return self.op("v", m, *a, **kw)
    def s(self, m, *a, **kw): return self.op("s", m, *a, **kw)
    def t(self, m, *a, **kw): return self.op("t", m, *a, **kw)

    def marks(self):
        toks = [("v", self.cnt["v"]), ("s", self.cnt["s"]), ("t", self.cnt["t"])]
        for q in ("y", "g"):
            for i in range(self.NSLOT):
                if self.qcnt[q][i] > 0:
                    toks.append((self.qsem[q][i], self.qcnt[q][i]))
        return toks


def build(debug=False):
    import os
    PH = int(os.environ.get("MK_PHASES", "9"))
    CS = int(os.environ.get("MK_CSUB", "9"))
    nc = bacc_mod.Bacc("TRN2")
    dram = lambda nm, shp, dt=F32, kind="ExternalInput": nc.dram_tensor(nm, shp, dt, kind=kind).ap()

    xo = dram("xown", [T, D])
    xf = dram("xfull", [B * T, D])
    wq_h = dram("Wq", [D, D]); bq_h = dram("bq", [D])
    wk_h = dram("Wk", [D, D]); bk_h = dram("bk", [D])
    we_h = dram("We", [H, LEN, LOWR]); wr_h = dram("Wr", [H, LEN, LOWR])
    wc_h = dram("Wc", [2 * LOWR, DK]); bc_h = dram("bc", [DK])
    wo_h = dram("Wo", [D, D]); bo_h = dram("bo", [D])
    rw_h = dram("router_w", [D, E]); rb_h = dram("router_b", [E])
    wg_h = dram("w_gate", [E, D, FF]); wu_h = dram("w_up", [E, D, FF])
    wd_h = dram("w_down", [E, FF, D])
    swg_h = dram("sw_gate", [D, FF]); swu_h = dram("sw_up", [D, FF])
    swd_h = dram("sw_down", [FF, D])
    ident_h = dram("c_ident", [128, 128])
    identb_h = dram("c_identb", [128, 128], BF16)
    triu_h = dram("c_triu", [128, 128])
    iota16_h = dram("c_iota16", [128, E])
    t1_h = dram("c_t1", [128, E])
    iotatok_h = dram("c_iotatok", [128, 1])
    tokv_h = dram("c_tokv", [128, NT])
    mask8_h = dram("c_mask8", [8, 1])
    headsel_h = dram("c_headsel", [128, KC, E], BF16)
    headbc_h = dram("c_headbc", [16, KC, 128], BF16)
    sampidx_h = dram("c_sampidx", [128, NSAMP // 16], I16)
    out_h = dram("out", [T, D], kind="ExternalOutput")
    xnrows = dram("d_xnrows", [T, D], BF16, kind="Internal")
    wgc_d = dram("d_wgc", [E, D, FF], BF16, kind="Internal")
    wqc_d = dram("d_wqc", [D, D], BF16, kind="Internal")
    wkc_d = dram("d_wkc", [D, D], BF16, kind="Internal")
    woc_d = dram("d_woc", [D, D], BF16, kind="Internal")
    wuc_d = dram("d_wuc", [E, D, FF], BF16, kind="Internal")
    wdc_d = dram("d_wdc", [E, FF, D], BF16, kind="Internal")
    tabf_d = dram("d_tab", [YROWS], kind="Internal")
    bgidx_d = dram("d_bgidx", [2, T], kind="Internal")
    yexp = dram("d_yexp", [YROWS, D], BF16, kind="Internal")
    ysh_d = dram("d_ysh", [T, D], BF16, kind="Internal")
    stm_d = dram("d_stm", [T, D], BF16, kind="Internal")
    cnt_d = dram("d_cnt", [B * E], kind="Internal")
    uex_d = dram("d_uex", [16], kind="Internal")
    uex2_d = dram("d_uex2", [16], kind="Internal")
    uinfo_d = dram("d_uinfo", [2 * NU], kind="Internal")
    dbg = {}
    if debug:
        dbg["xn"] = dram("dbg_xn", [T, D], BF16, kind="ExternalOutput")
        dbg["logits"] = dram("dbg_logits", [128, NT, E], kind="ExternalOutput")
        dbg["route"] = dram("dbg_route", [128, NT, 8], kind="ExternalOutput")
        dbg["cnt"] = dram("dbg_cnt", [8, E], kind="ExternalOutput")
        dbg["wts"] = dram("dbg_wts", [128, NT, 2], kind="ExternalOutput")
        dbg["flog"] = dram("dbg_flog", [128, B * NT, E], kind="ExternalOutput")
        dbg["s"] = dram("dbg_s", [T, D], BF16, kind="ExternalOutput")
        dbg["ysh"] = dram("dbg_ysh", [T, D], BF16, kind="ExternalOutput")
        dbg["moe"] = dram("dbg_moe", [T, D], kind="ExternalOutput")

    big = ExitStack()
    sbP = lambda nm, shp, dt=F32: big.enter_context(nc.sbuf_tensor(nm, shp, dt))
    ps_ = [big.enter_context(nc.psum_tensor(f"ps{i}", [128, 512], F32)) for i in range(8)]
    k = K(nc, big)

    xnfm = sbP("xnfm", [128, KC, T], BF16)
    possb = sbP("possb", [128, NT, E])
    idxk = sbP("idxk", [128, NT, 2])
    valk = sbP("valk", [128, NT, 2])
    rnkk = sbP("rnkk", [128, NT, 2])
    dstk = sbP("dstk", [128, NT, 2])
    wts = sbP("wts", [128, NT, 2])
    cntT = sbP("cntT", [8, E])
    cnt_row = sbP("cnt_row", [1, 8 * E])
    prior_b = sbP("prior_b", [128, E])
    ident = sbP("ident_s", [128, 128]); identb = sbP("identb_s", [128, 128], BF16)
    triu = sbP("triu_s", [128, 128])
    iota16 = sbP("iota16_s", [128, E]); t1c = sbP("t1_s", [128, E])
    iotatok = sbP("iotatok_s", [128, 1]); mask8 = sbP("mask8_s", [8, 1])
    headsel = sbP("headsel_s", [128, KC, E], BF16)
    headbc = sbP("headbc_s", [16, KC, 128], BF16)
    sampidx = sbP("sampidx_s", [128, NSAMP // 16], I16)
    ones1 = sbP("ones1", [1, 128]); onescol = sbP("onescol", [128, 1])
    epsln = sbP("epsln", [128, 1])
    rw_sb = sbP("rw_sb", [128, KC, E]); rw2 = sbP("rw2", [2, E])
    brow = sbP("brow", [1, E]); cwrow = sbP("cwrow", [1, E])
    wcpad32 = sbP("wcpad32", [128, 128]); wcpad = sbP("wcpad", [128, 128], BF16)
    bqs = sbP("bqs", [128, KC]); bks = sbP("bks", [128, KC]); bos_row = sbP("bos_row", [1, D])
    bcrep = sbP("bcrep", [128, 1]); bbo = sbP("bbo", [128, D], BF16)
    scWe = sbP("scWe", [128, KC, 128], BF16); scWr = sbP("scWr", [128, KC, 128], BF16)
    invq = sbP("invq", [16, 512]); invqb = sbP("invqb", [16, 512], BF16)
    invk_ = sbP("invk", [16, 512]); invkb = sbP("invkb", [16, 512], BF16)
    phAB = ExitStack()
    sbA = lambda nm, shp, dt=F32: phAB.enter_context(nc.sbuf_tensor(nm, shp, dt, side="right"))
    xfb = [sbA(f"xfb{j}", [128, 2, D]) for j in range(2)]
    xs32 = [xfb[0][:, 0, :], xfb[1][:, 0, :]]
    xn32 = sbA("xn32", [128, D]); fmscr = sbA("fmscr", [128, D])
    sqscr = sbA("sqscr", [128, D])
    xnb = [sbP(f"xnb{j}", [128, D], BF16) for j in range(2)]
    musum = sbP("musum", [128, 1]); ssq = sbP("ssq", [128, 1])
    mu = sbP("mu", [128, 1]); var = sbP("var", [128, 1]); var2 = sbP("var2", [128, 1])
    sig = sbP("sig", [128, 1]); inv = sbP("inv", [128, 1]); negmuinv = sbP("negmuinv", [128, 1])
    ltile = sbP("ltile", [128, E])
    m1 = sbP("m1", [128, 1]); m2 = sbP("m2", [128, 1]); negm1 = sbP("negm1", [128, 1])
    etile = sbP("etile", [128, E]); probs = sbP("probs", [128, E])
    oh1 = sbP("oh1", [128, E]); oh2 = sbP("oh2", [128, E]); ohs = sbP("ohs", [128, E])
    selt = sbP("selt", [128, E])
    i1 = sbP("i1", [128, 1]); i2 = sbP("i2", [128, 1])
    p1 = sbP("p1", [128, 1]); p2 = sbP("p2", [128, 1])
    ssum = sbP("ssum", [128, 1]); rcp = sbP("rcp", [128, 1])
    rsel = sbP("rsel", [128, E])
    r1 = sbP("r1", [128, 1]); r2 = sbP("r2", [128, 1])
    d1 = sbP("d1", [128, 1]); d2 = sbP("d2", [128, 1])
    tokval = sbP("tokval", [128, 1])
    a32 = sbP("a32", [128, NT, 2], I32)
    m1a = sbP("m1a", [128, NT, 1]); sa1 = sbP("sa1", [128, NT, 1]); ra1 = sbP("ra1", [128, NT, 1])
    p1a = sbP("p1a", [128, NT, 1]); p2a = sbP("p2a", [128, NT, 1])
    i1a = sbP("i1a", [128, NT, 1]); i2a = sbP("i2a", [128, NT, 1])
    pk16 = sbP("pk16", [128, NT, 1])
    ubs2 = sbP("ubs2", [128, NT, 2]); mlt2 = sbP("mlt2", [128, NT, 2]); rp2 = sbP("rp2", [128, NT, 2])
    j62 = sbP("j62", [128, NT, 2]); tA2 = sbP("tA2", [128, NT, 2]); tB2 = sbP("tB2", [128, NT, 2])
    rem2 = sbP("rem2", [128, NT, 2])
    tokv = sbP("tokv_s", [128, NT])
    offrow = sbP("offrow", [1, E]); tmprow = sbP("tmprow", [1, E])
    musig = sbP("musig", [128, 2]); musigT = sbP("musigT", [2, 128])
    orow = sbP("orow", [1, E]); urow = sbP("urow", [1, E]); trow = sbP("trow", [1, E])
    tot1 = sbP("tot1", [1, 1]); ubrow = sbP("ubrow", [1, E])
    ubcol = sbP("ubcol", [16, 1]); ucol = sbP("ucol", [16, 1]); totcol = sbP("totcol", [16, 1])
    eucol = sbP("eucol", [16, 1]); nacol = sbP("nacol", [16, 1]); m16 = sbP("m16", [16, E])
    uinf2 = sbP("uinf2", [16, 2]); uirow = sbP("uirow", [1, 2 * NU]); ui32 = sbP("ui32", [1, 2 * NU], I32)
    ubb = sbP("ubb", [128, E])
    rtile = sbP("rtile", [128, 1]); mlt = sbP("mlt", [128, 1]); rp_ = sbP("rp_", [128, 1])
    j6 = sbP("j6", [128, 1]); rem_ = sbP("rem_", [128, 1]); ubsel = sbP("ubsel", [128, 1])
    tA = sbP("tA", [128, 1]); tB = sbP("tB", [128, 1])
    dbgst = sbP("dbgst", [128, 8]); dbg_free = [("v", 0)]; ldbg_free = [("v", 0)]
    prow = sbP("prow", [1, E])
    idx_all = sbP("idx_all", [128, (E + NU) * COLS], I16)
    tabf16 = sbP("tabf16", [16, (E + NU) * COLS])
    zz = sbP("zz", [16, YROWS // 16])

    cl = {}
    for nm, dst, src in [("id", ident, ident_h), ("idb", identb, identb_h), ("tr", triu, triu_h),
                         ("io", iota16, iota16_h), ("t1", t1c, t1_h), ("it", iotatok, iotatok_h),
                         ("m8", mask8, mask8_h), ("hs", headsel, headsel_h), ("hb", headbc, headbc_h),
                         ("si", sampidx, sampidx_h), ("tv", tokv, tokv_h)]:
        cl[nm] = k.dma("y", out=dst[:], in_=src[:])
    t_ones = k.v("memset", ones1[:], 1.0)
    t_eps = k.v("memset", epsln[:], 1e-5)
    t_onc = k.v("memset", onescol[:], 1.0)
    t_zz = k.v("memset", zz[:], 0.0)
    rw_ld = [k.dma("y", out=rw_sb[:, kc, :], in_=rw_h[kc * 128:(kc + 1) * 128, :]) for kc in range(KC)]
    t_br = k.dma("y", out=brow[:], in_=rb_h.rearrange("(a b) -> a b", a=1))
    for kc in range(KC):
        k.t("matmul", ps_[7][0:1, 0:E], onescol[:], rw_sb[:, kc, :],
            start=(kc == 0), stop=(kc == KC - 1), wait=[rw_ld[kc], t_onc], inc=(kc == KC - 1))
    t_cw = ("t", k.cnt["t"])
    t_cwev = k.s("copy", cwrow[:], ps_[7][0:1, 0:E], wait=[t_cw])
    t_rw2a = k.v("tensor_scalar", rw2[0:1, :], cwrow[:], -1.0, None, op0=ALU.mult, wait=[t_cwev])
    t_rw2b = k.dma("y", out=rw2[1:2, :], in_=rb_h.rearrange("(a b) -> a b", a=1))
    t_tz = k.dma("y", out=tabf_d.rearrange("(p c) -> p c", p=16), in_=zz[:], wait=[t_zz])
    zbf = sbP("zbf", [128, D], BF16)
    t_zb = k.v("memset", zbf[:], 0.0)
    lt2 = sbP("ltile2", [128, E])
    mu2 = sbP("mu2", [128, 1]); sg2 = sbP("sig2", [128, 1])
    iv2b = [sbP("inv2a", [128, 1]), sbP("inv2b", [128, 1])]
    ms2 = sbP("musum2", [128, 1]); sq2 = sbP("ssq2", [128, 1]); vv2 = sbP("var2b", [128, 1])
    mm2t = sbP("m2t", [128, 1]); o1t = sbP("o1t", [128, E]); ost = sbP("ost", [128, E])
    musig2 = [sbP("musig2a", [128, 2]), sbP("musig2b", [128, 2])]
    musigT2 = [sbP("musigT2a", [2, 128]), sbP("musigT2b", [2, 128])]
    xsg = sbP("xsg", [128, KC, NSAMP], BF16)
    wesb0 = sbP("wesb0", [128, H, LOWR], BF16)
    wesb1 = sbP("wesb1", [72, H, LOWR], BF16)
    wrsb0 = sbP("wrsb0", [128, H, LOWR], BF16)
    wrsb1 = sbP("wrsb1", [72, H, LOWR], BF16)
    phDw = ExitStack()
    swg = phDw.enter_context(nc.sbuf_tensor("swg", [128, KC, FF], BF16))
    swu = phDw.enter_context(nc.sbuf_tensor("swu", [128, KC, FF], BF16))
    swd = phDw.enter_context(nc.sbuf_tensor("swd", [128, FC, D], BF16))
    t_swg = k.dma("g", out=swg[:], in_=swg_h.rearrange("(c p) f -> p c f", p=128))
    t_swu = k.dma("g", out=swu[:], in_=swu_h.rearrange("(c p) f -> p c f", p=128))
    t_swd = k.dma("g", out=swd[:], in_=swd_h.rearrange("(c p) f -> p c f", p=128))
    yexp_zero_toks = []
    for zi in range(YROWS // 128):
        yexp_zero_toks.append(k.dma("g", out=yexp[zi * 128:(zi + 1) * 128, :], in_=zbf[:], wait=[t_zb]))

    t_bq = [k.dma("y", out=bqs[:, kc:kc + 1], in_=bq_h[kc * 128:(kc + 1) * 128].rearrange("(p a) -> p a", a=1)) for kc in range(KC)]
    t_bk = [k.dma("y", out=bks[:, kc:kc + 1], in_=bk_h[kc * 128:(kc + 1) * 128].rearrange("(p a) -> p a", a=1)) for kc in range(KC)]
    t_bc1 = k.dma("y", out=bcrep[0:64, :], in_=bc_h.rearrange("(p a) -> p a", a=1))
    t_bc2 = k.dma("y", out=bcrep[64:128, :], in_=bc_h.rearrange("(p a) -> p a", a=1))
    t_bo = k.dma("y", out=bos_row[:], in_=bo_h.rearrange("(a b) -> a b", a=1))
    t_wz = k.v("memset", wcpad32[:], 0.0)
    t_wc = []
    t_wc.append(k.dma("y", out=wcpad32[0:20, 0:64], in_=wc_h[0:20, :], wait=[t_wz]))
    t_wc.append(k.dma("y", out=wcpad32[32:52, 0:64], in_=wc_h[20:40, :], wait=[t_wz]))
    t_wc.append(k.dma("y", out=wcpad32[64:84, 64:128], in_=wc_h[0:20, :], wait=[t_wz]))
    t_wc.append(k.dma("y", out=wcpad32[96:116, 64:128], in_=wc_h[20:40, :], wait=[t_wz]))
    t_wcc = k.v("tensor_copy", wcpad[:], wcpad32[:], wait=t_wc)
    bbo_toks = []
    for nt in range(2):
        tt = k.t("matmul", ps_[7][:, 0:512], ones1[0:1, 0:128], bos_row[0:1, nt * 512:(nt + 1) * 512],
                 start=True, stop=True, wait=[t_bo, t_ones, t_cwev] + bbo_toks)
        bbo_toks.append(k.s("copy", bbo[:, nt * 512:(nt + 1) * 512], ps_[7][:, 0:512], wait=[tt]))

    lall = sbA("lall", [128, NT, E])
    da = sbA("da", [128, NT, E])
    ea = sbA("ea", [128, NT, E])
    oh1a = sbA("oh1a", [128, NT, E])
    oh2a = sbA("oh2a", [128, NT, E])
    ohsa = sbA("ohsa", [128, NT, E])
    log_bank = ps_[2]
    off_bank = ps_[6]
    pos_bank = ps_[3]
    tp = [ps_[0], ps_[1]]

    a_done = []
    scat_toks = []
    own_ld = [k.dma("y", out=xfb[0][:, 0, :], in_=xo[0:128, :]), None]
    xnb_free = [[("v", 0)], [("v", 0)]]
    prev_log_evac = ("s", 0)
    prev_pos_evac = ("s", 0)
    prev_off_read = ("s", 0)
    prev_tmp_ev = ("s", 0)
    prev_fm_users = [("t", 0), ("v", 0)]
    prev_bg = [("v", 0), ("v", 0)]
    prev_g2 = ("v", 0)
    prev_tp_evac = [("s", 0), ("s", 0)]

    for i in range(NT):
        buf = xfb[i % 2][:, 0, :]
        ld_tok = own_ld[i % 2]
        if i + 1 < NT:
            w = [a_done[i - 1]] if i >= 1 else []
            own_ld[(i + 1) % 2] = k.dma("y", out=xfb[(i + 1) % 2][:, 0, :], in_=xo[128 * (i + 1):128 * (i + 2), :], wait=w)
        k.s("activation", sqscr[:], buf[:], AF.Copy, accum_out=musum[:], wait=[ld_tok])
        k.v("tensor_scalar", mu[:], musum[:], 1.0 / D, None, op0=ALU.mult, wait=[("s", k.cnt["s"])])
        t_nmu = k.v("tensor_scalar", negmuinv[:], mu[:], -1.0, None, op0=ALU.mult)
        k.s("activation", sqscr[:], buf[:], AF.Square, bias=negmuinv[:], scale=1.0,
            accum_out=ssq[:], wait=[t_nmu])
        k.v("tensor_scalar", var[:], ssq[:], 1.0 / D, None, op0=ALU.mult, wait=[("s", k.cnt["s"])])
        t_sig = k.s("activation", sig[:], var[:], AF.Sqrt, bias=epsln[:], scale=1.0, wait=[("v", k.cnt["v"]), t_eps])
        t_inv = k.v("reciprocal", inv[:], sig[:], wait=[t_sig])
        t_nmi = k.v("tensor_scalar", negmuinv[:], mu[:], inv[:], -1.0, op0=ALU.mult, op1=ALU.mult)
        t_xn = k.s("activation", xn32[:], buf[:], AF.Identity, bias=negmuinv[:], scale=inv[:],
                   wait=[t_nmi, prev_fm_users[0], prev_fm_users[1]])
        t_xnb = k.v("tensor_copy", xnb[i % 2][:], xn32[:], wait=[t_xn] + xnb_free[i % 2])
        xnb_free[i % 2] = [k.dma("y", out=xnrows[128 * i:128 * (i + 1), :], in_=xnb[i % 2][:], wait=[t_xnb])]
        if debug:
            xnb_free[i % 2].append(k.dma("y", out=dbg["xn"][128 * i:128 * (i + 1), :], in_=xnb[i % 2][:], wait=[t_xnb]))
        tt_toks = []
        for h2 in range(2):
            for j in range(4):
                kc = h2 * 4 + j
                tt_toks.append(k.t("transpose", tp[h2][:, j * 128:(j + 1) * 128],
                                   xn32[:, kc * 128:(kc + 1) * 128], ident[:],
                                   wait=[t_xn, prev_tp_evac[h2], cl["id"]]))
        ev0 = k.s("copy", fmscr[:, 0:512], tp[0][:, 0:512], wait=[tt_toks[3], prev_fm_users[0], prev_fm_users[1]])
        ev1 = k.s("copy", fmscr[:, 512:1024], tp[1][:, 0:512], wait=[tt_toks[7]])
        prev_tp_evac = [ev0, ev1]
        t_fmv = k.v("tensor_copy",
                    xnfm[:, :, 128 * i:128 * (i + 1)],
                    fmscr.rearrange("p (c t) -> p c t", c=KC), wait=[ev1])
        for kc in range(KC):
            k.t("matmul", log_bank[:, 16 * i:16 * i + 16], fmscr[:, kc * 128:(kc + 1) * 128],
                rw_sb[:, kc, :], start=(kc == 0), stop=False,
                wait=([ev0] if kc < 4 else [ev1]) + ([rw_ld[kc]] if kc == 0 else []), inc=False)
        t_rb = k.t("matmul", log_bank[:, 16 * i:16 * i + 16], ones1[0:1, 0:128], brow[:],
                   start=False, stop=True, wait=[t_br, t_ones])
        prev_fm_users = [t_rb, t_fmv]
        a_done.append(t_xn)

    t_lev = k.s("copy", lall[:], log_bank[:, 0:NT * E].rearrange("p (t e) -> p t e", e=E),
                wait=[prev_fm_users[0]])
    if debug:
        ldbg_free[0] = k.dma("y", out=dbg["logits"][:], in_=lall[:], wait=[t_lev])
    io3 = iota16[:].rearrange("p (a e) -> p a e", a=1).to_broadcast([128, NT, E])
    t13 = t1c[:].rearrange("p (a e) -> p a e", a=1).to_broadcast([128, NT, E])
    pa = da
    sel = lall
    pa2 = ea
    k.v("tensor_reduce", m1a[:], lall[:], axis=AX.X, op=ALU.max, wait=[t_lev])
    t_d = k.v("tensor_tensor", da[:], lall[:], m1a[:].to_broadcast([128, NT, E]), op=ALU.subtract)
    t_exp = k.s("activation", ea[:], da[:], AF.Exp, wait=[t_d])
    k.v("tensor_reduce", sa1[:], ea[:], axis=AX.X, op=ALU.add, wait=[t_exp])
    k.v("reciprocal", ra1[:], sa1[:])
    k.v("tensor_tensor", pa[:], ea[:], ra1[:].to_broadcast([128, NT, E]), op=ALU.mult)
    k.v("tensor_reduce", p1a[:], pa[:], axis=AX.X, op=ALU.max)
    k.v("tensor_tensor", sel[:], pa[:], p1a[:].to_broadcast([128, NT, E]), op=ALU.is_equal,
        wait=[ldbg_free[0]])
    k.v("tensor_tensor", sel[:], sel[:], t13, op=ALU.mult, wait=[cl["t1"]])
    k.v("tensor_scalar", sel[:], sel[:], BIG, None, op0=ALU.add)
    k.v("tensor_reduce", i1a[:], sel[:], axis=AX.X, op=ALU.min)
    k.v("tensor_tensor", oh1a[:], io3, i1a[:].to_broadcast([128, NT, E]), op=ALU.is_equal,
        wait=[cl["io"]])
    k.v("tensor_scalar", sel[:], oh1a[:], 2.0, None, op0=ALU.mult)
    k.v("tensor_tensor", pa2[:], pa[:], sel[:], op=ALU.subtract)
    k.v("tensor_reduce", p2a[:], pa2[:], axis=AX.X, op=ALU.max)
    k.v("tensor_tensor", sel[:], pa2[:], p2a[:].to_broadcast([128, NT, E]), op=ALU.is_equal)
    k.v("tensor_tensor", sel[:], sel[:], t13, op=ALU.mult)
    k.v("tensor_scalar", sel[:], sel[:], BIG, None, op0=ALU.add)
    k.v("tensor_reduce", i2a[:], sel[:], axis=AX.X, op=ALU.min)
    k.v("tensor_tensor", oh2a[:], io3, i2a[:].to_broadcast([128, NT, E]), op=ALU.is_equal)
    t_ohs = k.v("tensor_tensor", ohsa[:], oh1a[:], oh2a[:], op=ALU.add)
    k.v("tensor_tensor", sa1[:], p1a[:], p2a[:], op=ALU.add)
    k.v("tensor_scalar", sa1[:], sa1[:], 1e-9, None, op0=ALU.add)
    k.v("reciprocal", ra1[:], sa1[:])
    k.v("tensor_tensor", valk[:, :, 0:1], p1a[:], ra1[:], op=ALU.mult)
    k.v("tensor_tensor", valk[:, :, 1:2], p2a[:], ra1[:], op=ALU.mult)
    k.v("tensor_copy", idxk[:, :, 0:1], i1a[:])
    t_idxa = k.v("tensor_copy", idxk[:, :, 1:2], i2a[:])

    for i in range(NT):
        if i == 0:
            t_or = k.v("memset", offrow[:], 0.0)
        else:
            t_or = prev_off_read
        k.t("matmul", pos_bank[:, 0:16], triu[:], ohsa[:, i, :], start=True, stop=False,
            wait=[t_ohs, t_or, prev_pos_evac, cl["tr"]], inc=False)
        t_pm = k.t("matmul", pos_bank[:, 0:16], ones1[0:1, 0:128], offrow[:], start=False, stop=True)
        t_ou = k.t("matmul", off_bank[0:1, 0:16], onescol[:], ohsa[:, i, :],
                   start=True, stop=True, wait=[t_or, prev_tmp_ev])
        prev_tmp_ev = k.s("copy", tmprow[:], off_bank[0:1, 0:16], wait=[t_ou])
        prev_off_read = k.v("tensor_tensor", offrow[:], offrow[:], tmprow[:], op=ALU.add,
                            wait=[prev_tmp_ev, t_pm])
        t_pev = k.s("copy", possb[:, i, :], pos_bank[:, 0:16], wait=[t_pm])
        prev_pos_evac = t_pev

    k.v("tensor_tensor", da[:], possb[:], oh1a[:], op=ALU.mult, wait=[prev_pos_evac])
    k.v("tensor_reduce", rnkk[:, :, 0:1], da[:], axis=AX.X, op=ALU.add)
    k.v("tensor_tensor", da[:], possb[:], oh2a[:], op=ALU.mult)
    t_r2 = k.v("tensor_reduce", rnkk[:, :, 1:2], da[:], axis=AX.X, op=ALU.add)
    if debug:
        for i in range(NT):
            for col, srcb in [(0, i1a), (1, i2a), (2, p1a), (3, p2a)]:
                k.v("tensor_copy", dbgst[:, col:col + 1], srcb[:, i, :],
                    wait=[dbg_free[0]] if col == 0 else [])
            for col, kkk in [(4, 0), (5, 1), (6, 0), (7, 1)]:
                k.v("tensor_copy", dbgst[:, col:col + 1], rnkk[:, i, kkk:kkk + 1], wait=[t_r2])
            dbg_free[0] = k.dma("y", out=dbg["route"][:, i, :], in_=dbgst[:], wait=[("v", k.cnt["v"])])

    a_marks = k.marks()

    t_o = k.v("tensor_scalar", orow[:], offrow[:], -float(CB), 0.0, op0=ALU.add, op1=ALU.max,
              wait=[prev_off_read])
    t_u = k.v("memset", urow[:], 0.0)
    for j in range(5):
        k.v("tensor_scalar", trow[:], orow[:], float(US * j), None, op0=ALU.is_gt)
        t_u = k.v("tensor_tensor", urow[:], urow[:], trow[:], op=ALU.add)
    t_tot = k.v("reduce_sum", tot1[:], urow[:], axis=AX.X)
    y_u1 = k.dma("y", out=uex_d.rearrange("(a b) -> a b", a=1), in_=urow[:], wait=[t_u])
    y_u2 = k.dma("y", out=ucol[:], in_=uex_d.rearrange("(p a) -> p a", a=1), wait=[y_u1])
    t_ubm = k.t("matmul", pos_bank[0:16, 0:1], triu[0:16, 0:16], ucol[:], start=True, stop=True,
                wait=[y_u2, prev_pos_evac])
    t_ube = k.s("copy", ubcol[:], pos_bank[0:16, 0:1], wait=[t_ubm])
    y_u3 = k.dma("y", out=uex2_d.rearrange("(a b) -> a b", a=1),
                 in_=ubcol.rearrange("p a -> a (p a)"), wait=[t_ube]) if False else None
    y_u3 = k.dma("y", out=uex2_d.rearrange("(p a) -> p a", a=1), in_=ubcol[:], wait=[t_ube])
    y_u4 = k.dma("y", out=ubrow[:], in_=uex2_d.rearrange("(a b) -> a b", a=1), wait=[y_u3])
    t_ubbm = k.t("matmul", pos_bank[:, 1:17], ones1[0:1, 0:128], ubrow[:], start=True, stop=True,
                 wait=[y_u4, t_ube])
    t_ubbe = k.s("copy", ubb[:], pos_bank[:, 1:17], wait=[t_ubbm])
    k.v("tensor_scalar", m16[:], ubb[0:16, :] if False else ubb[0:16, :], None, None, op0=ALU.bypass) if False else None
    k.v("tensor_scalar", m16[:], ubb[0:16, :], iotatok[0:16, 0:1], None, op0=ALU.is_le, wait=[t_ubbe])
    k.v("reduce_sum", eucol[:], m16[:], axis=AX.X)
    t_eu = k.v("tensor_scalar", eucol[:], eucol[:], -1.0, 0.0, op0=ALU.add, op1=ALU.max)
    t_tc2 = k.t("matmul", pos_bank[0:16, 17:18], ones1[0:1, 0:16], tot1[:], start=True, stop=True,
                wait=[t_tot, t_ubbe])
    t_tce = k.s("copy", totcol[:], pos_bank[0:16, 17:18], wait=[t_tc2])
    k.v("tensor_scalar", nacol[:], iotatok[0:16, 0:1], totcol[:], None, op0=ALU.is_ge, wait=[t_tce])
    k.v("tensor_copy", uinf2[:, 0:1], eucol[:])
    t_ui = k.v("tensor_copy", uinf2[:, 1:2], nacol[:])
    y_u5 = k.dma("y", out=uinfo_d.rearrange("(p a) -> p a", a=2), in_=uinf2[0:NU, :], wait=[t_ui])
    y_u6 = k.dma("y", out=uirow[:], in_=uinfo_d.rearrange("(a b) -> a b", a=1), wait=[y_u5])
    t_ui32 = k.v("tensor_copy", ui32[:], uirow[:], wait=[y_u6])
    ubb3 = ubb[:].rearrange("p (a e) -> p a e", a=1).to_broadcast([128, NT, E])
    k.v("tensor_tensor", da[:], oh1a[:], ubb3, op=ALU.mult, wait=[t_ubbe])
    k.v("tensor_reduce", ubs2[:, :, 0:1], da[:], axis=AX.X, op=ALU.add)
    k.v("tensor_tensor", da[:], oh2a[:], ubb3, op=ALU.mult)
    k.v("tensor_reduce", ubs2[:, :, 1:2], da[:], axis=AX.X, op=ALU.add)
    k.v("tensor_scalar", mlt2[:], rnkk[:], float(CB), None, op0=ALU.is_lt, wait=[t_r2])
    k.v("tensor_scalar", rp2[:], rnkk[:], -float(CB), 0.0, op0=ALU.add, op1=ALU.max)
    k.v("memset", j62[:], 0.0)
    for mm in range(1, 5):
        k.v("tensor_scalar", tA2[:], rp2[:], float(US * mm), None, op0=ALU.is_ge)
        k.v("tensor_tensor", j62[:], j62[:], tA2[:], op=ALU.add)
    k.v("tensor_scalar", tA2[:], j62[:], float(US), None, op0=ALU.mult)
    k.v("tensor_tensor", rem2[:], rp2[:], tA2[:], op=ALU.subtract)
    k.v("tensor_tensor", tB2[:], ubs2[:], j62[:], op=ALU.add)
    k.v("tensor_scalar", tB2[:], tB2[:], float(US), float(E * CB), op0=ALU.mult, op1=ALU.add)
    k.v("tensor_tensor", tB2[:], tB2[:], rem2[:], op=ALU.add)
    k.v("tensor_scalar", tA2[:], idxk[:], float(CB), None, op0=ALU.mult, wait=[t_idxa])
    k.v("tensor_tensor", tA2[:], tA2[:], rnkk[:], op=ALU.add)
    k.v("tensor_tensor", tA2[:], tA2[:], tB2[:], op=ALU.subtract)
    k.v("tensor_tensor", tA2[:], tA2[:], mlt2[:], op=ALU.mult)
    k.v("tensor_tensor", dstk[:], tB2[:], tA2[:], op=ALU.add)
    t_dst = k.v("tensor_scalar", dstk[:], dstk[:], float(YROWS - 1), None, op0=ALU.min)
    t_a32 = k.v("tensor_copy", a32[:], dstk[:])
    prev_g2 = ("v", 0)
    a2_ps3 = t_ubbe
    for i in range(NT):
        for kk in range(2):
            prev_g2 = k.gop("indirect_dma_start", tabf_d.rearrange("(a b) -> a b", b=1),
                            bass.IndirectOffsetOnAxis(a32[:, i, kk:kk + 1], 0), tokv[:, i:i + 1], None,
                            wait=([t_a32, cl["tv"], t_tz] if (i == 0 and kk == 0) else [prev_g2]))
            scat_toks.append(prev_g2)
            k.dma("y", out=bgidx_d[kk, 128 * i:128 * (i + 1)].rearrange("(p a) -> p a", a=1),
                  in_=dstk[:, i, kk:kk + 1], wait=[t_dst] if (i == 0 and kk == 0) else [])
    a_bank_guard = {0: prev_tp_evac[0], 1: prev_tp_evac[1], 3: t_tce, 7: bbo_toks[-1]}
    a_fm_tok = prev_fm_users[1]
    a_marks = k.marks()

    def _finish():
        for i in range(K.NSLOT):
            if k.qcnt["y"][i] > 0:
                nc.sync.wait_ge(k.qsem["y"][i], k.qcnt["y"][i])
            if k.qcnt["g"][i] > 0:
                nc.sync.wait_ge(k.qsem["g"][i], k.qcnt["g"][i])
        nc.compile()
        return nc
    if PH <= 1:
        return _finish()

    bf_state = {"step": 0, "ld": [None, None],
                "prev_ev": ("s", 0), "prev_tp": ("s", 0), "prev_users": [("t", 0), ("t", 0)],
                "prev_lev": ("s", 0), "prev_cnt": ("t", 0), "guard": list(a_marks[:3]),
                "pending": None, "started": False,
                "msev": [("s", 0), ("s", 0)], "iv": [("v", 0), ("v", 0)]}
    fm2 = fmscr
    sq2scr = sqscr
    bf_tp = ps_[4]
    bf_log = ps_[5]
    NPAIR = NT // 2

    def bf_load(g):
        st = bf_state
        b2, p2 = divmod(g, NPAIR)
        st["ld"][g % 2] = k.dma("y", out=xfb[g % 2][:],
                                in_=xf[b2 * T + 256 * p2: b2 * T + 256 * (p2 + 1), :],
                                wait=[st["prev_users"][g % 2]] + st["guard"])
        st["guard"] = []

    def bf_stats(s2):
        st = bf_state
        par = s2 % 2
        j2 = s2 % NT
        buf = xfb[(j2 // 2) % 2][:, j2 % 2, :]
        ld_tok = st["ld"][(j2 // 2) % 2]
        k.s("activation", sq2scr[:], buf, AF.Copy, accum_out=ms2[:], wait=[ld_tok])
        t_s2 = k.s("activation", sq2scr[:], buf, AF.Square, accum_out=sq2[:])
        k.v("tensor_scalar", mu2[:], ms2[:], 1.0 / D, None, op0=ALU.mult, wait=[t_s2])
        k.v("tensor_tensor", mm2t[:], mu2[:], mu2[:], op=ALU.mult)
        k.v("tensor_scalar", vv2[:], sq2[:], 1.0 / D, None, op0=ALU.mult)
        t_vv = k.v("tensor_tensor", vv2[:], vv2[:], mm2t[:], op=ALU.subtract)
        t_sg = k.s("activation", sg2[:], vv2[:], AF.Sqrt, bias=epsln[:], scale=1.0, wait=[t_vv, t_eps])
        st["iv"][par] = k.v("reciprocal", iv2b[par][:], sg2[:], wait=[t_sg, st["prev_lev"]])
        k.v("tensor_copy", musig2[par][:, 0:1], mu2[:])
        t_msg = k.v("tensor_copy", musig2[par][:, 1:2], sg2[:])
        t_mst = k.t("transpose", bf_log[0:2, 128 + 128 * par:256 + 128 * par], musig2[par][:],
                    ident[:], wait=[t_msg, st["prev_lev"]])
        st["msev"][par] = k.s("copy", musigT2[par][:], bf_log[0:2, 128 + 128 * par:256 + 128 * par],
                              wait=[t_mst])

    def bf_pump(n):
        st = bf_state
        for _ in range(n):
            step = st["step"]
            if step >= B * NT:
                return
            b, j = divmod(step, NT)
            pair, jj = divmod(j, 2)
            bufi = pair % 2
            par = step % 2
            g = step // 2
            if not st["started"]:
                bf_load(g)
                bf_stats(0)
                st["started"] = True
            if st["pending"] is not None:
                pb, pj, p_os = st["pending"]
                st["prev_cnt"] = k.t("matmul", off_bank[0:1, 16 + 16 * pb:32 + 16 * pb], onescol[:],
                                     ost[:], start=(pj == 0), stop=(pj == NT - 1),
                                     wait=[p_os] + (a_marks[:3] if (pb == 0 and pj == 0) else []))
                st["pending"] = None
            buf = xfb[bufi][:, jj, :]
            ld_tok = st["ld"][bufi]
            if jj == 0 and g + 1 < B * NPAIR:
                bf_load(g + 1)
            tts = []
            for h2 in range(2):
                for jb in range(4):
                    kc = h2 * 4 + jb
                    tts.append(k.t("transpose", bf_tp[:, jb * 128:(jb + 1) * 128],
                                   buf[:, kc * 128:(kc + 1) * 128],
                                   ident[:], wait=[ld_tok, st["prev_tp"], st["prev_ev"]]))
                ev = k.s("copy", fm2[:, h2 * 512:(h2 + 1) * 512], bf_tp[:, 0:512], wait=[tts[-1]])
                st["prev_tp"] = ev
            st["prev_ev"] = st["prev_tp"]
            for kc in range(KC):
                k.t("matmul", bf_log[:, 0:16], fm2[:, kc * 128:(kc + 1) * 128], rw_sb[:, kc, :],
                    start=(kc == 0), stop=False,
                    wait=[st["prev_ev"], st["prev_lev"]] if kc == 0 else [], inc=False)
            t_rmm = k.t("matmul", bf_log[:, 0:16], musigT2[par][:], rw2[:], start=False, stop=True,
                        wait=[st["msev"][par], t_rw2a, t_rw2b])
            st["prev_users"][bufi] = t_rmm
            t_lev = k.s("activation", lt2[:], bf_log[:, 0:16], AF.Identity, scale=iv2b[par][:],
                        wait=[t_rmm, st["iv"][par], st.get("flog_free", ("v", 0))])
            st["prev_lev"] = t_lev
            if debug:
                st["flog_free"] = k.dma("y", out=dbg["flog"][:, b * NT + j, :], in_=lt2[:], wait=[t_lev])
            k.v("reduce_max", m1[:], lt2[:], axis=AX.X, wait=[t_lev, st["prev_cnt"]])
            k.v("tensor_scalar", o1t[:], lt2[:], m1[:], None, op0=ALU.is_equal)
            k.v("tensor_scalar", o1t[:], o1t[:], BIG, None, op0=ALU.mult)
            k.v("tensor_tensor", o1t[:], lt2[:], o1t[:], op=ALU.subtract)
            k.v("reduce_max", m2[:], o1t[:], axis=AX.X)
            t_os = k.v("tensor_scalar", ost[:], lt2[:], m2[:], None, op0=ALU.is_ge)
            st["pending"] = (b, j, t_os)
            if step + 1 < B * NT:
                bf_stats(step + 1)
            st["step"] += 1

    cast_state = {"step": 0}
    cast_toks = [[None, None, None] for _ in range(E)]
    attn_cast = {}

    def cast_pump(n):
        for _ in range(n):
            s3 = cast_state["step"]
            if s3 >= 3 * E + 3:
                return
            if s3 < 3:
                nm3, (sh, dh) = [("wq", (wq_h, wqc_d)), ("wk", (wk_h, wkc_d)),
                                 ("wo", (wo_h, woc_d))][s3]
                attn_cast[nm3] = k.dma("g", out=dh[:], in_=sh[:])
            else:
                e3, m3 = divmod(s3 - 3, 3)
                src3, dst3 = [(wg_h, wgc_d), (wu_h, wuc_d), (wd_h, wdc_d)][m3]
                cast_toks[e3][m3] = k.dma("g", out=dst3[e3], in_=src3[e3],
                                          wait=[bf_state["prev_lev"]])
            cast_state["step"] += 1

    def bf_flush():
        st = bf_state
        if st["pending"] is not None:
            pb, pj, p_os = st["pending"]
            st["prev_cnt"] = k.t("matmul", off_bank[0:1, 16 + 16 * pb:32 + 16 * pb], onescol[:],
                                 ost[:], start=(pj == 0), stop=(pj == NT - 1), wait=[p_os])
            st["pending"] = None

    t_wes, t_wrs = [], []
    for h in range(H):
        t_wes.append(k.dma("g", out=wesb0[:, h, :], in_=we_h[h, 0:128, :]))
        t_wes.append(k.dma("g", out=wesb1[:, h, :], in_=we_h[h, 128:200, :]))
        t_wrs.append(k.dma("g", out=wrsb0[:, h, :], in_=wr_h[h, 0:128, :]))
        t_wrs.append(k.dma("g", out=wrsb1[:, h, :], in_=wr_h[h, 128:200, :]))
    t_xsg = k.gop("dma_gather", xsg[:], xnrows[:], sampidx[:], NSAMP, NSAMP, D,
                  transpose=True, wait=xnb_free[0] + xnb_free[1] + [cl["si"]])

    with ExitStack() as phD:
        gs8 = phD.enter_context(nc.sbuf_tensor("gs8", [128, FC, 512], BF16))
        hs8 = phD.enter_context(nc.sbuf_tensor("hs8", [128, FC, 512], BF16))
        yst = [phD.enter_context(nc.sbuf_tensor(f"yst{j}", [128, D], BF16)) for j in range(2)]
        pr = [("s", 0), ("s", 0)]
        prd = ("s", 0)
        ys_free = [("y", 0), ("y", 0)]
        nbank = [0, 1, 3, 7]
        pr4 = [("s", 0)] * 4
        prd2 = [("s", 0), ("s", 0)]
        for g8 in range(4):
            c0 = g8 * 512
            for mt in range(FC):
                bank = ps_[nbank[mt % 4]]
                for kc in range(KC):
                    k.t("matmul", bank[:, 0:512], swg[:, kc, mt * 128:(mt + 1) * 128],
                        xnfm[:, kc, c0:c0 + 512], start=(kc == 0), stop=(kc == KC - 1),
                        wait=[t_swg, pr4[mt % 4]] + (prd2 if mt < 4 else [])
                             + ([a_bank_guard[nbank[mt % 4]], a_fm_tok] if (g8 == 0 and mt < 4) else []),
                        inc=(kc == KC - 1))
                tmm = ("t", k.cnt["t"])
                pr4[mt % 4] = k.s("activation", gs8[:, mt, :], bank[:, 0:512], AF.Silu, wait=[tmm])
            for mt in range(FC):
                bank = ps_[nbank[mt % 4]]
                for kc in range(KC):
                    k.t("matmul", bank[:, 0:512], swu[:, kc, mt * 128:(mt + 1) * 128],
                        xnfm[:, kc, c0:c0 + 512], start=(kc == 0), stop=(kc == KC - 1),
                        wait=[t_swu, pr4[mt % 4]], inc=(kc == KC - 1))
                tmm = ("t", k.cnt["t"])
                pr4[mt % 4] = k.v("tensor_tensor", hs8[:, mt, :], bank[:, 0:512], gs8[:, mt, :],
                                 op=ALU.mult, wait=[tmm, pr4[mt % 4]])
            for it in range(4):
                ybuf = yst[it % 2]
                bank = ps_[2 if it % 2 == 0 else 0]
                for nt2 in range(2):
                    for kc in range(FC):
                        k.t("matmul", bank[:, 0:512], hs8[:, kc, it * 128:(it + 1) * 128],
                            swd[:, kc, nt2 * 512:(nt2 + 1) * 512], start=(kc == 0), stop=(kc == FC - 1),
                            wait=([t_swd, prd2[it % 2], pr4[0], pr4[1], pr4[2], pr4[3]]
                                  + (a_marks if g8 == 0 else [])) if kc == 0 else [],
                            inc=(kc == FC - 1))
                    tmm = ("t", k.cnt["t"])
                    prd2[it % 2] = k.s("copy", ybuf[:, nt2 * 512:(nt2 + 1) * 512], bank[:, 0:512],
                              wait=[tmm, ys_free[it % 2]])
                ys_free[it % 2] = k.dma("y", out=ysh_d[c0 + it * 128:c0 + (it + 1) * 128, :],
                                        in_=ybuf[:], wait=[prd2[it % 2]])
            bf_pump(16)
            cast_pump(6)
        d_marks = k.marks()

    if PH <= 2:
        return _finish()
    phDw.close()
    t_tls = []
    for e in range(E + NU):
        t_tls.append(k.dma("y", out=tabf16[:, e * COLS:(e + 1) * COLS],
                           in_=tabf_d[e * CB:(e + 1) * CB].rearrange("(s p) -> p s", p=16),
                           wait=scat_toks if e == 0 else [], allow_slow_non_contiguous=True))
    t_tc = k.v("tensor_copy", idx_all[0:16, :], tabf16[:], wait=t_tls)
    rep_toks = []
    for r in range(1, 8):
        rep_toks.append(k.dma("y", out=idx_all[16 * r:16 * (r + 1), :], in_=idx_all[0:16, :], wait=[t_tc]))

    with ExitStack() as phC:
        wqsb = phC.enter_context(nc.sbuf_tensor("wqsb", [128, KC, D], BF16))
        wksb = phC.enter_context(nc.sbuf_tensor("wksb", [128, KC, D], BF16))
        wosb = phC.enter_context(nc.sbuf_tensor("wosb", [128, KC, D], BF16))
        qsb = phC.enter_context(nc.sbuf_tensor("qsb", [128, KC, 512], BF16))
        ksb = phC.enter_context(nc.sbuf_tensor("ksb", [128, KC, 512], BF16))
        scsb = phC.enter_context(nc.sbuf_tensor("scsb", [128, KC, 512], BF16))
        qq = phC.enter_context(nc.sbuf_tensor("qq", [128, 512], BF16))
        xst = phC.enter_context(nc.sbuf_tensor("xst", [128, 2, D], BF16))
        sst = [phC.enter_context(nc.sbuf_tensor(f"sst{j}", [128, D], BF16)) for j in range(2)]
        t_wq = k.dma("y", out=wqsb[:], in_=wqc_d.rearrange("(c p) f -> p c f", p=128),
                     wait=[d_marks[0], attn_cast["wq"]])
        t_wk = k.dma("y", out=wksb[:], in_=wkc_d.rearrange("(c p) f -> p c f", p=128),
                     wait=[attn_cast["wk"]])
        t_wo = k.dma("y", out=wosb[:], in_=woc_d.rearrange("(c p) f -> p c f", p=128),
                     wait=[attn_cast["wo"]])
        if CS < 2:
            t_z1 = k.v("memset", scWe[:], 0.0)
            t_z2 = k.v("memset", scWr[:], 0.0)
            c_prep = k.marks()
        t_xstv = []
        if CS < 2:
            pass
        else:
            ps3b = ps_[3][:].bitcast(BF16)
            prev_cp = ("v", 0)
            for h2 in range(2):
                for g4 in range(2):
                    tt = None
                    for j4 in range(4):
                        kc = g4 * 4 + j4
                        tt = k.t("transpose", ps3b[:, j4 * 128:(j4 + 1) * 128],
                                 xsg[:, kc, h2 * 128:(h2 + 1) * 128], identb[:],
                                 wait=[t_xsg, cl["idb"], prev_cp])
                    prev_cp = k.v("tensor_copy", xst[:, h2, g4 * 512:(g4 + 1) * 512],
                                  ps3b[:, 0:512], wait=[tt])
                    t_xstv.append(prev_cp)
            t_z1 = k.v("memset", scWe[:], 0.0)
            t_z2 = k.v("memset", scWr[:], 0.0)
            prev_we_ev = ("s", 0)
            for h in range(H if CS >= 3 else 0):
                kc, par = h // 2, h % 2
                for (w0, w1, dstt, colbase, wtoks) in [(wesb0, wesb1, scWe, 0, t_wes), (wrsb0, wrsb1, scWr, 32, t_wrs)]:
                    k.t("matmul", ps_[7][0:64, 0:LOWR], xst[:, 0, h * 64:(h + 1) * 64], w0[:, h, :],
                        start=True, stop=False,
                        wait=t_xstv[-1:] + [prev_we_ev, wtoks[2 * h]], inc=False)
                    tmm = k.t("matmul", ps_[7][0:64, 0:LOWR], xst[0:72, 1, h * 64:(h + 1) * 64], w1[:, h, :],
                              start=False, stop=True, wait=[wtoks[2 * h + 1]])
                    prev_we_ev = k.s("copy",
                                     dstt[par * 64:(par + 1) * 64, kc, par * 64 + colbase:par * 64 + colbase + LOWR],
                                     ps_[7][0:64, 0:LOWR], wait=[tmm, t_z1, t_z2])
        c_prep = k.marks()
        pr = [("s", 0), ("s", 0)]
        prs = ("v", 0)
        prq = ("v", 0)
        prqL = [("v", 0), ("v", 0)]
        prm = ("s", 0)
        ss_free = [("y", 0), ("y", 0)]
        sq_ev = ("s", 0)
        binv_ev = ("v", 0)
        nbank = [0, 1]
        for g8 in range(4):
            c0 = g8 * 512
            for (wsb, t_w, bias_t, qdst, invdst, invbdst) in [
                    (wqsb, t_wq, bqs, qsb, invq, invqb), (wksb, t_wk, bks, ksb, invk_, invkb)]:
                for mt in range(KC):
                    bank = ps_[nbank[mt % 2]]
                    for kc in range(KC):
                        k.t("matmul", bank[:, 0:512], wsb[:, kc, mt * 128:(mt + 1) * 128],
                            xnfm[:, kc, c0:c0 + 512], start=(kc == 0), stop=(kc == KC - 1),
                            wait=[t_w, pr[mt % 2]] + (prqL if mt < 2 else [])
                                 + (d_marks if (g8 == 0 and mt < 2) else []),
                            inc=(kc == KC - 1))
                    tmm = ("t", k.cnt["t"])
                    t_ev = k.s("activation", qdst[:, mt, :], bank[:, 0:512], AF.Identity,
                               bias=bias_t[:, mt:mt + 1], scale=1.0, wait=[tmm])
                    t_sq = k.s("activation", qq[:], qdst[:, mt, :], AF.Square, wait=[t_ev])
                    pr[mt % 2] = t_sq
                    k.t("matmul", ps_[3][0:16, 0:512], headsel[:, mt, :], qq[:],
                        start=(mt == 0), stop=(mt == KC - 1),
                        wait=[t_sq, cl["hs"]] + (([sq_ev, binv_ev, prm] + (c_prep if g8 == 0 else [])) if mt == 0 else []),
                        inc=(mt == KC - 1))
                t_ssq = ("t", k.cnt["t"])
                t_sqr = k.s("activation", invdst[0:16, :],
                            ps_[3][0:16, 0:512], AF.Sqrt, wait=[t_ssq])
                sq_ev = t_sqr
                k.v("tensor_scalar", invdst[0:16, :], invdst[0:16, :],
                    1e-12, None, op0=ALU.max, wait=[t_sqr])
                k.v("reciprocal", invdst[0:16, :], invdst[0:16, :])
                t_ivb = k.v("tensor_copy", invbdst[0:16, :], invdst[0:16, :])
                for kc in range(KC):
                    tbm = k.t("matmul", ps_[7][:, 0:512], headbc[:, kc, :], invbdst[0:16, :],
                              start=True, stop=True, wait=[t_ivb, cl["hb"], binv_ev])
                    binv_ev = k.v("tensor_tensor", qdst[:, kc, :], qdst[:, kc, :], ps_[7][:, 0:512],
                                  op=ALU.mult, wait=[tbm, pr[0], pr[1]])
            for kc in range(KC if CS >= 4 else 0):
                k.t("matmul", ps_[2][:, 0:512], scWe[:, kc, :], qsb[:, kc, :], start=True, stop=False,
                    wait=[binv_ev, prs, prm] + (c_prep if (g8 == 0 and kc == 0) else []), inc=False)
                tmm = k.t("matmul", ps_[2][:, 0:512], scWr[:, kc, :], ksb[:, kc, :], start=False, stop=True)
                prs = k.v("tensor_copy", scsb[:, kc, :], ps_[2][:, 0:512], wait=[tmm])
                tmm2 = k.t("matmul", ps_[3][:, 0:512], wcpad[:], scsb[:, kc, :], start=True, stop=True,
                           wait=[prs, t_wcc, prm] + ([sq_ev] if kc == 0 else []))
                prm = k.s("activation", scsb[:, kc, :], ps_[3][:, 0:512], AF.Identity,
                          bias=bcrep[:], scale=1.0, wait=[tmm2, t_bc1, t_bc2])
            for it in range(4 if CS >= 5 else 0):
                sbuf_ = sst[it % 2]
                bank = ps_[0 if it % 2 == 0 else 1]
                for nt2 in range(2):
                    for kc in range(KC):
                        k.t("matmul", bank[:, 0:512], scsb[:, kc, it * 128:(it + 1) * 128],
                            wosb[:, kc, nt2 * 512:(nt2 + 1) * 512], start=(kc == 0), stop=(kc == KC - 1),
                            wait=[t_wo, prm, prq, binv_ev, pr[0], pr[1]] if kc == 0 else [], inc=(kc == KC - 1))
                    tmm = ("t", k.cnt["t"])
                    prq = k.v("tensor_tensor", sbuf_[:, nt2 * 512:(nt2 + 1) * 512], bank[:, 0:512],
                              bbo[:, nt2 * 512:(nt2 + 1) * 512], op=ALU.add,
                              wait=[tmm, bbo_toks[nt2], ss_free[it % 2]])
                    prqL[it % 2] = prq
                ss_free[it % 2] = k.dma("y", out=stm_d[c0 + it * 128:c0 + (it + 1) * 128, :],
                                        in_=sbuf_[:], wait=[prq])
            bf_pump(16)
            cast_pump(6)
        c_marks = k.marks()

    if PH <= 3:
        return _finish()
    bf_pump(B * NT)
    cast_pump(3 * E + 3)
    bf_flush()
    phAB.close()

    cnt_evs = []
    for b in range(B):
        cnt_evs.append(k.s("copy", cnt_row[0:1, E * b:E * (b + 1)], off_bank[0:1, 16 + 16 * b:32 + 16 * b],
                           wait=[bf_state["prev_cnt"]]))
    cnt_evs = [k.dma("y", out=cnt_d.rearrange("(a b) -> a b", a=1), in_=cnt_row[:], wait=cnt_evs)]
    cnt_evs = [k.dma("y", out=cntT[:], in_=cnt_d.rearrange("(p e) -> p e", e=E), wait=cnt_evs)]
    if debug:
        k.dma("y", out=dbg["cnt"][:], in_=cntT[:], wait=cnt_evs)
    t_pm = k.t("matmul", ps_[5][0:1, 0:16], mask8[:], cntT[:], start=True, stop=True,
               wait=cnt_evs + [cl["m8"]])
    t_pev = k.s("copy", prow[:], ps_[5][0:1, 0:16], wait=[t_pm])
    t_pb = k.t("matmul", ps_[5][:, 16:32], ones1[0:1, 0:128], prow[:], start=True, stop=True, wait=[t_pev])
    t_pbe = k.s("copy", prior_b[:], ps_[5][:, 16:32], wait=[t_pb])

    io3c = iota16[:].rearrange("p (a e) -> p a e", a=1).to_broadcast([128, NT, E])
    pb3 = prior_b[:].rearrange("p (a e) -> p a e", a=1).to_broadcast([128, NT, E])
    for kkk in range(2):
        k.v("tensor_tensor", possb[:], io3c, idxk[:, :, kkk:kkk + 1].to_broadcast([128, NT, E]),
            op=ALU.is_equal, wait=[t_pbe, cl["io"]] if kkk == 0 else [])
        k.v("tensor_tensor", possb[:], possb[:], pb3, op=ALU.mult)
        k.v("tensor_reduce", pk16[:], possb[:], axis=AX.X, op=ALU.add)
        k.v("tensor_tensor", pk16[:], pk16[:], rnkk[:, :, kkk:kkk + 1], op=ALU.add)
        k.v("tensor_scalar", pk16[:], pk16[:], float(CAP), None, op0=ALU.is_lt)
        t_w = k.v("tensor_tensor", wts[:, :, kkk:kkk + 1], valk[:, :, kkk:kkk + 1], pk16[:], op=ALU.mult)
    wts_toks = [t_w] * NT
    if debug:
        k.dma("y", out=dbg["wts"][:], in_=wts[:], wait=[t_w])

    if PH <= 4:
        return _finish()
    with ExitStack() as phE:
        wgb = [phE.enter_context(nc.sbuf_tensor(f"wgb{j}", [128, KC, FF], BF16)) for j in range(2)]
        wub = phE.enter_context(nc.sbuf_tensor("wub", [128, KC, FF], BF16))
        wdb = phE.enter_context(nc.sbuf_tensor("wdb", [128, FC, D], BF16))
        hgb = [phE.enter_context(nc.sbuf_tensor(f"hgb{j}", [128, KC, CB], BF16)) for j in range(2)]
        gatb = phE.enter_context(nc.sbuf_tensor("gatb", [128, FC, CB], BF16))
        hbb = gatb
        yeb = xnb
        guard = c_marks
        st_ = {"gate_done": ("t", 0), "up_done": ("t", 0), "dn_done": ("t", 0),
               "pr": [("s", 0), ("s", 0)],
               "ye_free": [yexp_zero_toks[-2], yexp_zero_toks[-1]],
               "dma_y": lambda **kw: k.dma("y", **kw)}

        def moe_worker(wgbuf, wg_toks, hgbuf, t_hg_tok, wu_toks, wd_toks, yrow0, extra0):
            pr = st_["pr"]
            for mt in range(FC):
                bank = ps_[mt % 2]
                for kc in range(KC):
                    k.t("matmul", bank[:, 0:CB], wgbuf[:, kc, mt * 128:(mt + 1) * 128],
                        hgbuf[:, kc, :], start=(kc == 0), stop=(kc == KC - 1),
                        wait=[wg_toks[kc], t_hg_tok, pr[mt % 2]] + (extra0 if mt < 2 else []),
                        inc=(kc == KC - 1))
                tmm = ("t", k.cnt["t"])
                pr[mt % 2] = k.s("activation", gatb[:, mt, :], bank[:, 0:CB], AF.Silu, wait=[tmm])
            st_["gate_done"] = ("t", k.cnt["t"])
            for mt in range(FC):
                bank = ps_[mt % 2]
                for kc in range(KC):
                    k.t("matmul", bank[:, 0:CB], wub[:, kc, mt * 128:(mt + 1) * 128],
                        hgbuf[:, kc, :], start=(kc == 0), stop=(kc == KC - 1),
                        wait=[wu_toks[kc], pr[mt % 2]], inc=(kc == KC - 1))
                tmm = ("t", k.cnt["t"])
                pr[mt % 2] = k.v("tensor_tensor", hbb[:, mt, :], bank[:, 0:CB], gatb[:, mt, :],
                                 op=ALU.mult, wait=[tmm, pr[mt % 2]])
            st_["up_done"] = ("t", k.cnt["t"])
            prd = ("s", 0)
            for i3 in range(CB // 128):
                ybuf = yeb[i3 % 2]
                for nt2 in range(2):
                    for kc in range(FC):
                        k.t("matmul", ps_[2][:, 0:512], hbb[:, kc, i3 * 128:(i3 + 1) * 128],
                            wdb[:, kc, nt2 * 512:(nt2 + 1) * 512], start=(kc == 0), stop=(kc == FC - 1),
                            wait=([wd_toks[kc], prd, pr[0], pr[1]] + extra0) if kc == 0 else [wd_toks[kc]],
                            inc=(kc == FC - 1))
                    tmm = ("t", k.cnt["t"])
                    prd = k.s("copy", ybuf[:, nt2 * 512:(nt2 + 1) * 512], ps_[2][:, 0:512],
                              wait=[tmm, st_["ye_free"][i3 % 2]])
                st_["ye_free"][i3 % 2] = st_["dma_y"](out=yexp[yrow0 + i3 * 128:yrow0 + (i3 + 1) * 128, :],
                                                      in_=ybuf[:], wait=[prd])
            st_["dn_done"] = ("t", k.cnt["t"])

        t_wg = [None, None]
        t_hg = [None, None]
        t_wg[0] = k.dma("g", out=wgb[0][:], in_=wgc_d[0].rearrange("(c p) f -> p c f", p=128), wait=guard + [cast_toks[0][0]])
        t_hg[0] = k.gop("dma_gather", hgb[0][:], xnrows[:], idx_all[:, 0:COLS], CB, CB, D,
                        transpose=True, wait=[t_tc] + rep_toks + xnb_free[0] + xnb_free[1])
        for e in range(E):
            cur = e % 2
            if e + 1 < E:
                t_wg[1 - cur] = k.dma("g", out=wgb[1 - cur][:],
                                      in_=wgc_d[e + 1].rearrange("(c p) f -> p c f", p=128),
                                      wait=[st_["gate_done"], cast_toks[e + 1][0]])
                t_hg[1 - cur] = k.gop("dma_gather", hgb[1 - cur][:], xnrows[:],
                                      idx_all[:, (e + 1) * COLS:(e + 2) * COLS], CB, CB, D,
                                      transpose=True,
                                      wait=[st_["gate_done"], st_["up_done"], st_["dn_done"]] + rep_toks)
            t_wu = k.dma("g", out=wub[:], in_=wuc_d[e].rearrange("(c p) f -> p c f", p=128),
                         wait=[st_["up_done"], cast_toks[e][1]] + (guard if e == 0 else []))
            t_wd = k.dma("g", out=wdb[:], in_=wdc_d[e].rearrange("(c p) f -> p c f", p=128),
                         wait=[st_["dn_done"], cast_toks[e][2]] + (guard if e == 0 else []))
            moe_worker(wgb[cur], [t_wg[cur]] * KC, hgb[cur], t_hg[cur], [t_wu] * KC, [t_wd] * FC,
                       e * CB, guard if e == 0 else [])
        ET = mybir.EngineType
        sw_engs = [nc.tensor, nc.vector, nc.scalar, nc.sync]
        usems_pool = [[phE.enter_context(nc.semaphore(f"u{g}_{j}")) for j in range(4)] for g in range(2)]
        ucnt_pool = [[[0] for _ in range(4)] for g in range(2)]
        prev_hgu = [t_hg[0], t_hg[1]]
        for u in range(NU):
            t_hgu = k.gop("dma_gather", hgb[u % 2][:], xnrows[:],
                          idx_all[:, (E + u) * COLS:(E + u + 1) * COLS], CB, CB, D,
                          transpose=True,
                          wait=[st_["gate_done"], st_["up_done"], st_["dn_done"], prev_hgu[u % 2]] + rep_toks)
            prev_hgu[u % 2] = t_hgu
            usem_u = usems_pool[u % 2]
            ucnt = ucnt_pool[u % 2]
            tok_cnt = dict(k.cnt)
            ustart_tok = [c[0] for c in ucnt]
            nc.tensor.wait_ge(k.sem["t"], k.cnt["t"])
            nc.tensor.sem_inc(k.sem["t"], 1); k.cnt["t"] += 1
            nc.vector.wait_ge(k.sem["v"], k.cnt["v"])
            nc.vector.sem_inc(k.sem["v"], 1); k.cnt["v"] += 1
            nc.scalar.wait_ge(k.sem["s"], k.cnt["s"])
            nc.scalar.sem_inc(k.sem["s"], 1); k.cnt["s"] += 1
            for j in range(4):
                if ucnt[j][0] > 0:
                    nc.sync.wait_ge(usem_u[j], ucnt[j][0])
                nc.sync.sem_inc(usem_u[j], 16)
                ucnt[j][0] += 16
            sv_na = nc.values_load(ui32[0:1, 2 * u + 1:2 * u + 2],
                                   engines=[ET.PE, ET.DVE, ET.Activation, ET.SP],
                                   min_val=0, max_val=1)
            sv_eu = nc.values_load(ui32[0:1, 2 * u:2 * u + 1], engines=[ET.SP],
                                   min_val=0, max_val=E - 1)
            snap_cnt = dict(k.cnt)
            snap_lw = dict(k._lw)
            snap_wl = {kk2: set(vv2) for kk2, vv2 in k.wait_log.items()}
            ustart = [c[0] for c in ucnt]
            yesel = [0]
            old_dma_y = st_["dma_y"]
            for body in nc.Switch(engines=sw_engs, index=[sv_na] * 4, n=2):
                if body == 0:
                    st_["pr"] = [("s", tok_cnt["s"]), ("s", tok_cnt["s"])]
                    st_["gate_done"] = ("t", tok_cnt["t"])
                    st_["up_done"] = ("t", tok_cnt["t"])
                    st_["dn_done"] = ("t", tok_cnt["t"])
                    u_extra = [("v", tok_cnt["v"]), ("s", tok_cnt["s"])]

                    def u_dma_y(**kw):
                        j = 2 + yesel[0]
                        yesel[0] = 1 - yesel[0]
                        kw.setdefault("wait", [])
                        kw["wait"] = list(kw["wait"]) + [(usem_u[j], ustart_tok[j])]
                        return k.dma_sem("y", usem_u[j], ucnt[j], "dma_start", **kw)
                    st_["dma_y"] = u_dma_y

                    def chunk_loads(w_d, nchunk, fdim, dstbuf, first_waits):
                        toks = []
                        for kc in range(nchunk):
                            sj = kc % 2
                            ld = k.dma_sem("y", usem_u[sj], ucnt[sj], "dma_start",
                                           out=dstbuf[:, kc, 0:fdim],
                                           in_=w_d[sv_eu][kc * 128:(kc + 1) * 128, 0:fdim],
                                           wait=[(usem_u[sj], ucnt[sj][0])] + first_waits)
                            toks.append(ld)
                        return toks

                    wg_toks = chunk_loads(wgc_d, KC, FF, wgb[0], [st_["gate_done"]])
                    wu_toks = chunk_loads(wuc_d, KC, FF, wub, [st_["up_done"]])
                    wd_toks = chunk_loads(wdc_d, FC, D, wdb, [st_["dn_done"]])
                    moe_worker(wgb[0], wg_toks, hgb[u % 2], t_hgu, wu_toks, wd_toks,
                               E * CB + u * US, u_extra)
                    st_["dma_y"] = old_dma_y
                else:
                    def comp(eng, sem, lo, hi):
                        if hi <= lo:
                            return
                        pts = sorted(vv2 for vv2 in k.wait_log.get(id(sem), ())
                                     if lo < vv2 <= hi and vv2 not in snap_wl.get(id(sem), ()))
                        cur = lo
                        for p in pts:
                            if p > cur:
                                eng.sem_inc(sem, p - cur)
                                cur = p
                        if hi > cur:
                            eng.sem_inc(sem, hi - cur)
                    comp(nc.tensor, k.sem["t"], snap_cnt["t"], k.cnt["t"])
                    comp(nc.vector, k.sem["v"], snap_cnt["v"], k.cnt["v"])
                    comp(nc.scalar, k.sem["s"], snap_cnt["s"], k.cnt["s"])
                    for j in range(4):
                        if ucnt[j][0] > ustart[j] and ustart[j] > 0:
                            nc.sync.wait_ge(usem_u[j], ustart[j])
                        comp(nc.sync, usem_u[j], ustart[j], ucnt[j][0])
            k._lw = snap_lw
            nc.tensor.wait_ge(k.sem["t"], k.cnt["t"])
            nc.vector.wait_ge(k.sem["v"], k.cnt["v"])
            nc.scalar.wait_ge(k.sem["s"], k.cnt["s"])
            nc.tensor.sem_inc(k.sem["t"], 1); k.cnt["t"] += 1
            nc.vector.sem_inc(k.sem["v"], 1); k.cnt["v"] += 1
            nc.scalar.sem_inc(k.sem["s"], 1); k.cnt["s"] += 1
            for j in range(4):
                nc.sync.wait_ge(usem_u[j], ucnt[j][0])
                nc.sync.sem_inc(usem_u[j], 16)
                ucnt[j][0] += 16
            st_["pr"] = [("s", k.cnt["s"]), ("s", k.cnt["s"])]
            st_["gate_done"] = ("t", k.cnt["t"])
            st_["up_done"] = ("t", k.cnt["t"])
            st_["dn_done"] = ("t", k.cnt["t"])
        e_marks = k.marks()
        st_e_ye = st_["ye_free"]

    if PH <= 5:
        return _finish()
    with ExitStack() as phF:
        bga = phF.enter_context(nc.sbuf_tensor("bga", [128, NT, D], BF16))
        bgb = phF.enter_context(nc.sbuf_tensor("bgb", [128, NT, D], BF16))
        bgf = phF.enter_context(nc.sbuf_tensor("bgf", [16, T // 16], F32))
        bgi = [phF.enter_context(nc.sbuf_tensor(f"bgi{j}", [128, T // 16], I16)) for j in range(2)]
        scr1b = [phF.enter_context(nc.sbuf_tensor(f"scr1{j}", [128, D], F32)) for j in range(2)]
        scr2b = [phF.enter_context(nc.sbuf_tensor(f"scr2{j}", [128, D], F32)) for j in range(2)]
        xrt = [phF.enter_context(nc.sbuf_tensor(f"xrt{j}", [128, D], F32)) for j in range(2)]
        xnt = [phF.enter_context(nc.sbuf_tensor(f"xnt{j}", [128, D], BF16)) for j in range(2)]
        ysht = [phF.enter_context(nc.sbuf_tensor(f"ysht{j}", [128, D], BF16)) for j in range(2)]
        sstt = [phF.enter_context(nc.sbuf_tensor(f"sstt{j}", [128, D], BF16)) for j in range(2)]
        outb = [phF.enter_context(nc.sbuf_tensor(f"outb{j}", [128, D], F32)) for j in range(2)]
        FS = int(os.environ.get("MK_FSUB", "9"))
        gt = []
        mdbg_free = [("v", 0)]
        for kk in range(2 if FS >= 1 else 0):
            t_bls = []
            for qq_ in range(4):
                t_bls.append(k.dma("y", out=bgf[:, qq_ * 32:(qq_ + 1) * 32],
                             in_=bgidx_d[kk, qq_ * 512:(qq_ + 1) * 512].rearrange("(s p) -> p s", p=16),
                             wait=e_marks if qq_ == 0 else [], allow_slow_non_contiguous=True))
            t_bc_ = k.v("tensor_copy", bgi[kk][0:16, :], bgf[:], wait=t_bls)
            reps = [k.dma("y", out=bgi[kk][16 * r:16 * (r + 1), :], in_=bgi[kk][0:16, :], wait=[t_bc_])
                    for r in range(1, 8)]
            dstb = bga if kk == 0 else bgb
            for hh in range(2):
                gt.append(k.gop("dma_gather", dstb[:, hh * 8:(hh + 1) * 8, :], yexp[:],
                                bgi[kk][:, hh * 64:(hh + 1) * 64],
                                T // 2, T // 2, D, transpose=False,
                                wait=[t_bc_] + reps + [st_e_ye[0], st_e_ye[1]]))
        ld_toks = [[None] * 4, [None] * 4]
        st_free = [("v", 0), ("v", 0)]
        war_scr = [("v", 0), ("v", 0)]
        cons = [{}, {}]
        for i in range(NT if FS >= 2 else 0):
            cur = i % 2
            if i < 2:
                ld_toks[cur] = [
                    k.dma("y", out=xrt[cur][:], in_=xo[128 * i:128 * (i + 1), :], wait=e_marks),
                    k.dma("y", out=xnt[cur][:], in_=xnrows[128 * i:128 * (i + 1), :], wait=e_marks),
                    k.dma("y", out=ysht[cur][:], in_=ysh_d[128 * i:128 * (i + 1), :], wait=e_marks),
                    k.dma("y", out=sstt[cur][:], in_=stm_d[128 * i:128 * (i + 1), :], wait=e_marks),
                ]
            scr1 = scr1b[cur]; scr2 = scr2b[cur]
            t_s2 = k.s("activation", scr1[:], bga[:, i, :], AF.Identity, scale=wts[:, i, 0:1],
                       wait=gt + [wts_toks[i], mdbg_free[0], war_scr[cur]])
            t_s2 = k.s("activation", scr2[:], bgb[:, i, :], AF.Identity, scale=wts[:, i, 1:2])
            t_moe = k.v("tensor_tensor", scr1[:], scr1[:], scr2[:], op=ALU.add, wait=[t_s2])
            if debug and "moe" in dbg:
                mdbg_free[0] = k.dma("y", out=dbg["moe"][128 * i:128 * (i + 1), :], in_=scr1[:], wait=[t_moe])
            t_xy = k.v("tensor_tensor", scr2[:], xnt[cur][:], ysht[cur][:], op=ALU.add, wait=ld_toks[cur][1:3])
            k.v("tensor_tensor", scr1[:], scr1[:], scr2[:], op=ALU.add, wait=[mdbg_free[0]])
            t_g = k.s("activation", scr2[:], scr1[:], AF.Silu, wait=[("v", k.cnt["v"])])
            t_sg = k.v("tensor_tensor", scr1[:], sstt[cur][:], scr2[:], op=ALU.mult,
                       wait=[t_g, ld_toks[cur][3]])
            t_out = k.v("tensor_tensor", outb[cur][:], xrt[cur][:], scr1[:], op=ALU.add,
                        wait=[ld_toks[cur][0], st_free[cur]])
            war_scr[cur] = t_out
            st_free[cur] = k.dma("y", out=out_h[128 * i:128 * (i + 1), :], in_=outb[cur][:], wait=[t_out])
            if i + 2 < NT:
                ld_toks[cur] = [
                    k.dma("y", out=xrt[cur][:], in_=xo[128 * (i + 2):128 * (i + 3), :], wait=[t_out]),
                    k.dma("y", out=xnt[cur][:], in_=xnrows[128 * (i + 2):128 * (i + 3), :], wait=[t_xy]),
                    k.dma("y", out=ysht[cur][:], in_=ysh_d[128 * (i + 2):128 * (i + 3), :], wait=[t_xy]),
                    k.dma("y", out=sstt[cur][:], in_=stm_d[128 * (i + 2):128 * (i + 3), :], wait=[t_sg]),
                ]

    if debug:
        k.dma("y", out=dbg["s"][:], in_=stm_d[:])
        k.dma("y", out=dbg["ysh"][:], in_=ysh_d[:])
    return _finish()


def make_consts():
    c = {}
    c["c_ident"] = np.eye(128, dtype=np.float32)
    c["c_identb"] = np.eye(128, dtype=ml_dtypes.bfloat16)
    c["c_triu"] = np.triu(np.ones((128, 128), dtype=np.float32), k=1)
    c["c_iota16"] = np.tile(np.arange(E, dtype=np.float32), (128, 1))
    c["c_t1"] = c["c_iota16"] - BIG
    c["c_iotatok"] = np.arange(128, dtype=np.float32).reshape(128, 1)
    c["c_tokv"] = (np.arange(NT, dtype=np.float32)[None, :] * 128
                   + np.arange(128, dtype=np.float32)[:, None]).astype(np.float32)
    hs = np.zeros((128, KC, E), dtype=ml_dtypes.bfloat16)
    for dl in range(128):
        for kc in range(KC):
            hs[dl, kc, 2 * kc + (dl >= 64)] = 1
    c["c_headsel"] = hs
    hb = np.zeros((16, KC, 128), dtype=ml_dtypes.bfloat16)
    for h in range(16):
        for kc in range(KC):
            for col in range(128):
                if h == 2 * kc + (col >= 64):
                    hb[h, kc, col] = 1
    c["c_headbc"] = hb
    samp = np.linspace(0.0, T - 1, LEN).astype(np.int32)
    sampp = np.zeros(NSAMP, dtype=np.int16)
    sampp[:LEN] = samp
    si = np.zeros((16, NSAMP // 16), dtype=np.int16)
    for j in range(NSAMP):
        si[j % 16, j // 16] = sampp[j]
    c["c_sampidx"] = np.tile(si, (8, 1))
    return c


_NC_CACHE = {}


def kernel(L0_attn_x, L0_attn_Wq, L0_attn_bq, L0_attn_Wk, L0_attn_bk, L0_attn_We,
           L0_attn_Wr, L0_attn_Wc, L0_attn_bc, L0_attn_Wo, L0_attn_bo,
           L0_ffn_router_w, L0_ffn_router_b, L0_ffn_w_gate, L0_ffn_w_up,
           L0_ffn_w_down, L0_ffn_sw_gate, L0_ffn_sw_up, L0_ffn_sw_down,
           debug=False):
    if ("nc", debug) not in _NC_CACHE:
        _NC_CACHE[("nc", debug)] = build(debug=debug)
    nc = _NC_CACHE[("nc", debug)]
    consts = make_consts()
    x = np.ascontiguousarray(np.asarray(L0_attn_x, dtype=np.float32))
    xflat = x.reshape(B * T, D)
    shared = {
        "xfull": xflat,
        "Wq": np.asarray(L0_attn_Wq, np.float32), "bq": np.asarray(L0_attn_bq, np.float32),
        "Wk": np.asarray(L0_attn_Wk, np.float32), "bk": np.asarray(L0_attn_bk, np.float32),
        "We": np.asarray(L0_attn_We, np.float32), "Wr": np.asarray(L0_attn_Wr, np.float32),
        "Wc": np.asarray(L0_attn_Wc, np.float32), "bc": np.asarray(L0_attn_bc, np.float32),
        "Wo": np.asarray(L0_attn_Wo, np.float32), "bo": np.asarray(L0_attn_bo, np.float32),
        "router_w": np.asarray(L0_ffn_router_w, np.float32),
        "router_b": np.asarray(L0_ffn_router_b, np.float32),
        "w_gate": np.asarray(L0_ffn_w_gate, np.float32),
        "w_up": np.asarray(L0_ffn_w_up, np.float32),
        "w_down": np.asarray(L0_ffn_w_down, np.float32),
        "sw_gate": np.asarray(L0_ffn_sw_gate, np.float32),
        "sw_up": np.asarray(L0_ffn_sw_up, np.float32),
        "sw_down": np.asarray(L0_ffn_sw_down, np.float32),
    }
    shared.update(consts)
    in_maps = []
    for c in range(B):
        m = dict(shared)
        m["xown"] = x[c]
        m["c_mask8"] = (np.arange(8) < c).astype(np.float32).reshape(8, 1)
        in_maps.append(m)
    res = run_bass_kernel_spmd(nc, in_maps, core_ids=list(range(B)))
    out = np.stack([res.results[c]["out"] for c in range(B)], axis=0)
    if debug:
        return out, res
    return out
